# version 11
# baseline (speedup 1.0000x reference)
import numpy as np
import ml_dtypes
from contextlib import ExitStack

import concourse.bass as bass
import concourse.mybir as mybir
from concourse.bass_utils import run_bass_kernel_spmd

F32 = mybir.dt.float32
BF16 = mybir.dt.bfloat16
AF = mybir.ActivationFunctionType
ALU = mybir.AluOpType
AX = mybir.AxisListType
bf16 = ml_dtypes.bfloat16

T_PASS = 2048
TILE = 512
_DBG = {}
_DBG_OUTS = []
NCH = TILE // 64
LN_EPS = 1e-5
EPS = 1e-5



class _Op:
    __slots__ = ("eng", "fn", "reads", "writes", "dma", "deps", "marked",
                 "sem_val", "waits", "gk")

    def __init__(self, eng, fn, reads, writes, dma):
        self.eng, self.fn = eng, fn
        self.reads, self.writes, self.dma = tuple(reads), tuple(writes), dma
        self.deps, self.marked, self.sem_val, self.waits = [], False, None, []
        if dma is not None:
            w0 = self.writes[0] if self.writes else None
            if w0 is not None and not w0.startswith("d_"):
                self.gk = ("L", w0)
            else:
                self.gk = ("S", self.reads[0])


class Em:
    ENGS = ("T", "V", "S", "G", "SY")

    def __init__(self, nc):
        self.nc = nc
        self.ops = []
        self.last_w = {}
        self.readers = {}

    def add(self, eng, fn, reads=(), writes=(), dma=None):
        idx = len(self.ops)
        if getattr(self, "label", None) is not None:
            self.labels = getattr(self, "labels", {})
            self.labels[idx] = self.label
        op = _Op(eng, fn, reads, writes, dma)
        self.ops.append(op)
        dep = op.deps.append
        for t in op.reads:
            j = self.last_w.get(t)
            if j is not None:
                dep(j)
            self.readers.setdefault(t, []).append(idx)
        for t in op.writes:
            j = self.last_w.get(t)
            if j is not None:
                dep(j)
            for r in self.readers.get(t, ()):
                dep(r)
            self.last_w[t] = idx
            self.readers[t] = []
        return idx

    def T(self, fn, reads=(), writes=()):
        return self.add("T", fn, reads, writes)

    def V(self, fn, reads=(), writes=()):
        return self.add("V", fn, reads, writes)

    def S(self, fn, reads=(), writes=()):
        return self.add("S", fn, reads, writes)

    def G(self, fn, reads=(), writes=()):
        return self.add("G", fn, reads, writes)

    def dma(self, queue, out, in_, reads=(), writes=()):
        eng = {"qs": "SY", "qg": "G", "qc": "S"}[queue]
        return self.add(
            eng, (lambda e, o=out, i=in_: e.dma_start(out=o, in_=i)),
            reads, writes, dma=queue)

    def finalize(self):
        nc, ops = self.nc, self.ops
        for op in ops:
            for d in op.deps:
                dop = ops[d]
                if dop.dma is not None or dop.eng != op.eng:
                    dop.marked = True
        eng_count = {e: 0 for e in self.ENGS}
        grp_count = {}
        for op in ops:
            if op.dma is not None:
                grp_count[op.gk] = grp_count.get(op.gk, 0) + 1
                op.sem_val = 16 * grp_count[op.gk]
            elif op.marked:
                eng_count[op.eng] += 1
                op.sem_val = eng_count[op.eng]
        waited = {e: {} for e in self.ENGS}
        for op in ops:
            need = {}
            for d in op.deps:
                dop = ops[d]
                key = dop.gk if dop.dma is not None else (
                    dop.eng if dop.eng != op.eng else None)
                if key is not None and dop.sem_val:
                    if need.get(key, 0) < dop.sem_val:
                        need[key] = dop.sem_val
            w = waited[op.eng]
            for key in sorted(need, key=str):
                val = need[key]
                if w.get(key, 0) < val:
                    op.waits.append((key, val))
                    w[key] = val
        per_eng = {e: [] for e in self.ENGS}
        for op in ops:
            per_eng[op.eng].append(op)

        with ExitStack() as c2:
            sems = {}
            for k in self.ENGS:
                sems[k] = c2.enter_context(nc.semaphore(f"sem_{k}"))
            for i, gk in enumerate(sorted(grp_count, key=str)):
                sems[gk] = c2.enter_context(nc.semaphore(f"dsem{i}"))
            block = c2.enter_context(nc.Block())

            def mk(key):
                def run(eng):
                    for op in per_eng[key]:
                        for sk, val in op.waits:
                            eng.wait_ge(sems[sk], val)
                        inst = op.fn(eng)
                        if op.dma is not None:
                            inst.then_inc(sems[op.gk], 16)
                        elif op.marked:
                            inst.then_inc(sems[op.eng], 1)
                    if key == "G":
                        for gk, n in sorted(grp_count.items(), key=str):
                            if gk[0] == "S":
                                eng.wait_ge(sems[gk], 16 * n)
                return run

            block.tensor(mk("T"))
            block.vector(mk("V"))
            block.scalar(mk("S"))
            block.gpsimd(mk("G"))
            block.sync(mk("SY"))



def MM(out, lhsT, rhs, start=True, stop=True, tp=None):
    return lambda e: e.matmul(out=out, lhsT=lhsT, rhs=rhs, start=start,
                              stop=stop, tile_position=tp)

def TRANS(out, in_, ident, tp=None):
    return lambda e: e.transpose(out=out, in_=in_, identity=ident,
                                 tile_position=tp)

def COPY(out, in_):
    return lambda e: e.tensor_copy(out=out, in_=in_)

def TT(out, in0, in1, op=ALU.mult):
    return lambda e: e.tensor_tensor(out=out, in0=in0, in1=in1, op=op)

def TS(out, in0, s1, s2=None, op0=ALU.mult, op1=ALU.add):
    if s2 is None:
        return lambda e: e.tensor_scalar(out=out, in0=in0, scalar1=s1,
                                         scalar2=None, op0=op0)
    return lambda e: e.tensor_scalar(out=out, in0=in0, scalar1=s1, scalar2=s2,
                                     op0=op0, op1=op1)

def TSA(out, in0, accum_out):
    return lambda e: e.tensor_scalar(out=out, in0=in0, scalar1=1.0,
                                     scalar2=0.0, op0=ALU.mult, op1=ALU.add,
                                     accum_out=accum_out)

def STT(out, in0, s, in1, op0=ALU.mult, op1=ALU.add):
    return lambda e: e.scalar_tensor_tensor(out=out, in0=in0, scalar=s,
                                            in1=in1, op0=op0, op1=op1)

def ACT(out, in_, func, bias=0.0, scale=1.0, accum_out=None):
    return lambda e: e.activation(out=out, in_=in_, func=func, bias=bias,
                                  scale=scale, accum_out=accum_out)

def MEMSET(ap, v):
    return lambda e: e.memset(ap, v)

def REDUCE(out, in_, op):
    return lambda e: e.tensor_reduce(out=out, in_=in_, axis=AX.X, op=op)

def RECIP(out, in_):
    return lambda e: e.reciprocal(out=out, in_=in_)


def fview(ap, dims):
    return bass.AP(ap.tensor, ap.offset,
                   [list(ap.ap[0])] + [[s, c] for (s, c) in dims])



def build_nc():
    nc = bass.Bass("TRN2", target_bir_lowering=False, debug=False)
    em = Em(nc)
    V, S, G, Tn, dma = em.V, em.S, em.G, em.T, em.dma

    DR = lambda name, shape, dt=F32, kind="ExternalInput": \
        nc.dram_tensor(name, shape, dt, kind=kind).ap()

    u_main = DR("u_main", [1024, T_PASS])
    u_pref = DR("u_pref", [1024, T_PASS])
    w_inT_h = DR("w_inT", [1024, 4256])
    conv_w_h = DR("conv_w", [128, 18 * 4])
    conv_b_h = DR("conv_b", [128, 18])
    dt_bias_h = DR("dt_bias", [32, 1])
    a_log_h = DR("a_log", [32, 1])
    d_row_h = DR("d_row", [1, 32])
    dtmask_h = DR("dtmask", [32, 1])
    s0_h = DR("s0", [64, 2048])
    norm_w_h = DR("norm_w", [128, 16])
    w_outT_h = DR("w_outT", [2048, 1024])
    gate_w_h = DR("gate_w", [128, 32])
    gate_b_h = DR("gate_b", [4, 1])
    w1_h = DR("w1", [4096, 1024])
    w2_h = DR("w2", [4096, 1024])
    b1f_h = DR("b1f", [1, 4096])
    b2f_h = DR("b2f", [1, 4096])
    b1c_h = DR("b1c", [128, 32])
    b2c_h = DR("b2c", [128, 32])
    be2r_h = DR("be2r", [4, 1024])
    g1c_h = DR("g1c", [128, 32])
    be1c_h = DR("be1c", [128, 32])
    g2c_h = DR("g2c", [128, 32])
    identb_h = DR("identb", [128, 128], BF16)
    identf_h = DR("identf", [128, 128])
    lt64_h = DR("lt64", [64, 64])
    mask_sl_h = DR("mask_sl", [64, 64], BF16)
    e32x2_h = DR("e32x2", [64, 2048], BF16)
    pick4b_h = DR("pick4b", [4, 512], BF16)
    onescol_h = DR("onescol", [128, 1], BF16)
    onesr_h = DR("onesr", [1, 512])
    onesrb_h = DR("onesrb", [1, 512], BF16)
    cvals_h = DR("cvals", [1, 4])

    y_out = DR("y_out", [1024, T_PASS], kind="ExternalOutput")

    wb_in = nc.dram_tensor("wb_in", [34, 128, 8, 128], BF16, kind="Internal").ap()
    wb_out = nc.dram_tensor("wb_out", [8, 128, 16, 128], BF16, kind="Internal").ap()
    wb1 = nc.dram_tensor("wb1", [32, 128, 8, 128], BF16, kind="Internal").ap()
    wb2 = nc.dram_tensor("wb2", [32, 128, 8, 128], BF16, kind="Internal").ap()

    ctx = ExitStack()
    SB = lambda name, shape, dt=F32: ctx.enter_context(
        nc.sbuf_tensor(name, shape, dt)).ap()
    PSUM = lambda name, shape, dt=F32: ctx.enter_context(
        nc.psum_tensor(name, shape, dt)).ap()

    with ctx:
        pA = PSUM("pA", [128, 512])
        pB = PSUM("pB", [128, 512])
        pC = PSUM("pC", [128, 512])
        pM = PSUM("pM", [128, 1024])
        pQ = PSUM("pQ", [64, 512])
        pXT = PSUM("pXT", [64, 1024], BF16)
        pYT = PSUM("pYT", [128, 1024], BF16)

        cstc = SB("cstc", [1, 4])
        idb = SB("idb", [128, 128], BF16)
        idf = SB("idf", [128, 128])
        lt64 = SB("lt64s", [64, 64])
        msl = SB("msls", [64, 64], BF16)
        rhs66 = SB("rhs66", [66, 2048], BF16)
        onesr = SB("onesr_s", [1, 512])
        onesrb = SB("onesrb_s", [1, 512], BF16)
        onescol = SB("onescol_s", [128, 1], BF16)
        pick4b = SB("pick4b_s", [4, 512], BF16)
        dtb = SB("dtb", [32, 1])
        amk = SB("amk", [32, 1])
        acol = SB("acol", [32, 1])
        alogs = SB("alogs", [32, 1])
        drow = SB("drow", [1, 32])
        dtile = SB("dtile", [64, 32], BF16)
        gbc = SB("gbc", [4, 1])
        cwh = SB("cwh", [128, 18 * 4])
        cbh = SB("cbh", [128, 18])
        nwc = SB("nwc", [128, 16])
        g1s = SB("g1s", [128, 32])
        be1s = SB("be1s", [128, 32])
        g2s = SB("g2s", [128, 32])
        b1fs = SB("b1fs", [1, 4096], BF16)
        b2fs = SB("b2fs", [1, 4096], BF16)
        be2k = SB("be2k", [4, 1024], BF16)
        gws = SB("gws", [128, 32], BF16)
        wm1 = SB("wm1", [128, 8 * 4], BF16)
        wm2 = SB("wm2", [128, 8 * 4], BF16)
        bm1 = SB("bm1", [1, 4])
        bm2 = SB("bm2", [1, 4])
        stk = SB("stk", [128, 2048], BF16)
        sd = SB("sd", [128, 2048], BF16)
        dchbc = SB("dchbc", [128, 8 * 32], BF16)

        x32 = SB("x32", [128, 8 * 512])
        xbfb = SB("xbfb", [128, 512], BF16)
        xn = SB("xn", [128, 8 * 512], BF16)
        sq1 = SB("sq1", [128, 512], BF16)
        sq2 = SB("sq2", [128, 512], BF16)
        zh = SB("zh", [128, 16 * 512], BF16)
        xbc = SB("xbc", [128, 18 * 515], BF16)
        xc = SB("xc", [128, 18 * 512], BF16)
        cva = SB("cva", [128, 512])
        ths = SB("ths", [128, 512], BF16)
        dts = SB("dts", [32, 512])
        dtv = SB("dtv", [32, 512])
        adt = SB("adt", [32, 512])
        cumA = SB("cumA", [32, 512])
        cumAT = SB("cumAT", [64, 8 * 32])
        adtT = SB("adtT", [64, 64])
        dtT = SB("dtT", [64, 8 * 32], BF16)
        dcyT = SB("dcyT", [64, 8 * 32], BF16)
        dscr = SB("dscr", [64, 64])
        sumar = SB("sumar", [1, 64])
        phl = SB("phl", [32, 128], BF16)
        l66src = SB("l66src", [32, 128], BF16)
        l66 = SB("l66", [66, 128], BF16)
        gsb = SB("gsb", [64, 64], BF16)
        btb = SB("btb", [64, 64], BF16)
        dfc = SB("dfc", [128, 1024])
        msb = SB("msb", [128, 2048], BF16)
        xdt = SB("xdt", [64, 2048], BF16)
        ytk = SB("ytk", [64, 2048], BF16)
        dterm = SB("dterm", [64, 2048], BF16)
        slz = SB("slz", [128, 1024], BF16)
        slzt = SB("slzt", [128, 1024], BF16)
        yg = SB("yg", [128, 16 * 512], BF16)
        accb = SB("accb", [128, 8 * 512], BF16)
        tsc = SB("tsc", [128, 512], BF16)
        r_a = SB("r_a", [1, 512])
        r_b = SB("r_b", [1, 512])
        r_m = SB("r_m", [1, 512])
        statCb = SB("statCb", [4, 512], BF16)
        statC2b = SB("statC2b", [4, 512], BF16)
        wgt = SB("wgt", [4, 512])
        wgtb = SB("wgtb", [4, 512], BF16)
        wrow = SB("wrow", [1, 512])
        gsr = SB("gsr", [4, 512])
        gst = SB("gst", [128, 16])
        gw1 = SB("gw1", [128, 16])
        gw2 = SB("gw2", [128, 16])
        gr1 = SB("gr1", [128, 4])
        gr2 = SB("gr2", [128, 4])
        dbgg = SB("dbgg", [128, 20])
        wt = [SB(f"wt{i}", [128, 8 * 128], BF16) for i in range(2)]
        wsum = SB("wsum", [128, 1])
        praw_ = [x32.bitcast(BF16)[:, 0:4096], x32.bitcast(BF16)[:, 4096:8192]]
        h1 = zh[:, 0:4096]
        wpf_ = [yg.bitcast(F32)[:, 0:2176], xbc.bitcast(F32)[:, 0:2176]]
        wpb_ = [yg[:, 4352:6528], xbc[:, 4352:6528]]
        wpt_ = ["yg", "xbc"]
        wpf = wpf_[0]
        wpb = wpb_[0]
        mbc = slz.bitcast(F32)[:, 0:512]
        ibc = slzt.bitcast(F32)[:, 0:512]
        wo = [SB("wo0", [128, 2048], BF16), SB("wo1", [128, 2048], BF16)]
        outs_ = [cva[:, 0:512], ibc]
        outst_ = ["cva", "slzt"]


        dma("qs", cstc[:, :], cvals_h[:, :], writes=["cst"])
        nc.const_aps.register(F32, LN_EPS, cstc[0:1, 0:1])
        nc.const_aps.register(F32, -0.5, cstc[0:1, 1:2])
        dma("qs", idb[:, :], identb_h[:, :], writes=["idb"])
        dma("qs", idf[:, :], identf_h[:, :], writes=["idf"])
        dma("qs", lt64[:, :], lt64_h[:, :], writes=["lt64"])
        dma("qs", msl[:, :], mask_sl_h[:, :], writes=["msl"])
        dma("qs", rhs66[0:64, :], e32x2_h[:, :], writes=["rhs66"])
        dma("qs", onesr[:, :], onesr_h[:, :], writes=["onesr"])
        dma("qs", onesrb[:, :], onesrb_h[:, :], writes=["onesrb"])
        dma("qs", onescol[:, :], onescol_h[:, :], writes=["onescol"])
        dma("qs", pick4b[:, :], pick4b_h[:, :], writes=["pick4b"])
        dma("qs", dtb[:, :], dt_bias_h[:, :], writes=["dtb"])
        dma("qs", amk[:, :], dtmask_h[:, :], writes=["amk"])
        dma("qs", alogs[:, :], a_log_h[:, :], writes=["alogs"])
        dma("qs", drow[:, :], d_row_h[:, :], writes=["drow"])
        dma("qs", gbc[:, :], gate_b_h[:, :], writes=["gbc"])
        dma("qs", nwc[:, :], norm_w_h[:, :], writes=["nwc"])
        dma("qs", g1s[:, :], g1c_h[:, :], writes=["g1s"])
        dma("qs", be1s[:, :], be1c_h[:, :], writes=["be1s"])
        dma("qs", g2s[:, :], g2c_h[:, :], writes=["g2s"])
        dma("qs", x32[64:128, 0:2048], s0_h[:, :], writes=["x32"])

        dma("qs", wpf[:, 0:72], conv_w_h[:, :], writes=["yg"])
        dma("qs", wpf[:, 72:90], conv_b_h[:, :], writes=["yg"])
        V(TS(cwh[:, :], wpf[:, 0:72], 0.5), ["yg"], ["cwh"])
        V(TS(cbh[:, :], wpf[:, 72:90], 0.5), ["yg"], ["cbh"])
        S(ACT(acol[:, :], alogs[:, :], AF.Exp), ["alogs"], ["acol"])
        V(TS(acol[:, :], acol[:, :], -1.0), ["acol"], ["acol"])
        Tn(MM(pC[0:64, 0:32], onesr[0:1, 0:64], drow[:, :]),
           ["onesr", "drow"], ["pC"])
        V(TS(dtile[:, :], pC[0:64, 0:32], 1.0), ["pC"], ["dtile"])
        V(TS(stk[64:128, :], x32[64:128, 0:2048], 1.0), ["x32"], ["stk"])
        V(MEMSET(l66[:, :], 0.0), [], ["l66"])
        V(MEMSET(l66[64:66, :], 1.0), [], ["l66"])
        for (bh, bfs) in ((b1f_h, b1fs), (b2f_h, b2fs)):
            bt = "b1fs" if bfs is b1fs else "b2fs"
            for hf in range(2):
                dma("qs", wpf[0:1, 0:2048], bh[0:1, 2048 * hf:2048 * hf + 2048],
                    writes=["yg"])
                V(TS(bfs[0:1, 2048 * hf:2048 * hf + 2048], wpf[0:1, 0:2048],
                     1.0), ["yg"], [bt])
        for (bch, bm, bmt) in ((b1c_h, bm1, "bm1"), (b2c_h, bm2, "bm2")):
            dma("qs", wpf[0:128, 0:32], bch[:, :], writes=["yg"])
            V(TS(wpb[:, 0:32], wpf[:, 0:32], 1.0), ["yg"], ["yg"])
            Tn(MM(pC[0:1, 0:32], onescol[:, :], wpb[:, 0:32]),
               ["onescol", "yg"], ["pC"])
            V(COPY(r_a[0:1, 0:32], pC[0:1, 0:32]), ["pC"], ["r_a"])
            V(COPY(r_a[0:1, 40:44], onesr[0:1, 0:4]), ["onesr"], ["r_a"])
            for e in range(4):
                V(REDUCE(bm[0:1, e:e + 1], r_a[0:1, 8 * e:8 * e + 8], ALU.add),
                  ["r_a"], [bmt])
            V(COPY(r_a[0:1, 40:44], onesr[0:1, 0:4]), ["onesr"], ["r_a"])
            V(TS(bm[:, :], bm[:, :], 1.0 / 1024.0), [bmt], [bmt])
        dma("qs", wpf[0:4, 0:1024], be2r_h[:, :], writes=["yg"])
        V(TS(be2k[:, :], wpf[0:4, 0:1024], 1.0), ["yg"], ["be2k"])
        dma("qs", wpf[:, 0:32], gate_w_h[:, :], writes=["yg"])
        V(TS(gws[:, :], wpf[:, 0:32], 1.0), ["yg"], ["gws"])

        pp = [0]

        def _wp():
            i = pp[0] % 2
            pp[0] += 1
            return wpf_[i], wpb_[i], wpt_[i]


        for j in range(8):
            f, b_, t_ = _wp()
            dma("qs", f[:, 0:2176], w_inT_h[128 * j:128 * j + 128, 0:2176],
                writes=[t_])
            V(TS(b_[:, 0:2048], f[:, 0:2048], 0.5), [t_], [t_])
            V(TS(b_[:, 2048:2176], f[:, 2048:2176], 1.0), [t_], [t_])
            for ec in range(17):
                dma("qc", wb_in[ec, :, j, :], b_[:, 128 * ec:128 * ec + 128],
                    reads=[t_], writes=["d_wbin"])
            f, b_, t_ = _wp()
            dma("qs", f[:, 0:2080], w_inT_h[128 * j:128 * j + 128, 2176:4256],
                writes=[t_])
            V(TS(b_[:, 0:2080], f[:, 0:2080], 1.0), [t_], [t_])
            for ec in range(17, 34):
                w = 128 if ec < 33 else 32
                dma("qc", wb_in[ec, :, j, 0:w],
                    b_[:, 128 * (ec - 17):128 * (ec - 17) + w],
                    reads=[t_], writes=["d_wbin"])
        zhf = zh.bitcast(F32)
        ppf = [yg.bitcast(F32)[:, 1024 * s:1024 * s + 1024] for s in range(4)]
        ppb = [zh[:, 1024 * s:1024 * s + 1024] for s in range(4)]
        pws = [zhf[:, 2048 + s:2049 + s] for s in range(4)]
        lp_iters = []

        def mk_out_iter(j):
            def ld(s):
                dma("qs", ppf[s][:, :], w_outT_h[128 * j:128 * j + 128, :],
                    writes=[f"pp{s}"])

            def rest(s):
                V(TS(ppb[s][:, :], ppf[s][:, :], nwc[:, j:j + 1]),
                  [f"pp{s}", "nwc"], [f"pq{s}"])
                for oc in range(8):
                    dma("qc", wb_out[oc, :, j, :],
                        ppb[s][:, 128 * oc:128 * oc + 128],
                        reads=[f"pq{s}"], writes=["d_wbout"])
            return ld, rest

        def mk_w_iter(wh, wbd, wmv, e, j):
            wmt = "wm1" if wmv is wm1 else "wm2"
            dt_ = "d_wb1" if wbd is wb1 else "d_wb2"

            def ld(s):
                dma("qs", ppf[s][:, :],
                    wh[1024 * e + 128 * j:1024 * e + 128 * j + 128, :],
                    writes=[f"pp{s}"])

            def rest(s):
                V(MEMSET(pws[s], 0.0), [], [f"pw{s}"])
                S(ACT(ppb[s][:, :], ppf[s][:, :], AF.Copy, accum_out=pws[s]),
                  [f"pp{s}", f"pw{s}"], [f"pq{s}", f"pw{s}"])
                V(TS(wmv[:, 4 * j + e:4 * j + e + 1], pws[s], 1.0 / 1024.0),
                  [f"pw{s}"], [wmt])
                for ec in range(8):
                    dma("qc", wbd[8 * e + ec, :, j, :],
                        ppb[s][:, 128 * ec:128 * ec + 128],
                        reads=[f"pq{s}"], writes=[dt_])
            return ld, rest

        for j in range(16):
            lp_iters.append(mk_out_iter(j))
        for (wh, wbd, wmv) in ((w1_h, wb1, wm1), (w2_h, wb2, wm2)):
            for e in range(4):
                for j in range(8):
                    lp_iters.append(mk_w_iter(wh, wbd, wmv, e, j))
        LPN = len(lp_iters)
        LP_NP = 64
        lp_state = [0]

        def lp_win(p):
            return range((LPN * p) // LP_NP, (LPN * (p + 1)) // LP_NP)

        def lp_pump():
            p = lp_state[0]
            lp_state[0] += 1
            if p >= LP_NP:
                return
            for i in lp_win(p):
                lp_iters[i][1](i % 4)
            if p + 1 < LP_NP:
                for i in lp_win(p + 1):
                    lp_iters[i][0](i % 4)

        V(COPY(dbgg[0:1, 0:4], onesr[0:1, 0:4]), ["onesr"],
          ["d_wbin", "yg", "xbc", "pp0", "pp1", "pp2", "pp3", "dbgg"])
        for i in lp_win(0):
            lp_iters[i][0](i % 4)
        V(MEMSET(xbc[:, :], 0.0), [], ["xbc"])

        def vfence():
            V(COPY(dbgg[0:1, 0:4], onesr[0:1, 0:4]), ["onesr"], ["dbgg"])

        def dump(name, ap, tag):
            if name not in _DBG:
                return
            h = nc.dram_tensor("dbg_" + name, [ap.shape[0], ap.free_size()],
                               ap.dtype, kind="ExternalOutput").ap()
            dma("qg", h, ap, reads=[tag] if isinstance(tag, str) else list(tag))

        def emit_tile(u_hbm, t0, main, prefix_mask, tname="", chunk_cb=None):
            wti = [0]

            def load_wt(block_idx, wbd):
                b = wt[wti[0] % 2]
                tag = f"wt{wti[0] % 2}"
                wti[0] += 1
                dtag = ("d_wbin" if wbd is wb_in else
                        ("d_wb1" if wbd is wb1 else "d_wb2"))
                dma("qs", b[:, :], wbd[block_idx, :, :, :], reads=[dtag],
                    writes=[tag])
                return b, tag

            xt = lambda k: "x32" if k < 4 else "x32b"
            for k in range(8):
                dma("qs", x32[:, 512 * k:512 * k + 512],
                    u_hbm[128 * k:128 * k + 128, t0:t0 + 512], writes=[xt(k)])
            for k in range(8):
                xs = x32[:, 512 * k:512 * k + 512]
                S(ACT(xbfb[:, :], xs, AF.Copy), [xt(k)], ["xbfb"])
                sq = [sq1, sq2][k % 2]
                sqt = ["sq1", "sq2"][k % 2]
                S(ACT(sq[:, :], xs, AF.Square), [xt(k)], [sqt])
                Tn(MM(pM[0:1, 0:512], onescol[:, :], xbfb[:, :],
                      start=(k == 0), stop=(k == 7)),
                   ["onescol", "xbfb"], ["pM"])
                Tn(MM(pM[0:1, 512:1024], onescol[:, :], sq[:, :],
                      start=(k == 0), stop=(k == 7)),
                   ["onescol", sqt], ["pM"])
            V(TS(r_m[:, :], pM[0:1, 0:512], 1.0 / 1024.0), ["pM"], ["r_m"])
            V(TT(r_a[:, :], r_m[:, :], r_m[:, :]), ["r_m"], ["r_a"])
            V(STT(r_b[:, :], pM[0:1, 512:1024], 1.0 / 1024.0, r_a[:, :],
                  ALU.mult, ALU.subtract), ["pM", "r_a"], ["r_b"])
            S(ACT(r_a[:, :], r_b[:, :], AF.Ln, bias=LN_EPS), ["r_b", "cst"], ["r_a"])
            S(ACT(r_b[:, :], r_a[:, :], AF.Exp, scale=-0.5), ["r_a", "cst"], ["r_b"])
            Tn(MM(pM[:, 0:512], onesr[0:1, 0:128], r_m[:, :]),
               ["onesr", "r_m"], ["pM"])
            Tn(MM(pM[:, 512:1024], onesr[0:1, 0:128], r_b[:, :]),
               ["onesr", "r_b"], ["pM"])
            V(COPY(mbc[:, :], pM[:, 0:512]), ["pM"], ["slz"])
            V(COPY(ibc[:, :], pM[:, 512:1024]), ["pM"], ["slzt"])
            for k in range(8):
                V(TT(cva[:, :], x32[:, 512 * k:512 * k + 512], mbc[:, :],
                     ALU.subtract), [xt(k), "slz"], ["cva"])
                G(TT(xn[:, 512 * k:512 * k + 512], cva[:, :], ibc[:, :]),
                  ["cva", "slzt"], ["xn"])
            dump(tname + "_xn", xn, "xn")

            for b in range(18):
                V(COPY(xbc[:, 515 * b:515 * b + 3],
                       xbc[:, 515 * b + 512:515 * b + 515]), ["xbc"], ["xbc"])

            ping = [0]

            def inproj_chunk(ec, dest_fn, m_rows=128, tp_extra=None):
                wtb, wtag = load_wt(ec, wb_in)
                ps, pst = (pA, "pA") if ping[0] % 2 == 0 else (pB, "pB")
                ping[0] += 1
                for k in range(8):
                    Tn(MM(ps[0:m_rows, 0:512],
                          wtb[:, 128 * k:128 * k + m_rows],
                          xn[:, 512 * k:512 * k + 512],
                          start=(k == 0), stop=(k == 7), tp=tp_extra),
                       [wtag, "xn"], [pst])
                dest_fn(ps, pst)
                return ps, pst

            if main:
                ecs = list(range(16, 34)) + list(range(0, 16))
            else:
                ecs = list(range(16, 34))
            for ec in ecs:
                if 16 <= ec < 32:
                    b = ec - 16
                    inproj_chunk(ec, lambda ps, pst, b=b: S(
                        ACT(xbc[:, 515 * b + 3:515 * b + 515], ps[:, 0:512],
                            AF.Copy), [pst], ["xbc"]))
                elif ec == 32:
                    wtb, wtag = load_wt(32, wb_in)
                    ps, pst = (pA, "pA") if ping[0] % 2 == 0 else (pB, "pB")
                    ping[0] += 1
                    for k in range(8):
                        Tn(MM(ps[0:64, 0:512], wtb[:, 128 * k:128 * k + 64],
                              xn[:, 512 * k:512 * k + 512],
                              start=(k == 0), stop=(k == 7)),
                           [wtag, "xn"], [pst])
                    for k in range(8):
                        Tn(MM(ps[64:128, 0:512], wtb[:, 128 * k + 64:128 * k + 128],
                              xn[:, 512 * k:512 * k + 512],
                              start=(k == 0), stop=(k == 7), tp=(0, 64)),
                           [wtag, "xn"], [pst])
                    S(ACT(xbc[:, 515 * 16 + 3:515 * 16 + 515], ps[:, 0:512],
                          AF.Copy), [pst], ["xbc"])
                    ps2, pst2 = (pA, "pA") if ping[0] % 2 == 0 else (pB, "pB")
                    ping[0] += 1
                    for k in range(8):
                        Tn(MM(ps2[64:128, 0:512], wtb[:, 128 * k:128 * k + 64],
                              xn[:, 512 * k:512 * k + 512],
                              start=(k == 0), stop=(k == 7), tp=(0, 64)),
                           [wtag, "xn"], [pst2])
                    S(ACT(xbc[64:128, 515 * 17 + 3:515 * 17 + 515],
                          ps2[64:128, 0:512], AF.Copy), [pst2], ["xbc"])
                elif ec == 33:
                    wtb, wtag = load_wt(33, wb_in)
                    for k in range(8):
                        Tn(MM(pC[0:32, 0:512], wtb[:, 128 * k:128 * k + 32],
                              xn[:, 512 * k:512 * k + 512],
                              start=(k == 0), stop=(k == 7)),
                           [wtag, "xn"], ["pC"])
                    S(ACT(dts[:, :], pC[0:32, 0:512], AF.Exp, bias=dtb[:, :]),
                      ["pC", "dtb"], ["dts"])
                    V(TS(dts[:, :], dts[:, :], 1.0, None, ALU.add), ["dts"], ["dts"])
                    S(ACT(dtv[:, :], dts[:, :], AF.Ln), ["dts"], ["dtv"])
                    if not main:
                        V(TS(dtv[:, :], dtv[:, :], amk[:, :]), ["dtv", "amk"],
                          ["dtv"])
                    V(TS(adt[:, :], dtv[:, :], acol[:, :]), ["dtv", "acol"],
                      ["adt"])
                    V(COPY(dbgg[0:1, 0:4], onesr[0:1, 0:4]), ["onesr"],
                      ["pQ", "pM", "pC", "dtbk0", "dtbk1", "dtbk4", "dbgg"])
                    for c in range(NCH):
                        cc = 64 * c
                        p = c % 2
                        bq = pQ[0:64, 0:512] if p == 0 else pM[0:64, 512:1024]
                        bk = f"dtbk{p}"
                        at = adtT[:, 32 * p:32 * p + 32]
                        ds = dscr[:, 32 * p:32 * p + 32]
                        su = sumar[0:1, 32 * p:32 * p + 32]
                        Tn(TRANS(bq[:, 0:32], adt[:, cc:cc + 64],
                                 idf[0:32, 0:32]), ["adt", "idf"], [bk])
                        V(COPY(at, bq[:, 0:32]), [bk], [f"adtT{p}"])
                        Tn(MM(bq[:, 32:64], lt64[:, :], at),
                           ["lt64", f"adtT{p}"], [bk])
                        V(COPY(cumAT[:, 32 * c:32 * c + 32], bq[:, 32:64]),
                          [bk], [f"cumAT{c}"])
                        Tn(TRANS(bq[0:32, 64:128],
                                 cumAT[:, 32 * c:32 * c + 32],
                                 idf[0:64, 0:64]), [f"cumAT{c}", "idf"], [bk])
                        V(COPY(cumA[:, cc:cc + 64], bq[0:32, 64:128]),
                          [bk], [f"cumA{c}"])
                        Tn(TRANS(bq[:, 128:160], dtv[:, cc:cc + 64],
                                 idf[0:32, 0:32]), ["dtv", "idf"], [bk])
                        V(TS(dtT[:, 32 * c:32 * c + 32], bq[:, 128:160],
                             1.0), [bk], [f"dtT{c}"])
                        dma("qs", su, cumA[0:32, cc + 63:cc + 64],
                            reads=[f"cumA{c}"], writes=[f"sumar{p}"])
                        Tn(MM(bq[:, 160:192], onesr[0:1, 0:64], su),
                           ["onesr", f"sumar{p}"], [bk])
                        V(TT(ds, bq[:, 160:192],
                             cumAT[:, 32 * c:32 * c + 32], ALU.subtract),
                          [bk, f"cumAT{c}"], [f"dscr{p}"])
                        S(ACT(dcyT[:, 32 * c:32 * c + 32], ds, AF.Exp),
                          [f"dscr{p}"], [f"dcyT{c}"])
                        Tn(MM(pC[64:128, 256 + 32 * p:288 + 32 * p],
                              onesr[0:1, 0:64], su, tp=(0, 64)),
                           ["onesr", f"sumar{p}"], ["dtbk4"])
                        S(ACT(dchbc[64:128, 32 * c:32 * c + 32],
                              pC[64:128, 256 + 32 * p:288 + 32 * p], AF.Exp),
                          ["dtbk4"], [f"dchbc{c}"])
                        if chunk_cb is not None:
                            chunk_cb()
                    V(COPY(dbgg[0:1, 0:4], onesr[0:1, 0:4]), ["onesr"],
                      ["pQ", "pM", "pC", "dtbk0", "dtbk1", "dtbk4", "dbgg"])
                else:
                    inproj_chunk(ec, lambda ps, pst, ec=ec: S(
                        ACT(zh[:, 512 * ec:512 * ec + 512], ps[:, 0:512],
                            AF.Copy), [pst], ["zh"]))

            for b in range(18):
                lo = 64 if b == 17 else 0
                xb = lambda k: xbc[lo:128, 515 * b + k:515 * b + k + 512]
                V(TS(cva[lo:128, :], xb(0), cwh[lo:128, 4 * b:4 * b + 1],
                     cbh[lo:128, b:b + 1], ALU.mult, ALU.add),
                  ["xbc", "cwh", "cbh"], ["cva"])
                for k in (1, 2, 3):
                    V(STT(cva[lo:128, :], xb(k), cwh[lo:128, 4 * b + k:4 * b + k + 1],
                          cva[lo:128, :]), ["xbc", "cwh", "cva"], ["cva"])
                S(ACT(ths[lo:128, :], cva[lo:128, :], AF.Tanh), ["cva"], ["ths"])
                V(STT(xc[lo:128, 512 * b:512 * b + 512], ths[lo:128, :], 1.0,
                      cva[lo:128, :], ALU.add, ALU.mult),
                  ["ths", "cva"], ["xc"])

            dump(tname + "_zh", zh, "zh")
            dump(tname + "_xc", xc, "xc")
            dump(tname + "_dtv", dtv, "dtv")
            dump(tname + "_cumA", cumA, [f"cumA{i}" for i in range(NCH)])
            dump(tname + "_cumAT", cumAT, [f"cumAT{i}" for i in range(NCH)])
            dump(tname + "_dtT", dtT, [f"dtT{i}" for i in range(NCH)])
            dump(tname + "_dcyT", dcyT, [f"dcyT{i}" for i in range(NCH)])
            dump(tname + "_dchbc", dchbc, [f"dchbc{i}" for i in range(NCH)])
            for c in range(NCH):
                cc = 64 * c
                if main:
                    V(TS(l66src[:, 0:64], cumA[:, cc:cc + 64], -1.0),
                      [f"cumA{c}"], ["l66src"])
                    vfence()
                    V(STT(l66src[:, 64:128], cumA[:, cc:cc + 64], -1.0,
                          l66src[:, 0:64], ALU.mult, ALU.subtract),
                      [f"cumA{c}", "l66src"], ["l66src"])
                    vfence()
                    dma("qc", l66[0:64, 0:64], l66src[:, :],
                        reads=["l66src"], writes=["l66"])
                    V(TS(phl[:, 0:64], cumA[:, cc:cc + 64], 1.0),
                      [f"cumA{c}"], ["phl"])
                    vfence()
                    V(STT(phl[:, 64:128], cumA[:, cc:cc + 64], 1.0,
                          phl[:, 0:64], ALU.mult, ALU.subtract),
                      [f"cumA{c}", "phl"], ["phl"])
                    vfence()
                    dma("qc", rhs66[64:65, :], phl[:, 0:64],
                        reads=["phl"], writes=["rhs66"])
                    dma("qc", rhs66[65:66, :], phl[:, 64:128],
                        reads=["phl"], writes=["rhs66"])
                    Tn(MM(pC[0:64, 64:128], xc[64:128, 512 * 17 + cc:512 * 17 + cc + 64],
                          xc[64:128, 512 * 16 + cc:512 * 16 + cc + 64],
                          tp=(64, 0)), ["xc"], ["pC"])
                    V(TT(gsb[:, :], pC[0:64, 64:128], msl[:, :]),
                      ["pC", "msl"], ["gsb"])
                    vfence()
                    for half in range(2):
                        for gq in range(2):
                            Tn(MM(pM[:, 512 * gq:512 * gq + 512], l66[:, :],
                                  rhs66[:, 1024 * half + 512 * gq:
                                        1024 * half + 512 * gq + 512]),
                               ["l66", "rhs66"], ["pM"])
                        V(TS(dfc[:, :], pM[:, :], 0.0, None, ALU.min),
                          ["pM"], ["dfc"])
                        S(ACT(msb[:, 1024 * half:1024 * half + 1024],
                              dfc[:, :], AF.Exp), ["dfc"], ["msb"])
                    V(TT(msb[0:64, :], msb[0:64, :],
                         fview(gsb[0:64, 0:64], [(0, 32), (1, 64)])),
                      ["gsb", "msb"], ["msb"])
                    G(TT(msb[64:128, :], msb[64:128, :],
                         fview(xc[64:128, 512 * 16 + cc:512 * 16 + cc + 64],
                               [(0, 32), (1, 64)])),
                      ["xc", "msb"], ["msb"])

                for half in range(2):
                    for j8 in range(8):
                        j = 8 * half + j8
                        Tn(TRANS(pXT[0:64, 128 * j8:128 * j8 + 128],
                                 xc[:, 512 * j + cc:512 * j + cc + 64],
                                 idb[:, :]), ["xc", "idb"], ["pXT"])
                    dtview = fview(dtT[0:64, 32 * c + 16 * half:32 * c + 16 * half + 16],
                                   [(2, 8), (1, 2), (0, 64)])
                    V(TT(stk[0:64, 1024 * half:1024 * half + 1024],
                         pXT[:, :], dtview), ["pXT", f"dtT{c}"], ["stk"])
                    if main:
                        dview = fview(dtile[0:64, 16 * half:16 * half + 16],
                                      [(2, 8), (1, 2), (0, 64)])
                        V(TT(dterm[0:64, 1024 * half:1024 * half + 1024],
                             pXT[:, :], dview), ["pXT", "dtile"], ["dterm"])
                G(TT(xdt[:, :], stk[0:64, :],
                     fview(dcyT[0:64, 32 * c:32 * c + 32], [(1, 32), (0, 64)])),
                  ["stk", f"dcyT{c}"], ["xdt"])
                dump(tname + f"_c{c}_stkx", stk[0:64, :], "stk")
                dump(tname + f"_c{c}_xdt", xdt, "xdt")
                Tn(TRANS(pXT[0:64, 0:64],
                         xc[64:128, 512 * 17 + cc:512 * 17 + cc + 64],
                         idb[64:128, 64:128], tp=(64, 0)), ["xc", "idb"], ["pXT"])
                V(TS(btb[:, :], pXT[0:64, 0:64], 1.0), ["pXT"], ["btb"])

                if main:
                    dump(tname + f"_c{c}_msb", msb, "msb")
                    dump(tname + f"_c{c}_gsb", gsb, "gsb")
                if main:
                    for q in range(4):
                        ps, pst = (pQ, "pQ") if q % 2 == 0 else (pA, "pA")
                        for h8 in range(8):
                            h = 8 * q + h8
                            Tn(MM(ps[0:64, 64 * h8:64 * h8 + 64],
                                  msb[:, 64 * h:64 * h + 64],
                                  stk[:, 64 * h:64 * h + 64]),
                               ["msb", "stk"], [pst])
                        V(TT(ytk[:, 512 * q:512 * q + 512], ps[0:64, :],
                             dterm[:, 512 * q:512 * q + 512], ALU.add),
                          [pst, "dterm"], ["ytk"])

                for gq in range(4):
                    off = 512 * (gq % 2)
                    Tn(MM(pM[64:128, off:off + 512], btb[:, :],
                          xdt[:, 512 * gq:512 * gq + 512], tp=(0, 64)),
                       ["btb", "xdt"], ["pM"])
                    if gq == 0:
                        G(TT(sd[64:128, :], stk[64:128, :],
                             fview(dchbc[64:128, 32 * c:32 * c + 32],
                                   [(1, 32), (0, 64)])),
                          ["stk", f"dchbc{c}"], ["sd"])
                    V(TT(stk[64:128, 512 * gq:512 * gq + 512],
                         sd[64:128, 512 * gq:512 * gq + 512],
                         pM[64:128, off:off + 512], ALU.add),
                      ["sd", "pM"], ["stk"])
                dump(tname + f"_c{c}_S", stk[64:128, :], "stk")

                if main:
                    zview = fview(zh[:, cc:cc + 64], [(512, 16), (1, 64)])
                    S(ACT(slzt[:, :], zview, AF.Tanh), ["zh"], ["slzt"])
                    V(STT(slz[:, :], slzt[:, :], 1.0, zview, ALU.add, ALU.mult),
                      ["slzt", "zh"], ["slz"])
                    for j in range(16):
                        Tn(TRANS(pYT[:, 64 * j:64 * j + 64],
                                 ytk[0:64, 128 * j:128 * j + 128],
                                 idb[0:64, 0:64]), ["ytk", "idb"], ["pYT"])
                    ygview = fview(yg[:, cc:cc + 64], [(512, 16), (1, 64)])
                    V(TT(ygview, pYT[:, :], slz[:, :]), ["pYT", "slz"], ["yg"])
                    dump(tname + f"_c{c}_ytk", ytk, "ytk")
                if chunk_cb is not None:
                    chunk_cb()

            if not main:
                return

            for k in range(8):
                Tn(MM(pC[0:4, 0:512], gws[:, 4 * k:4 * k + 4],
                      xn[:, 512 * k:512 * k + 512],
                      start=(k == 0), stop=(k == 7)), ["gws", "xn"], ["pC"])
            V(TS(gsr[:, :], pC[0:4, 0:512], 1.0, gbc[:, :], ALU.mult, ALU.add),
              ["pC", "gbc"], ["gsr"])
            vfence()
            dump(tname + "_gsr", gsr, "gsr")
            for gq in range(4):
                Tn(TRANS(pC[0:128, 448 + 4 * gq:448 + 4 * gq + 4],
                         gsr[0:4, 128 * gq:128 * gq + 128], idf[0:4, 0:4]),
                   ["gsr", "idf"], ["pC"])
            V(COPY(gst[:, :], pC[:, 448:464]), ["pC"], ["gst"])
            vfence()
            dump(tname + "_gst", gst, "gst")
            gv = lambda t: fview(t[0:128, 0:4], [(1, 4), (0, 4)])
            vfence()
            for gq in range(4):
                V(REDUCE(gr1[:, gq:gq + 1], gst[:, 4 * gq:4 * gq + 4], ALU.max),
                  ["gst"], ["gr1"])
            vfence()
            dump(tname + "_gr1", gr1, "gr1")
            V(TT(gw1[:, :], gst[:, :], gv(gr1), ALU.is_equal), ["gst", "gr1"],
              ["gw1"])
            vfence()
            dump(tname + "_gw1a", gw1, "gw1")
            V(STT(gw2[:, :], gw1[:, :], -1e30, gst[:, :], ALU.mult, ALU.add),
              ["gw1", "gst"], ["gw2"])
            vfence()
            vfence()
            for gq in range(4):
                V(REDUCE(gr2[:, gq:gq + 1], gw2[:, 4 * gq:4 * gq + 4], ALU.max),
                  ["gw2"], ["gr2"])
            vfence()
            dump(tname + "_m2", gr2, "gr2")
            V(TT(gw1[:, :], gst[:, :], gv(gr2), ALU.is_ge), ["gst", "gr2"],
              ["gw1"])
            vfence()
            dump(tname + "_top2", gw1, "gw1")
            for gq in range(4):
                V(REDUCE(dbgg[:, gq:gq + 1], gst[:, 4 * gq:4 * gq + 4],
                         ALU.max), ["gst"], ["dbgg"])
            V(COPY(dbgg[:, 4:20], gst[:, :]), ["gst"], ["dbgg"])
            dump(tname + "_dbgg", dbgg, "dbgg")
            V(TT(gw2[:, :], gst[:, :], gv(gr1), ALU.subtract), ["gst", "gr1"],
              ["gw2"])
            vfence()
            dump(tname + "_gw2s", gw2, "gw2")
            S(ACT(gst[:, :], gw2[:, :], AF.Exp), ["gw2"], ["gst"])
            V(TT(gw2[:, :], gst[:, :], gw1[:, :]), ["gst", "gw1"], ["gw2"])
            vfence()
            vfence()
            for gq in range(4):
                V(REDUCE(dbgg[:, 16 + gq:16 + gq + 1],
                         gw2[:, 4 * gq:4 * gq + 4], ALU.add),
                  ["gw2"], ["dbgg"])
            vfence()
            dump(tname + "_den", dbgg[:, 16:20], "dbgg")
            V(RECIP(gr2[:, :], dbgg[:, 16:20]), ["dbgg"], ["gr2"])
            vfence()
            dump(tname + "_rec", gr2, "gr2")
            V(TT(gw1[:, :], gw2[:, :], gv(gr2)), ["gw2", "gr2"], ["gw1"])
            vfence()
            dump(tname + "_gwf", gw1, "gw1")
            dump(tname + "_gex", gw2, "gw2")
            for gq in range(4):
                Tn(TRANS(pC[0:4, 0 + 128 * gq:0 + 128 * gq + 128],
                         gw1[:, 4 * gq:4 * gq + 4], idf[:, :]),
                   ["gw1", "idf"], ["pC"])
            V(COPY(wgt[:, :], pC[0:4, 0:512]), ["pC"], ["wgt"])
            vfence()
            dump(tname + "_wgt", wgt, "wgt")
            dump(tname + "_yg", yg, "yg")
            V(TS(wgtb[:, :], wgt[:, :], 1.0), ["wgt"], ["wgtb"])

            for k in range(8):
                Tn(MM(pM[0:4, 0:512], wm1[:, 4 * k:4 * k + 4],
                      xn[:, 512 * k:512 * k + 512],
                      start=(k == 0), stop=False), ["wm1", "xn"], ["pM"])
            Tn(MM(pM[0:4, 0:512], bm1[:, :], onesr[:, :], start=False,
                  stop=True), ["bm1", "onesr", "pM"], ["pM"])
            V(TS(statCb[:, :], pM[0:4, 0:512], -1.0), ["pM"], ["statCb"])
            for ec in range(8):
                ps, pst = (pA, "pA") if ping[0] % 2 == 0 else (pB, "pB")
                ping[0] += 1
                Tn(MM(ps[:, 0:512], be2k[:, 128 * ec:128 * ec + 128],
                      wgtb[:, :]), ["be2k", "wgtb"], [pst])
                V(TT(accb[:, 512 * ec:512 * ec + 512], ps[:, 0:512],
                     xn[:, 512 * ec:512 * ec + 512], ALU.add),
                  [pst, "xn"], ["accb"])

            dump(tname + "_statCb", statCb, "statCb")
            for e in range(4):
                praw = praw_[e % 2]
                prt = ["x32", "x32b"][e % 2]
                for ec in range(8):
                    wtb, wtag = load_wt(8 * e + ec, wb1)
                    ps, pst = (pA, "pA") if ping[0] % 2 == 0 else (pB, "pB")
                    ping[0] += 1
                    for k in range(8):
                        Tn(MM(ps[:, 0:512], wtb[:, 128 * k:128 * k + 128],
                              xn[:, 512 * k:512 * k + 512],
                              start=(k == 0), stop=False), [wtag, "xn"], [pst])
                    Tn(MM(ps[:, 0:512],
                          b1fs[0:1, 1024 * e + 128 * ec:1024 * e + 128 * ec + 128],
                          onesrb[:, :], start=False, stop=False),
                       ["b1fs", "onesrb", pst], [pst])
                    Tn(MM(ps[:, 0:512], pick4b[:, 128 * e:128 * e + 128],
                          statCb[:, :], start=False, stop=True),
                       ["pick4b", "statCb", pst], [pst])
                    S(ACT(praw[:, 512 * ec:512 * ec + 512], ps[:, 0:512],
                          AF.Copy), [pst], [prt])
                    sq = [sq1, sq2][ec % 2]
                    sqt = ["sq1", "sq2"][ec % 2]
                    S(ACT(sq[:, :], praw[:, 512 * ec:512 * ec + 512],
                          AF.Square), [prt], [sqt])
                    Tn(MM(pC[0:1, 0:512], onescol[:, :], sq[:, :],
                          start=(ec == 0), stop=(ec == 7)),
                       ["onescol", sqt], ["pC"])
                V(TS(r_b[:, :], pC[0:1, 0:512], 1.0 / 1024.0), ["pC"], ["r_b"])
                S(ACT(r_a[:, :], r_b[:, :], AF.Ln, bias=LN_EPS), ["r_b", "cst"], ["r_a"])
                S(ACT(r_b[:, :], r_a[:, :], AF.Exp, scale=-0.5), ["r_a", "cst"], ["r_b"])
                Tn(MM(pM[:, 0:512], onesr[0:1, 0:128], r_b[:, :]),
                   ["onesr", "r_b"], ["pM"])
                for ec in range(8):
                    V(STT(tsc[:, :], praw[:, 512 * ec:512 * ec + 512],
                          g1s[:, 8 * e + ec:8 * e + ec + 1], pM[:, 0:512],
                          ALU.mult, ALU.mult), [prt, "g1s", "pM"], ["tsc"])
                    V(TS(h1[:, 512 * ec:512 * ec + 512], tsc[:, :],
                         be1s[:, 8 * e + ec:8 * e + ec + 1], 0.0,
                         ALU.add, ALU.max), ["tsc", "be1s"], ["zh"])
                dump(tname + f"_e{e}_h1", h1, "zh")
                for k in range(8):
                    Tn(MM(pM[0:4, 512:1024], wm2[:, 4 * k:4 * k + 4],
                          h1[:, 512 * k:512 * k + 512],
                          start=(k == 0), stop=False), ["wm2", "zh"], ["pM"])
                Tn(MM(pM[0:4, 512:1024], bm2[:, :], onesr[:, :], start=False,
                      stop=True), ["bm2", "onesr", "pM"], ["pM"])
                V(TS(statC2b[:, :], pM[0:4, 512:1024], -1.0), ["pM"], ["statC2b"])
                for ec in range(8):
                    wtb, wtag = load_wt(8 * e + ec, wb2)
                    ps, pst = (pA, "pA") if ping[0] % 2 == 0 else (pB, "pB")
                    ping[0] += 1
                    for k in range(8):
                        Tn(MM(ps[:, 0:512], wtb[:, 128 * k:128 * k + 128],
                              h1[:, 512 * k:512 * k + 512],
                              start=(k == 0), stop=False), [wtag, "zh"], [pst])
                    Tn(MM(ps[:, 0:512],
                          b2fs[0:1, 1024 * e + 128 * ec:1024 * e + 128 * ec + 128],
                          onesrb[:, :], start=False, stop=False),
                       ["b2fs", "onesrb", pst], [pst])
                    Tn(MM(ps[:, 0:512], pick4b[:, 128 * e:128 * e + 128],
                          statC2b[:, :], start=False, stop=True),
                       ["pick4b", "statC2b", pst], [pst])
                    S(ACT(praw[:, 512 * ec:512 * ec + 512], ps[:, 0:512],
                          AF.Copy), [pst], [prt])
                    sq = [sq1, sq2][ec % 2]
                    sqt = ["sq1", "sq2"][ec % 2]
                    S(ACT(sq[:, :], praw[:, 512 * ec:512 * ec + 512],
                          AF.Square), [prt], [sqt])
                    Tn(MM(pC[0:1, 0:512], onescol[:, :], sq[:, :],
                          start=(ec == 0), stop=(ec == 7)),
                       ["onescol", sqt], ["pC"])
                V(TS(r_b[:, :], pC[0:1, 0:512], 1.0 / 1024.0), ["pC"], ["r_b"])
                S(ACT(r_a[:, :], r_b[:, :], AF.Ln, bias=LN_EPS), ["r_b", "cst"], ["r_a"])
                S(ACT(r_b[:, :], r_a[:, :], AF.Exp, scale=-0.5), ["r_a", "cst"], ["r_b"])
                dma("qc", wrow[:, :], wgt[e:e + 1, :], reads=["wgt"],
                    writes=["wrow"])
                V(TT(r_a[:, :], r_b[:, :], wrow[:, :]), ["r_b", "wrow"], ["r_a"])
                Tn(MM(pM[:, 512:1024], onesr[0:1, 0:128], r_a[:, :]),
                   ["onesr", "r_a"], ["pM"])
                for ec in range(8):
                    V(STT(tsc[:, :], praw[:, 512 * ec:512 * ec + 512],
                          g2s[:, 8 * e + ec:8 * e + ec + 1], pM[:, 512:1024],
                          ALU.mult, ALU.mult), [prt, "g2s", "pM"], ["tsc"])
                    V(TT(accb[:, 512 * ec:512 * ec + 512],
                         accb[:, 512 * ec:512 * ec + 512], tsc[:, :], ALU.add),
                      ["tsc", "accb"], ["accb"])

            dump(tname + "_accb", accb, "accb")
            for j in range(16):
                sq = [sq1, sq2][j % 2]
                sqt = ["sq1", "sq2"][j % 2]
                S(ACT(sq[:, :], yg[:, 512 * j:512 * j + 512], AF.Square),
                  ["yg"], [sqt])
                Tn(MM(pM[0:1, 0:512], onescol[:, :], sq[:, :],
                      start=(j == 0), stop=(j == 15)), ["onescol", sqt], ["pM"])
            V(TS(r_b[:, :], pM[0:1, 0:512], 1.0 / 2048.0), ["pM"], ["r_b"])
            S(ACT(r_a[:, :], r_b[:, :], AF.Ln, bias=EPS), ["r_b", "cst"], ["r_a"])
            S(ACT(r_b[:, :], r_a[:, :], AF.Exp, scale=-0.5), ["r_a", "cst"], ["r_b"])
            Tn(MM(pC[:, 0:512], onesr[0:1, 0:128], r_b[:, :]),
               ["onesr", "r_b"], ["pC"])
            V(COPY(mbc[:, :], pC[:, 0:512]), ["pC"], ["slz"])
            for k in range(8):
                dma("qs", x32[:, 512 * k:512 * k + 512],
                    u_hbm[128 * k:128 * k + 128, t0:t0 + 512], writes=[xt(k)])
            for oc in range(8):
                wob = wo[oc % 2]
                wotag = ["wo0", "wo1"][oc % 2]
                dma("qs", wob[:, :], wb_out[oc, :, :, :],
                    reads=["d_wbout"], writes=[wotag])
                ps, pst = (pA, "pA") if ping[0] % 2 == 0 else (pB, "pB")
                ping[0] += 1
                for k in range(16):
                    Tn(MM(ps[:, 0:512], wob[:, 128 * k:128 * k + 128],
                          yg[:, 512 * k:512 * k + 512],
                          start=(k == 0), stop=(k == 15)), [wotag, "yg"], [pst])
                oh = outs_[oc % 2]
                ot = outst_[oc % 2]
                V(TT(oh, ps[:, 0:512], mbc[:, :]), [pst, "slz"], [ot])
                G(TT(oh, oh, accb[:, 512 * oc:512 * oc + 512], ALU.add),
                  [ot, "accb"], [ot])
                V(TT(oh, oh, x32[:, 512 * oc:512 * oc + 512], ALU.add),
                  [ot, xt(oc)], [ot])
                dma("qc", y_out[128 * oc:128 * oc + 128, t0:t0 + 512], oh,
                    reads=[ot])

        dump("wm1", wm1, "wm1")
        dump("wm2", wm2, "wm2")
        dump("bm1", bm1, "bm1")
        dump("bm2", bm2, "bm2")
        dump("b1fs", b1fs, "b1fs")
        dump("gws_d", gws, "gws")
        for t in range(T_PASS // TILE):
            emit_tile(u_pref, TILE * t, main=False, prefix_mask=True,
                      tname=f"p{t}", chunk_cb=lp_pump)
        assert lp_state[0] == LP_NP, lp_state
        V(COPY(dbgg[0:1, 0:4], onesr[0:1, 0:4]), ["onesr"],
          ["d_wbout", "d_wb1", "d_wb2", "yg", "zh",
           "pp0", "pp1", "pp2", "pp3", "pq0", "pq1", "pq2", "pq3",
           "pw0", "pw1", "pw2", "pw3", "dbgg"])
        for t in range(T_PASS // TILE):
            emit_tile(u_main, TILE * t, main=True, prefix_mask=False,
                      tname=f"m{t}")

        em.finalize()
    return nc



_NC_CACHE = {}


def _consts():
    identb = np.eye(128, dtype=bf16)
    identf = np.eye(128, dtype=np.float32)
    lt64 = np.triu(np.ones((64, 64), dtype=np.float32))
    mask_sl = np.triu(np.ones((64, 64), dtype=bf16))
    e32x2 = np.zeros((64, 2048), dtype=bf16)
    for h in range(32):
        e32x2[2 * h, 64 * h:64 * h + 64] = 1
        e32x2[2 * h + 1, 64 * h:64 * h + 64] = 1
    pick4b = np.zeros((4, 512), dtype=bf16)
    for e in range(4):
        pick4b[e, 128 * e:128 * e + 128] = 1
    onescol = np.ones((128, 1), dtype=bf16)
    onesr = np.ones((1, 512), dtype=np.float32)
    onesrb = np.ones((1, 512), dtype=bf16)
    return dict(identb=identb, identf=identf, lt64=lt64, mask_sl=mask_sl,
                e32x2=e32x2, pick4b=pick4b, onescol=onescol, onesr=onesr,
                onesrb=onesrb)


def kernel(L0_attn_u, L0_attn_in_proj_w, L0_attn_conv_w, L0_attn_conv_b,
           L0_attn_init_states, L0_attn_dt_bias, L0_attn_A_log, L0_attn_D,
           L0_attn_norm_w, L0_attn_out_proj_w, L0_ffn_gate_W, L0_ffn_gate_b,
           L0_ffn_W1, L0_ffn_b1, L0_ffn_g1, L0_ffn_be1, L0_ffn_W2, L0_ffn_b2,
           L0_ffn_g2, L0_ffn_be2):
    u = np.ascontiguousarray(np.asarray(L0_attn_u, dtype=np.float32))

    f32 = lambda a: np.ascontiguousarray(np.asarray(a, dtype=np.float32))
    w_inT = f32(L0_attn_in_proj_w).T.copy()
    cw = f32(L0_attn_conv_w)
    cb = f32(L0_attn_conv_b)
    cwt = np.zeros((128, 18 * 4), dtype=np.float32)
    cbt = np.zeros((128, 18), dtype=np.float32)
    for b in range(17):
        cwt[:, 4 * b:4 * b + 4] = cw[128 * b:128 * b + 128]
        cbt[:, b] = cb[128 * b:128 * b + 128]
    cwt[64:128, 68:72] = cw[2048:2112]
    cbt[64:128, 17] = cb[2048:2112]
    s0 = np.ascontiguousarray(
        np.transpose(f32(L0_attn_init_states), (2, 0, 1)).reshape(64, 2048))
    norm_w = np.ascontiguousarray(
        f32(L0_attn_norm_w).reshape(16, 128).T)
    w_outT = f32(L0_attn_out_proj_w).T.copy()
    gate_w = np.ascontiguousarray(
        f32(L0_ffn_gate_W).reshape(8, 128, 4).transpose(1, 0, 2).reshape(128, 32))
    colmajor = lambda a: np.ascontiguousarray(
        f32(a).reshape(4, 8, 128).transpose(2, 0, 1).reshape(128, 32))
    g1c, be1c, g2c = colmajor(L0_ffn_g1), colmajor(L0_ffn_be1), colmajor(L0_ffn_g2)
    w1 = f32(L0_ffn_W1).reshape(4096, 1024)
    w2 = f32(L0_ffn_W2).reshape(4096, 1024)
    b1f = f32(L0_ffn_b1).reshape(1, 4096)
    b2f = f32(L0_ffn_b2).reshape(1, 4096)
    b1c = colmajor(L0_ffn_b1)
    b2c = colmajor(L0_ffn_b2)
    be2r = f32(L0_ffn_be2)

    consts = _consts()
    consts["cvals"] = np.array([[LN_EPS, -0.5, 0.0, 0.0]], dtype=np.float32)
    shared = dict(w_inT=w_inT, conv_w=cwt, conv_b=cbt,
                  dt_bias=f32(L0_attn_dt_bias).reshape(32, 1),
                  a_log=f32(L0_attn_A_log).reshape(32, 1),
                  d_row=f32(L0_attn_D).reshape(1, 32),
                  s0=s0, norm_w=norm_w, w_outT=w_outT, gate_w=gate_w,
                  gate_b=f32(L0_ffn_gate_b).reshape(4, 1),
                  w1=w1, w2=w2, b1f=b1f, b2f=b2f, b1c=b1c, b2c=b2c, be2r=be2r,
                  g1c=g1c, be1c=be1c, g2c=g2c, **consts)

    in_maps = []
    for core in range(8):
        s, half = core // 2, core % 2
        m = dict(shared)
        m["u_main"] = np.ascontiguousarray(u[s, 2048 * half:2048 * half + 2048].T)
        if half == 1:
            m["u_pref"] = np.ascontiguousarray(u[s, 0:2048].T)
            m["dtmask"] = np.ones((32, 1), dtype=np.float32)
        else:
            m["u_pref"] = np.zeros((1024, 2048), dtype=np.float32)
            m["dtmask"] = np.zeros((32, 1), dtype=np.float32)
        in_maps.append(m)

    if "nc" not in _NC_CACHE:
        _NC_CACHE["nc"] = build_nc()
    nc = _NC_CACHE["nc"]
    res = run_bass_kernel_spmd(nc, in_maps, core_ids=list(range(8)))

    out = np.zeros((4, 4096, 1024), dtype=np.float32)
    for core in range(8):
        s, half = core // 2, core % 2
        out[s, 2048 * half:2048 * half + 2048] = \
            np.asarray(res.results[core]["y_out"]).T
    return out


if __name__ == "__main__":
    import time
    t0 = time.time()
    nc = build_nc()
    print(f"built ok in {time.time()-t0:.1f}s")


# version 12
# speedup vs baseline: 1.0079x; 1.0079x over previous
import numpy as np
import ml_dtypes
from contextlib import ExitStack

import concourse.bass as bass
import concourse.mybir as mybir
from concourse.bass_utils import run_bass_kernel_spmd

F32 = mybir.dt.float32
BF16 = mybir.dt.bfloat16
AF = mybir.ActivationFunctionType
ALU = mybir.AluOpType
AX = mybir.AxisListType
bf16 = ml_dtypes.bfloat16

T_PASS = 2048
TILE = 512
_DBG = {}
_DBG_OUTS = []
NCH = TILE // 64
LN_EPS = 1e-5
EPS = 1e-5



class _Op:
    __slots__ = ("eng", "fn", "reads", "writes", "dma", "deps", "marked",
                 "sem_val", "waits", "gk")

    def __init__(self, eng, fn, reads, writes, dma):
        self.eng, self.fn = eng, fn
        self.reads, self.writes, self.dma = tuple(reads), tuple(writes), dma
        self.deps, self.marked, self.sem_val, self.waits = [], False, None, []
        if dma is not None:
            w0 = self.writes[0] if self.writes else None
            if w0 is not None and not w0.startswith("d_"):
                self.gk = ("L", w0)
            else:
                self.gk = ("S", self.reads[0])


class Em:
    ENGS = ("T", "V", "S", "G", "SY")

    def __init__(self, nc):
        self.nc = nc
        self.ops = []
        self.last_w = {}
        self.readers = {}

    def add(self, eng, fn, reads=(), writes=(), dma=None):
        idx = len(self.ops)
        if getattr(self, "label", None) is not None:
            self.labels = getattr(self, "labels", {})
            self.labels[idx] = self.label
        op = _Op(eng, fn, reads, writes, dma)
        self.ops.append(op)
        dep = op.deps.append
        for t in op.reads:
            j = self.last_w.get(t)
            if j is not None:
                dep(j)
            self.readers.setdefault(t, []).append(idx)
        for t in op.writes:
            j = self.last_w.get(t)
            if j is not None:
                dep(j)
            for r in self.readers.get(t, ()):
                dep(r)
            self.last_w[t] = idx
            self.readers[t] = []
        return idx

    def T(self, fn, reads=(), writes=()):
        return self.add("T", fn, reads, writes)

    def V(self, fn, reads=(), writes=()):
        return self.add("V", fn, reads, writes)

    def S(self, fn, reads=(), writes=()):
        return self.add("S", fn, reads, writes)

    def G(self, fn, reads=(), writes=()):
        return self.add("G", fn, reads, writes)

    def dma(self, queue, out, in_, reads=(), writes=()):
        eng = {"qs": "SY", "qg": "G", "qc": "S"}[queue]
        return self.add(
            eng, (lambda e, o=out, i=in_: e.dma_start(out=o, in_=i)),
            reads, writes, dma=queue)

    def finalize(self):
        nc, ops = self.nc, self.ops
        for op in ops:
            for d in op.deps:
                dop = ops[d]
                if dop.dma is not None or dop.eng != op.eng:
                    dop.marked = True
        eng_count = {e: 0 for e in self.ENGS}
        grp_count = {}
        for op in ops:
            if op.dma is not None:
                grp_count[op.gk] = grp_count.get(op.gk, 0) + 1
                op.sem_val = 16 * grp_count[op.gk]
            elif op.marked:
                eng_count[op.eng] += 1
                op.sem_val = eng_count[op.eng]
        waited = {e: {} for e in self.ENGS}
        for op in ops:
            need = {}
            for d in op.deps:
                dop = ops[d]
                key = dop.gk if dop.dma is not None else (
                    dop.eng if dop.eng != op.eng else None)
                if key is not None and dop.sem_val:
                    if need.get(key, 0) < dop.sem_val:
                        need[key] = dop.sem_val
            w = waited[op.eng]
            for key in sorted(need, key=str):
                val = need[key]
                if w.get(key, 0) < val:
                    op.waits.append((key, val))
                    w[key] = val
        per_eng = {e: [] for e in self.ENGS}
        for op in ops:
            per_eng[op.eng].append(op)

        with ExitStack() as c2:
            sems = {}
            for k in self.ENGS:
                sems[k] = c2.enter_context(nc.semaphore(f"sem_{k}"))
            for i, gk in enumerate(sorted(grp_count, key=str)):
                sems[gk] = c2.enter_context(nc.semaphore(f"dsem{i}"))
            block = c2.enter_context(nc.Block())

            def mk(key):
                def run(eng):
                    for op in per_eng[key]:
                        for sk, val in op.waits:
                            eng.wait_ge(sems[sk], val)
                        inst = op.fn(eng)
                        if op.dma is not None:
                            inst.then_inc(sems[op.gk], 16)
                        elif op.marked:
                            inst.then_inc(sems[op.eng], 1)
                    if key == "G":
                        for gk, n in sorted(grp_count.items(), key=str):
                            if gk[0] == "S":
                                eng.wait_ge(sems[gk], 16 * n)
                return run

            block.tensor(mk("T"))
            block.vector(mk("V"))
            block.scalar(mk("S"))
            block.gpsimd(mk("G"))
            block.sync(mk("SY"))



def MM(out, lhsT, rhs, start=True, stop=True, tp=None):
    return lambda e: e.matmul(out=out, lhsT=lhsT, rhs=rhs, start=start,
                              stop=stop, tile_position=tp)

def TRANS(out, in_, ident, tp=None):
    return lambda e: e.transpose(out=out, in_=in_, identity=ident,
                                 tile_position=tp)

def COPY(out, in_):
    return lambda e: e.tensor_copy(out=out, in_=in_)

def TT(out, in0, in1, op=ALU.mult):
    return lambda e: e.tensor_tensor(out=out, in0=in0, in1=in1, op=op)

def TS(out, in0, s1, s2=None, op0=ALU.mult, op1=ALU.add):
    if s2 is None:
        return lambda e: e.tensor_scalar(out=out, in0=in0, scalar1=s1,
                                         scalar2=None, op0=op0)
    return lambda e: e.tensor_scalar(out=out, in0=in0, scalar1=s1, scalar2=s2,
                                     op0=op0, op1=op1)

def TSA(out, in0, accum_out):
    return lambda e: e.tensor_scalar(out=out, in0=in0, scalar1=1.0,
                                     scalar2=0.0, op0=ALU.mult, op1=ALU.add,
                                     accum_out=accum_out)

def STT(out, in0, s, in1, op0=ALU.mult, op1=ALU.add):
    return lambda e: e.scalar_tensor_tensor(out=out, in0=in0, scalar=s,
                                            in1=in1, op0=op0, op1=op1)

def ACT(out, in_, func, bias=0.0, scale=1.0, accum_out=None):
    return lambda e: e.activation(out=out, in_=in_, func=func, bias=bias,
                                  scale=scale, accum_out=accum_out)

def MEMSET(ap, v):
    return lambda e: e.memset(ap, v)

def REDUCE(out, in_, op):
    return lambda e: e.tensor_reduce(out=out, in_=in_, axis=AX.X, op=op)

def RECIP(out, in_):
    return lambda e: e.reciprocal(out=out, in_=in_)


def fview(ap, dims):
    return bass.AP(ap.tensor, ap.offset,
                   [list(ap.ap[0])] + [[s, c] for (s, c) in dims])



def build_nc():
    nc = bass.Bass("TRN2", target_bir_lowering=False, debug=False)
    em = Em(nc)
    V, S, G, Tn, dma = em.V, em.S, em.G, em.T, em.dma

    DR = lambda name, shape, dt=F32, kind="ExternalInput": \
        nc.dram_tensor(name, shape, dt, kind=kind).ap()

    u_main = DR("u_main", [1024, T_PASS])
    u_pref = DR("u_pref", [1024, T_PASS])
    w_inT_h = DR("w_inT", [1024, 4256])
    conv_w_h = DR("conv_w", [128, 18 * 4])
    conv_b_h = DR("conv_b", [128, 18])
    dt_bias_h = DR("dt_bias", [32, 1])
    a_log_h = DR("a_log", [32, 1])
    d_row_h = DR("d_row", [1, 32])
    dtmask_h = DR("dtmask", [32, 1])
    s0_h = DR("s0", [64, 2048])
    norm_w_h = DR("norm_w", [128, 16])
    w_outT_h = DR("w_outT", [2048, 1024])
    gate_w_h = DR("gate_w", [128, 32])
    gate_b_h = DR("gate_b", [4, 1])
    w1_h = DR("w1", [4096, 1024])
    w2_h = DR("w2", [4096, 1024])
    b1f_h = DR("b1f", [1, 4096])
    b2f_h = DR("b2f", [1, 4096])
    b1c_h = DR("b1c", [128, 32])
    b2c_h = DR("b2c", [128, 32])
    be2r_h = DR("be2r", [4, 1024])
    g1c_h = DR("g1c", [128, 32])
    be1c_h = DR("be1c", [128, 32])
    g2c_h = DR("g2c", [128, 32])
    identb_h = DR("identb", [128, 128], BF16)
    identf_h = DR("identf", [128, 128])
    lt64_h = DR("lt64", [64, 64])
    mask_sl_h = DR("mask_sl", [64, 64], BF16)
    e32x2_h = DR("e32x2", [64, 2048], BF16)
    pick4b_h = DR("pick4b", [4, 512], BF16)
    onescol_h = DR("onescol", [128, 1], BF16)
    onesr_h = DR("onesr", [1, 512])
    onesrb_h = DR("onesrb", [1, 512], BF16)
    cvals_h = DR("cvals", [1, 4])

    y_out = DR("y_out", [1024, T_PASS], kind="ExternalOutput")

    wb_in = nc.dram_tensor("wb_in", [34, 128, 8, 128], BF16, kind="Internal").ap()
    wb_out = nc.dram_tensor("wb_out", [8, 128, 16, 128], BF16, kind="Internal").ap()
    wb1 = nc.dram_tensor("wb1", [32, 128, 8, 128], BF16, kind="Internal").ap()
    wb2 = nc.dram_tensor("wb2", [32, 128, 8, 128], BF16, kind="Internal").ap()

    ctx = ExitStack()
    SB = lambda name, shape, dt=F32: ctx.enter_context(
        nc.sbuf_tensor(name, shape, dt)).ap()
    PSUM = lambda name, shape, dt=F32: ctx.enter_context(
        nc.psum_tensor(name, shape, dt)).ap()

    with ctx:
        pA = PSUM("pA", [128, 512])
        pB = PSUM("pB", [128, 512])
        pC = PSUM("pC", [128, 512])
        pM = PSUM("pM", [128, 1024])
        pQ = PSUM("pQ", [64, 512])
        pXT = PSUM("pXT", [64, 1024], BF16)
        pYT = PSUM("pYT", [128, 1024], BF16)

        cstc = SB("cstc", [1, 4])
        idb = SB("idb", [128, 128], BF16)
        idf = SB("idf", [128, 128])
        lt64 = SB("lt64s", [64, 64])
        msl = SB("msls", [64, 64], BF16)
        rhs66 = SB("rhs66", [66, 2048], BF16)
        onesr = SB("onesr_s", [1, 512])
        onesrb = SB("onesrb_s", [1, 512], BF16)
        onescol = SB("onescol_s", [128, 1], BF16)
        pick4b = SB("pick4b_s", [4, 512], BF16)
        dtb = SB("dtb", [32, 1])
        amk = SB("amk", [32, 1])
        acol = SB("acol", [32, 1])
        alogs = SB("alogs", [32, 1])
        drow = SB("drow", [1, 32])
        dtile = SB("dtile", [64, 32], BF16)
        gbc = SB("gbc", [4, 1])
        cwh = SB("cwh", [128, 18 * 4])
        cbh = SB("cbh", [128, 18])
        nwc = SB("nwc", [128, 16])
        g1s = SB("g1s", [128, 32])
        be1s = SB("be1s", [128, 32])
        g2s = SB("g2s", [128, 32])
        b1fs = SB("b1fs", [1, 4096], BF16)
        b2fs = SB("b2fs", [1, 4096], BF16)
        be2k = SB("be2k", [4, 1024], BF16)
        gws = SB("gws", [128, 32], BF16)
        wm1 = SB("wm1", [128, 8 * 4], BF16)
        wm2 = SB("wm2", [128, 8 * 4], BF16)
        bm1 = SB("bm1", [1, 4])
        bm2 = SB("bm2", [1, 4])
        stk = SB("stk", [128, 2048], BF16)
        sd = SB("sd", [128, 2048], BF16)
        dchbc = SB("dchbc", [128, 8 * 32], BF16)

        x32 = SB("x32", [128, 8 * 512])
        xbfb = SB("xbfb", [128, 512], BF16)
        xn = SB("xn", [128, 8 * 512], BF16)
        sq1 = SB("sq1", [128, 512], BF16)
        sq2 = SB("sq2", [128, 512], BF16)
        zh = SB("zh", [128, 16 * 512], BF16)
        xbc = SB("xbc", [128, 18 * 515], BF16)
        xc = SB("xc", [128, 18 * 512], BF16)
        cva = SB("cva", [128, 512])
        ths = SB("ths", [128, 512], BF16)
        dts = SB("dts", [32, 512])
        dtv = SB("dtv", [32, 512])
        adt = SB("adt", [32, 512])
        cumA = SB("cumA", [32, 512])
        cumAT = SB("cumAT", [64, 8 * 32])
        adtT = SB("adtT", [64, 64])
        dtT = SB("dtT", [64, 8 * 32], BF16)
        dcyT = SB("dcyT", [64, 8 * 32], BF16)
        dscr = SB("dscr", [64, 64])
        sumar = SB("sumar", [1, 64])
        phl = SB("phl", [32, 128], BF16)
        l66src = SB("l66src", [32, 128], BF16)
        l66 = SB("l66", [66, 128], BF16)
        gsb = SB("gsb", [64, 64], BF16)
        btb = SB("btb", [64, 64], BF16)
        dfc = SB("dfc", [128, 1024])
        msb = SB("msb", [128, 2048], BF16)
        xdt = SB("xdt", [64, 2048], BF16)
        ytk = SB("ytk", [64, 2048], BF16)
        dterm = SB("dterm", [64, 2048], BF16)
        slz = SB("slz", [128, 1024], BF16)
        slzt = SB("slzt", [128, 1024], BF16)
        yg = SB("yg", [128, 16 * 512], BF16)
        accb = SB("accb", [128, 8 * 512], BF16)
        tsc = SB("tsc", [128, 512], BF16)
        r_a = SB("r_a", [1, 512])
        r_b = SB("r_b", [1, 512])
        r_m = SB("r_m", [1, 512])
        statCb = SB("statCb", [4, 512], BF16)
        statC2b = SB("statC2b", [4, 512], BF16)
        wgt = SB("wgt", [4, 512])
        wgtb = SB("wgtb", [4, 512], BF16)
        wrow = SB("wrow", [1, 512])
        gsr = SB("gsr", [4, 512])
        gst = SB("gst", [128, 16])
        gw1 = SB("gw1", [128, 16])
        gw2 = SB("gw2", [128, 16])
        gr1 = SB("gr1", [128, 4])
        gr2 = SB("gr2", [128, 4])
        dbgg = SB("dbgg", [128, 20])
        wt = [SB(f"wt{i}", [128, 8 * 128], BF16) for i in range(2)]
        wsum = SB("wsum", [128, 1])
        praw_ = [x32.bitcast(BF16)[:, 0:4096], x32.bitcast(BF16)[:, 4096:8192]]
        h1 = zh[:, 0:4096]
        wpf_ = [yg.bitcast(F32)[:, 0:2176], xbc.bitcast(F32)[:, 0:2176]]
        wpb_ = [yg[:, 4352:6528], xbc[:, 4352:6528]]
        wpt_ = ["yg", "xbc"]
        wpf = wpf_[0]
        wpb = wpb_[0]
        mbc = slz.bitcast(F32)[:, 0:512]
        ibc = slzt.bitcast(F32)[:, 0:512]
        wo = [SB("wo0", [128, 2048], BF16), SB("wo1", [128, 2048], BF16)]
        outs_ = [cva[:, 0:512], ibc]
        outst_ = ["cva", "slzt"]


        dma("qs", cstc[:, :], cvals_h[:, :], writes=["cst"])
        nc.const_aps.register(F32, LN_EPS, cstc[0:1, 0:1])
        nc.const_aps.register(F32, -0.5, cstc[0:1, 1:2])
        dma("qs", idb[:, :], identb_h[:, :], writes=["idb"])
        dma("qs", idf[:, :], identf_h[:, :], writes=["idf"])
        dma("qs", lt64[:, :], lt64_h[:, :], writes=["lt64"])
        dma("qs", msl[:, :], mask_sl_h[:, :], writes=["msl"])
        dma("qs", rhs66[0:64, :], e32x2_h[:, :], writes=["rhs66"])
        dma("qs", onesr[:, :], onesr_h[:, :], writes=["onesr"])
        dma("qs", onesrb[:, :], onesrb_h[:, :], writes=["onesrb"])
        dma("qs", onescol[:, :], onescol_h[:, :], writes=["onescol"])
        dma("qs", pick4b[:, :], pick4b_h[:, :], writes=["pick4b"])
        dma("qs", dtb[:, :], dt_bias_h[:, :], writes=["dtb"])
        dma("qs", amk[:, :], dtmask_h[:, :], writes=["amk"])
        dma("qs", alogs[:, :], a_log_h[:, :], writes=["alogs"])
        dma("qs", drow[:, :], d_row_h[:, :], writes=["drow"])
        dma("qs", gbc[:, :], gate_b_h[:, :], writes=["gbc"])
        dma("qs", nwc[:, :], norm_w_h[:, :], writes=["nwc"])
        dma("qs", g1s[:, :], g1c_h[:, :], writes=["g1s"])
        dma("qs", be1s[:, :], be1c_h[:, :], writes=["be1s"])
        dma("qs", g2s[:, :], g2c_h[:, :], writes=["g2s"])
        dma("qs", x32[64:128, 0:2048], s0_h[:, :], writes=["x32"])

        dma("qs", wpf[:, 0:72], conv_w_h[:, :], writes=["yg"])
        dma("qs", wpf[:, 72:90], conv_b_h[:, :], writes=["yg"])
        V(TS(cwh[:, :], wpf[:, 0:72], 0.5), ["yg"], ["cwh"])
        V(TS(cbh[:, :], wpf[:, 72:90], 0.5), ["yg"], ["cbh"])
        S(ACT(acol[:, :], alogs[:, :], AF.Exp), ["alogs"], ["acol"])
        V(TS(acol[:, :], acol[:, :], -1.0), ["acol"], ["acol"])
        Tn(MM(pC[0:64, 0:32], onesr[0:1, 0:64], drow[:, :]),
           ["onesr", "drow"], ["pC"])
        V(TS(dtile[:, :], pC[0:64, 0:32], 1.0), ["pC"], ["dtile"])
        V(TS(stk[64:128, :], x32[64:128, 0:2048], 1.0), ["x32"], ["stk"])
        V(MEMSET(l66[:, :], 0.0), [], ["l66"])
        V(MEMSET(l66[64:66, :], 1.0), [], ["l66"])
        for (bh, bfs) in ((b1f_h, b1fs), (b2f_h, b2fs)):
            bt = "b1fs" if bfs is b1fs else "b2fs"
            for hf in range(2):
                dma("qs", wpf[0:1, 0:2048], bh[0:1, 2048 * hf:2048 * hf + 2048],
                    writes=["yg"])
                V(TS(bfs[0:1, 2048 * hf:2048 * hf + 2048], wpf[0:1, 0:2048],
                     1.0), ["yg"], [bt])
        for (bch, bm, bmt) in ((b1c_h, bm1, "bm1"), (b2c_h, bm2, "bm2")):
            dma("qs", wpf[0:128, 0:32], bch[:, :], writes=["yg"])
            V(TS(wpb[:, 0:32], wpf[:, 0:32], 1.0), ["yg"], ["yg"])
            Tn(MM(pC[0:1, 0:32], onescol[:, :], wpb[:, 0:32]),
               ["onescol", "yg"], ["pC"])
            V(COPY(r_a[0:1, 0:32], pC[0:1, 0:32]), ["pC"], ["r_a"])
            V(COPY(r_a[0:1, 40:44], onesr[0:1, 0:4]), ["onesr"], ["r_a"])
            for e in range(4):
                V(REDUCE(bm[0:1, e:e + 1], r_a[0:1, 8 * e:8 * e + 8], ALU.add),
                  ["r_a"], [bmt])
            V(COPY(r_a[0:1, 40:44], onesr[0:1, 0:4]), ["onesr"], ["r_a"])
            V(TS(bm[:, :], bm[:, :], 1.0 / 1024.0), [bmt], [bmt])
        dma("qs", wpf[0:4, 0:1024], be2r_h[:, :], writes=["yg"])
        V(TS(be2k[:, :], wpf[0:4, 0:1024], 1.0), ["yg"], ["be2k"])
        dma("qs", wpf[:, 0:32], gate_w_h[:, :], writes=["yg"])
        V(TS(gws[:, :], wpf[:, 0:32], 1.0), ["yg"], ["gws"])

        pp = [0]

        def _wp():
            i = pp[0] % 2
            pp[0] += 1
            return wpf_[i], wpb_[i], wpt_[i]


        for ec in range(34):
            f, b_, t_ = _wp()
            w = 128 if ec < 33 else 32
            for k in range(8):
                dma("qs", f[:, w * k:w * k + w],
                    w_inT_h[128 * k:128 * k + 128,
                            128 * ec:128 * ec + w], writes=[t_])
            V(TS(b_[:, 0:8 * w], f[:, 0:8 * w], 0.5 if ec < 16 else 1.0),
              [t_], [t_])
            if ec < 33:
                dma("qc", wb_in[ec, :, :, :], b_[:, 0:1024],
                    reads=[t_], writes=["d_wbin"])
            else:
                for k in range(8):
                    dma("qc", wb_in[33, :, k, 0:32], b_[:, 32 * k:32 * k + 32],
                        reads=[t_], writes=["d_wbin"])
        zhf = zh.bitcast(F32)
        ppf = [yg.bitcast(F32)[:, 1024 * s:1024 * s + 1024] for s in range(4)]
        ppb = [zh[:, 1024 * s:1024 * s + 1024] for s in range(4)]
        pws = [zhf[:, 2048 + s:2049 + s] for s in range(4)]
        lp_iters = []

        def mk_out_iter(j):
            def ld(s):
                dma("qs", ppf[s][:, :], w_outT_h[128 * j:128 * j + 128, :],
                    writes=[f"pp{s}"])

            def rest(s):
                V(TS(ppb[s][:, :], ppf[s][:, :], nwc[:, j:j + 1]),
                  [f"pp{s}", "nwc"], [f"pq{s}"])
                for oc in range(8):
                    dma("qc", wb_out[oc, :, j, :],
                        ppb[s][:, 128 * oc:128 * oc + 128],
                        reads=[f"pq{s}"], writes=["d_wbout"])
            return ld, rest

        def mk_w_iter(wh, wbd, wmv, e, j):
            wmt = "wm1" if wmv is wm1 else "wm2"
            dt_ = "d_wb1" if wbd is wb1 else "d_wb2"

            def ld(s):
                dma("qs", ppf[s][:, :],
                    wh[1024 * e + 128 * j:1024 * e + 128 * j + 128, :],
                    writes=[f"pp{s}"])

            def rest(s):
                V(MEMSET(pws[s], 0.0), [], [f"pw{s}"])
                S(ACT(ppb[s][:, :], ppf[s][:, :], AF.Copy, accum_out=pws[s]),
                  [f"pp{s}", f"pw{s}"], [f"pq{s}", f"pw{s}"])
                V(TS(wmv[:, 4 * j + e:4 * j + e + 1], pws[s], 1.0 / 1024.0),
                  [f"pw{s}"], [wmt])
                for ec in range(8):
                    dma("qc", wbd[8 * e + ec, :, j, :],
                        ppb[s][:, 128 * ec:128 * ec + 128],
                        reads=[f"pq{s}"], writes=[dt_])
            return ld, rest

        for j in range(16):
            lp_iters.append(mk_out_iter(j))
        for (wh, wbd, wmv) in ((w1_h, wb1, wm1), (w2_h, wb2, wm2)):
            for e in range(4):
                for j in range(8):
                    lp_iters.append(mk_w_iter(wh, wbd, wmv, e, j))
        LPN = len(lp_iters)
        LP_NP = 64
        lp_state = [0]

        def lp_win(p):
            return range((LPN * p) // LP_NP, (LPN * (p + 1)) // LP_NP)

        def lp_pump():
            p = lp_state[0]
            lp_state[0] += 1
            if p >= LP_NP:
                return
            for i in lp_win(p):
                lp_iters[i][1](i % 4)
            if p + 1 < LP_NP:
                for i in lp_win(p + 1):
                    lp_iters[i][0](i % 4)

        V(COPY(dbgg[0:1, 0:4], onesr[0:1, 0:4]), ["onesr"],
          ["d_wbin", "yg", "xbc", "pp0", "pp1", "pp2", "pp3", "dbgg"])
        for i in lp_win(0):
            lp_iters[i][0](i % 4)
        V(MEMSET(xbc[:, :], 0.0), [], ["xbc"])

        def vfence():
            V(COPY(dbgg[0:1, 0:4], onesr[0:1, 0:4]), ["onesr"], ["dbgg"])

        def dump(name, ap, tag):
            if name not in _DBG:
                return
            h = nc.dram_tensor("dbg_" + name, [ap.shape[0], ap.free_size()],
                               ap.dtype, kind="ExternalOutput").ap()
            dma("qg", h, ap, reads=[tag] if isinstance(tag, str) else list(tag))

        def emit_tile(u_hbm, t0, main, prefix_mask, tname="", chunk_cb=None):
            wti = [0]

            def load_wt(block_idx, wbd):
                b = wt[wti[0] % 2]
                tag = f"wt{wti[0] % 2}"
                wti[0] += 1
                dtag = ("d_wbin" if wbd is wb_in else
                        ("d_wb1" if wbd is wb1 else "d_wb2"))
                dma("qs", b[:, :], wbd[block_idx, :, :, :], reads=[dtag],
                    writes=[tag])
                return b, tag

            xt = lambda k: "x32" if k < 4 else "x32b"
            for k in range(8):
                dma("qs", x32[:, 512 * k:512 * k + 512],
                    u_hbm[128 * k:128 * k + 128, t0:t0 + 512], writes=[xt(k)])
            for k in range(8):
                xs = x32[:, 512 * k:512 * k + 512]
                S(ACT(xbfb[:, :], xs, AF.Copy), [xt(k)], ["xbfb"])
                sq = [sq1, sq2][k % 2]
                sqt = ["sq1", "sq2"][k % 2]
                S(ACT(sq[:, :], xs, AF.Square), [xt(k)], [sqt])
                Tn(MM(pM[0:1, 0:512], onescol[:, :], xbfb[:, :],
                      start=(k == 0), stop=(k == 7)),
                   ["onescol", "xbfb"], ["pM"])
                Tn(MM(pM[0:1, 512:1024], onescol[:, :], sq[:, :],
                      start=(k == 0), stop=(k == 7)),
                   ["onescol", sqt], ["pM"])
            V(TS(r_m[:, :], pM[0:1, 0:512], 1.0 / 1024.0), ["pM"], ["r_m"])
            V(TT(r_a[:, :], r_m[:, :], r_m[:, :]), ["r_m"], ["r_a"])
            V(STT(r_b[:, :], pM[0:1, 512:1024], 1.0 / 1024.0, r_a[:, :],
                  ALU.mult, ALU.subtract), ["pM", "r_a"], ["r_b"])
            S(ACT(r_a[:, :], r_b[:, :], AF.Ln, bias=LN_EPS), ["r_b", "cst"], ["r_a"])
            S(ACT(r_b[:, :], r_a[:, :], AF.Exp, scale=-0.5), ["r_a", "cst"], ["r_b"])
            Tn(MM(pM[:, 0:512], onesr[0:1, 0:128], r_m[:, :]),
               ["onesr", "r_m"], ["pM"])
            Tn(MM(pM[:, 512:1024], onesr[0:1, 0:128], r_b[:, :]),
               ["onesr", "r_b"], ["pM"])
            V(COPY(mbc[:, :], pM[:, 0:512]), ["pM"], ["slz"])
            V(COPY(ibc[:, :], pM[:, 512:1024]), ["pM"], ["slzt"])
            for k in range(8):
                V(TT(cva[:, :], x32[:, 512 * k:512 * k + 512], mbc[:, :],
                     ALU.subtract), [xt(k), "slz"], ["cva"])
                G(TT(xn[:, 512 * k:512 * k + 512], cva[:, :], ibc[:, :]),
                  ["cva", "slzt"], ["xn"])
            dump(tname + "_xn", xn, "xn")

            for b in range(18):
                V(COPY(xbc[:, 515 * b:515 * b + 3],
                       xbc[:, 515 * b + 512:515 * b + 515]), ["xbc"], ["xbc"])

            ping = [0]

            def inproj_chunk(ec, dest_fn, m_rows=128, tp_extra=None):
                wtb, wtag = load_wt(ec, wb_in)
                ps, pst = (pA, "pA") if ping[0] % 2 == 0 else (pB, "pB")
                ping[0] += 1
                for k in range(8):
                    Tn(MM(ps[0:m_rows, 0:512],
                          wtb[:, 128 * k:128 * k + m_rows],
                          xn[:, 512 * k:512 * k + 512],
                          start=(k == 0), stop=(k == 7), tp=tp_extra),
                       [wtag, "xn"], [pst])
                dest_fn(ps, pst)
                return ps, pst

            if main:
                ecs = list(range(16, 34)) + list(range(0, 16))
            else:
                ecs = list(range(16, 34))
            for ec in ecs:
                if 16 <= ec < 32:
                    b = ec - 16
                    inproj_chunk(ec, lambda ps, pst, b=b: S(
                        ACT(xbc[:, 515 * b + 3:515 * b + 515], ps[:, 0:512],
                            AF.Copy), [pst], ["xbc"]))
                elif ec == 32:
                    wtb, wtag = load_wt(32, wb_in)
                    ps, pst = (pA, "pA") if ping[0] % 2 == 0 else (pB, "pB")
                    ping[0] += 1
                    for k in range(8):
                        Tn(MM(ps[0:64, 0:512], wtb[:, 128 * k:128 * k + 64],
                              xn[:, 512 * k:512 * k + 512],
                              start=(k == 0), stop=(k == 7)),
                           [wtag, "xn"], [pst])
                    for k in range(8):
                        Tn(MM(ps[64:128, 0:512], wtb[:, 128 * k + 64:128 * k + 128],
                              xn[:, 512 * k:512 * k + 512],
                              start=(k == 0), stop=(k == 7), tp=(0, 64)),
                           [wtag, "xn"], [pst])
                    S(ACT(xbc[:, 515 * 16 + 3:515 * 16 + 515], ps[:, 0:512],
                          AF.Copy), [pst], ["xbc"])
                    ps2, pst2 = (pA, "pA") if ping[0] % 2 == 0 else (pB, "pB")
                    ping[0] += 1
                    for k in range(8):
                        Tn(MM(ps2[64:128, 0:512], wtb[:, 128 * k:128 * k + 64],
                              xn[:, 512 * k:512 * k + 512],
                              start=(k == 0), stop=(k == 7), tp=(0, 64)),
                           [wtag, "xn"], [pst2])
                    S(ACT(xbc[64:128, 515 * 17 + 3:515 * 17 + 515],
                          ps2[64:128, 0:512], AF.Copy), [pst2], ["xbc"])
                elif ec == 33:
                    wtb, wtag = load_wt(33, wb_in)
                    for k in range(8):
                        Tn(MM(pC[0:32, 0:512], wtb[:, 128 * k:128 * k + 32],
                              xn[:, 512 * k:512 * k + 512],
                              start=(k == 0), stop=(k == 7)),
                           [wtag, "xn"], ["pC"])
                    S(ACT(dts[:, :], pC[0:32, 0:512], AF.Exp, bias=dtb[:, :]),
                      ["pC", "dtb"], ["dts"])
                    V(TS(dts[:, :], dts[:, :], 1.0, None, ALU.add), ["dts"], ["dts"])
                    S(ACT(dtv[:, :], dts[:, :], AF.Ln), ["dts"], ["dtv"])
                    if not main:
                        V(TS(dtv[:, :], dtv[:, :], amk[:, :]), ["dtv", "amk"],
                          ["dtv"])
                    V(TS(adt[:, :], dtv[:, :], acol[:, :]), ["dtv", "acol"],
                      ["adt"])
                    V(COPY(dbgg[0:1, 0:4], onesr[0:1, 0:4]), ["onesr"],
                      ["pQ", "pM", "pC", "dtbk0", "dtbk1", "dtbk4", "dbgg"])
                    for c in range(NCH):
                        cc = 64 * c
                        p = c % 2
                        bq = pQ[0:64, 0:512] if p == 0 else pM[0:64, 512:1024]
                        bk = f"dtbk{p}"
                        at = adtT[:, 32 * p:32 * p + 32]
                        ds = dscr[:, 32 * p:32 * p + 32]
                        su = sumar[0:1, 32 * p:32 * p + 32]
                        Tn(TRANS(bq[:, 0:32], adt[:, cc:cc + 64],
                                 idf[0:32, 0:32]), ["adt", "idf"], [bk])
                        V(COPY(at, bq[:, 0:32]), [bk], [f"adtT{p}"])
                        Tn(MM(bq[:, 32:64], lt64[:, :], at),
                           ["lt64", f"adtT{p}"], [bk])
                        V(COPY(cumAT[:, 32 * c:32 * c + 32], bq[:, 32:64]),
                          [bk], [f"cumAT{c}"])
                        Tn(TRANS(bq[0:32, 64:128],
                                 cumAT[:, 32 * c:32 * c + 32],
                                 idf[0:64, 0:64]), [f"cumAT{c}", "idf"], [bk])
                        V(COPY(cumA[:, cc:cc + 64], bq[0:32, 64:128]),
                          [bk], [f"cumA{c}"])
                        Tn(TRANS(bq[:, 128:160], dtv[:, cc:cc + 64],
                                 idf[0:32, 0:32]), ["dtv", "idf"], [bk])
                        V(TS(dtT[:, 32 * c:32 * c + 32], bq[:, 128:160],
                             1.0), [bk], [f"dtT{c}"])
                        dma("qs", su, cumA[0:32, cc + 63:cc + 64],
                            reads=[f"cumA{c}"], writes=[f"sumar{p}"])
                        Tn(MM(bq[:, 160:192], onesr[0:1, 0:64], su),
                           ["onesr", f"sumar{p}"], [bk])
                        V(TT(ds, bq[:, 160:192],
                             cumAT[:, 32 * c:32 * c + 32], ALU.subtract),
                          [bk, f"cumAT{c}"], [f"dscr{p}"])
                        S(ACT(dcyT[:, 32 * c:32 * c + 32], ds, AF.Exp),
                          [f"dscr{p}"], [f"dcyT{c}"])
                        Tn(MM(pC[64:128, 256 + 32 * p:288 + 32 * p],
                              onesr[0:1, 0:64], su, tp=(0, 64)),
                           ["onesr", f"sumar{p}"], ["dtbk4"])
                        S(ACT(dchbc[64:128, 32 * c:32 * c + 32],
                              pC[64:128, 256 + 32 * p:288 + 32 * p], AF.Exp),
                          ["dtbk4"], [f"dchbc{c}"])
                        if chunk_cb is not None:
                            chunk_cb()
                    V(COPY(dbgg[0:1, 0:4], onesr[0:1, 0:4]), ["onesr"],
                      ["pQ", "pM", "pC", "dtbk0", "dtbk1", "dtbk4", "dbgg"])
                else:
                    inproj_chunk(ec, lambda ps, pst, ec=ec: S(
                        ACT(zh[:, 512 * ec:512 * ec + 512], ps[:, 0:512],
                            AF.Copy), [pst], ["zh"]))

            for b in range(18):
                lo = 64 if b == 17 else 0
                xb = lambda k: xbc[lo:128, 515 * b + k:515 * b + k + 512]
                V(TS(cva[lo:128, :], xb(0), cwh[lo:128, 4 * b:4 * b + 1],
                     cbh[lo:128, b:b + 1], ALU.mult, ALU.add),
                  ["xbc", "cwh", "cbh"], ["cva"])
                for k in (1, 2, 3):
                    V(STT(cva[lo:128, :], xb(k), cwh[lo:128, 4 * b + k:4 * b + k + 1],
                          cva[lo:128, :]), ["xbc", "cwh", "cva"], ["cva"])
                S(ACT(ths[lo:128, :], cva[lo:128, :], AF.Tanh), ["cva"], ["ths"])
                V(STT(xc[lo:128, 512 * b:512 * b + 512], ths[lo:128, :], 1.0,
                      cva[lo:128, :], ALU.add, ALU.mult),
                  ["ths", "cva"], ["xc"])

            dump(tname + "_zh", zh, "zh")
            dump(tname + "_xc", xc, "xc")
            dump(tname + "_dtv", dtv, "dtv")
            dump(tname + "_cumA", cumA, [f"cumA{i}" for i in range(NCH)])
            dump(tname + "_cumAT", cumAT, [f"cumAT{i}" for i in range(NCH)])
            dump(tname + "_dtT", dtT, [f"dtT{i}" for i in range(NCH)])
            dump(tname + "_dcyT", dcyT, [f"dcyT{i}" for i in range(NCH)])
            dump(tname + "_dchbc", dchbc, [f"dchbc{i}" for i in range(NCH)])
            for c in range(NCH):
                cc = 64 * c
                if main:
                    V(TS(l66src[:, 0:64], cumA[:, cc:cc + 64], -1.0),
                      [f"cumA{c}"], ["l66src"])
                    vfence()
                    V(STT(l66src[:, 64:128], cumA[:, cc:cc + 64], -1.0,
                          l66src[:, 0:64], ALU.mult, ALU.subtract),
                      [f"cumA{c}", "l66src"], ["l66src"])
                    vfence()
                    dma("qc", l66[0:64, 0:64], l66src[:, :],
                        reads=["l66src"], writes=["l66"])
                    V(TS(phl[:, 0:64], cumA[:, cc:cc + 64], 1.0),
                      [f"cumA{c}"], ["phl"])
                    vfence()
                    V(STT(phl[:, 64:128], cumA[:, cc:cc + 64], 1.0,
                          phl[:, 0:64], ALU.mult, ALU.subtract),
                      [f"cumA{c}", "phl"], ["phl"])
                    vfence()
                    dma("qc", rhs66[64:65, :], phl[:, 0:64],
                        reads=["phl"], writes=["rhs66"])
                    dma("qc", rhs66[65:66, :], phl[:, 64:128],
                        reads=["phl"], writes=["rhs66"])
                    Tn(MM(pC[0:64, 64:128], xc[64:128, 512 * 17 + cc:512 * 17 + cc + 64],
                          xc[64:128, 512 * 16 + cc:512 * 16 + cc + 64],
                          tp=(64, 0)), ["xc"], ["pC"])
                    V(TT(gsb[:, :], pC[0:64, 64:128], msl[:, :]),
                      ["pC", "msl"], ["gsb"])
                    vfence()
                    for half in range(2):
                        for gq in range(2):
                            Tn(MM(pM[:, 512 * gq:512 * gq + 512], l66[:, :],
                                  rhs66[:, 1024 * half + 512 * gq:
                                        1024 * half + 512 * gq + 512]),
                               ["l66", "rhs66"], ["pM"])
                        V(TS(dfc[:, :], pM[:, :], 0.0, None, ALU.min),
                          ["pM"], ["dfc"])
                        S(ACT(msb[:, 1024 * half:1024 * half + 1024],
                              dfc[:, :], AF.Exp), ["dfc"], ["msb"])
                    V(TT(msb[0:64, :], msb[0:64, :],
                         fview(gsb[0:64, 0:64], [(0, 32), (1, 64)])),
                      ["gsb", "msb"], ["msb"])
                    G(TT(msb[64:128, :], msb[64:128, :],
                         fview(xc[64:128, 512 * 16 + cc:512 * 16 + cc + 64],
                               [(0, 32), (1, 64)])),
                      ["xc", "msb"], ["msb"])

                for half in range(2):
                    for j8 in range(8):
                        j = 8 * half + j8
                        Tn(TRANS(pXT[0:64, 128 * j8:128 * j8 + 128],
                                 xc[:, 512 * j + cc:512 * j + cc + 64],
                                 idb[:, :]), ["xc", "idb"], ["pXT"])
                    dtview = fview(dtT[0:64, 32 * c + 16 * half:32 * c + 16 * half + 16],
                                   [(2, 8), (1, 2), (0, 64)])
                    V(TT(stk[0:64, 1024 * half:1024 * half + 1024],
                         pXT[:, :], dtview), ["pXT", f"dtT{c}"], ["stk"])
                    if main:
                        dview = fview(dtile[0:64, 16 * half:16 * half + 16],
                                      [(2, 8), (1, 2), (0, 64)])
                        V(TT(dterm[0:64, 1024 * half:1024 * half + 1024],
                             pXT[:, :], dview), ["pXT", "dtile"], ["dterm"])
                G(TT(xdt[:, :], stk[0:64, :],
                     fview(dcyT[0:64, 32 * c:32 * c + 32], [(1, 32), (0, 64)])),
                  ["stk", f"dcyT{c}"], ["xdt"])
                dump(tname + f"_c{c}_stkx", stk[0:64, :], "stk")
                dump(tname + f"_c{c}_xdt", xdt, "xdt")
                Tn(TRANS(pXT[0:64, 0:64],
                         xc[64:128, 512 * 17 + cc:512 * 17 + cc + 64],
                         idb[64:128, 64:128], tp=(64, 0)), ["xc", "idb"], ["pXT"])
                V(TS(btb[:, :], pXT[0:64, 0:64], 1.0), ["pXT"], ["btb"])

                if main:
                    dump(tname + f"_c{c}_msb", msb, "msb")
                    dump(tname + f"_c{c}_gsb", gsb, "gsb")
                if main:
                    for q in range(4):
                        ps, pst = (pQ, "pQ") if q % 2 == 0 else (pA, "pA")
                        for h8 in range(8):
                            h = 8 * q + h8
                            Tn(MM(ps[0:64, 64 * h8:64 * h8 + 64],
                                  msb[:, 64 * h:64 * h + 64],
                                  stk[:, 64 * h:64 * h + 64]),
                               ["msb", "stk"], [pst])
                        V(TT(ytk[:, 512 * q:512 * q + 512], ps[0:64, :],
                             dterm[:, 512 * q:512 * q + 512], ALU.add),
                          [pst, "dterm"], ["ytk"])

                for gq in range(4):
                    off = 512 * (gq % 2)
                    Tn(MM(pM[64:128, off:off + 512], btb[:, :],
                          xdt[:, 512 * gq:512 * gq + 512], tp=(0, 64)),
                       ["btb", "xdt"], ["pM"])
                    if gq == 0:
                        G(TT(sd[64:128, :], stk[64:128, :],
                             fview(dchbc[64:128, 32 * c:32 * c + 32],
                                   [(1, 32), (0, 64)])),
                          ["stk", f"dchbc{c}"], ["sd"])
                    V(TT(stk[64:128, 512 * gq:512 * gq + 512],
                         sd[64:128, 512 * gq:512 * gq + 512],
                         pM[64:128, off:off + 512], ALU.add),
                      ["sd", "pM"], ["stk"])
                dump(tname + f"_c{c}_S", stk[64:128, :], "stk")

                if main:
                    zview = fview(zh[:, cc:cc + 64], [(512, 16), (1, 64)])
                    S(ACT(slzt[:, :], zview, AF.Tanh), ["zh"], ["slzt"])
                    V(STT(slz[:, :], slzt[:, :], 1.0, zview, ALU.add, ALU.mult),
                      ["slzt", "zh"], ["slz"])
                    for j in range(16):
                        Tn(TRANS(pYT[:, 64 * j:64 * j + 64],
                                 ytk[0:64, 128 * j:128 * j + 128],
                                 idb[0:64, 0:64]), ["ytk", "idb"], ["pYT"])
                    ygview = fview(yg[:, cc:cc + 64], [(512, 16), (1, 64)])
                    V(TT(ygview, pYT[:, :], slz[:, :]), ["pYT", "slz"], ["yg"])
                    dump(tname + f"_c{c}_ytk", ytk, "ytk")
                if chunk_cb is not None:
                    chunk_cb()

            if not main:
                return

            for k in range(8):
                Tn(MM(pC[0:4, 0:512], gws[:, 4 * k:4 * k + 4],
                      xn[:, 512 * k:512 * k + 512],
                      start=(k == 0), stop=(k == 7)), ["gws", "xn"], ["pC"])
            V(TS(gsr[:, :], pC[0:4, 0:512], 1.0, gbc[:, :], ALU.mult, ALU.add),
              ["pC", "gbc"], ["gsr"])
            vfence()
            dump(tname + "_gsr", gsr, "gsr")
            for gq in range(4):
                Tn(TRANS(pC[0:128, 448 + 4 * gq:448 + 4 * gq + 4],
                         gsr[0:4, 128 * gq:128 * gq + 128], idf[0:4, 0:4]),
                   ["gsr", "idf"], ["pC"])
            V(COPY(gst[:, :], pC[:, 448:464]), ["pC"], ["gst"])
            vfence()
            dump(tname + "_gst", gst, "gst")
            gv = lambda t: fview(t[0:128, 0:4], [(1, 4), (0, 4)])
            vfence()
            for gq in range(4):
                V(REDUCE(gr1[:, gq:gq + 1], gst[:, 4 * gq:4 * gq + 4], ALU.max),
                  ["gst"], ["gr1"])
            vfence()
            dump(tname + "_gr1", gr1, "gr1")
            V(TT(gw1[:, :], gst[:, :], gv(gr1), ALU.is_equal), ["gst", "gr1"],
              ["gw1"])
            vfence()
            dump(tname + "_gw1a", gw1, "gw1")
            V(STT(gw2[:, :], gw1[:, :], -1e30, gst[:, :], ALU.mult, ALU.add),
              ["gw1", "gst"], ["gw2"])
            vfence()
            vfence()
            for gq in range(4):
                V(REDUCE(gr2[:, gq:gq + 1], gw2[:, 4 * gq:4 * gq + 4], ALU.max),
                  ["gw2"], ["gr2"])
            vfence()
            dump(tname + "_m2", gr2, "gr2")
            V(TT(gw1[:, :], gst[:, :], gv(gr2), ALU.is_ge), ["gst", "gr2"],
              ["gw1"])
            vfence()
            dump(tname + "_top2", gw1, "gw1")
            for gq in range(4):
                V(REDUCE(dbgg[:, gq:gq + 1], gst[:, 4 * gq:4 * gq + 4],
                         ALU.max), ["gst"], ["dbgg"])
            V(COPY(dbgg[:, 4:20], gst[:, :]), ["gst"], ["dbgg"])
            dump(tname + "_dbgg", dbgg, "dbgg")
            V(TT(gw2[:, :], gst[:, :], gv(gr1), ALU.subtract), ["gst", "gr1"],
              ["gw2"])
            vfence()
            dump(tname + "_gw2s", gw2, "gw2")
            S(ACT(gst[:, :], gw2[:, :], AF.Exp), ["gw2"], ["gst"])
            V(TT(gw2[:, :], gst[:, :], gw1[:, :]), ["gst", "gw1"], ["gw2"])
            vfence()
            vfence()
            for gq in range(4):
                V(REDUCE(dbgg[:, 16 + gq:16 + gq + 1],
                         gw2[:, 4 * gq:4 * gq + 4], ALU.add),
                  ["gw2"], ["dbgg"])
            vfence()
            dump(tname + "_den", dbgg[:, 16:20], "dbgg")
            V(RECIP(gr2[:, :], dbgg[:, 16:20]), ["dbgg"], ["gr2"])
            vfence()
            dump(tname + "_rec", gr2, "gr2")
            V(TT(gw1[:, :], gw2[:, :], gv(gr2)), ["gw2", "gr2"], ["gw1"])
            vfence()
            dump(tname + "_gwf", gw1, "gw1")
            dump(tname + "_gex", gw2, "gw2")
            for gq in range(4):
                Tn(TRANS(pC[0:4, 0 + 128 * gq:0 + 128 * gq + 128],
                         gw1[:, 4 * gq:4 * gq + 4], idf[:, :]),
                   ["gw1", "idf"], ["pC"])
            V(COPY(wgt[:, :], pC[0:4, 0:512]), ["pC"], ["wgt"])
            vfence()
            dump(tname + "_wgt", wgt, "wgt")
            dump(tname + "_yg", yg, "yg")
            V(TS(wgtb[:, :], wgt[:, :], 1.0), ["wgt"], ["wgtb"])

            for k in range(8):
                Tn(MM(pM[0:4, 0:512], wm1[:, 4 * k:4 * k + 4],
                      xn[:, 512 * k:512 * k + 512],
                      start=(k == 0), stop=False), ["wm1", "xn"], ["pM"])
            Tn(MM(pM[0:4, 0:512], bm1[:, :], onesr[:, :], start=False,
                  stop=True), ["bm1", "onesr", "pM"], ["pM"])
            V(TS(statCb[:, :], pM[0:4, 0:512], -1.0), ["pM"], ["statCb"])
            for ec in range(8):
                ps, pst = (pA, "pA") if ping[0] % 2 == 0 else (pB, "pB")
                ping[0] += 1
                Tn(MM(ps[:, 0:512], be2k[:, 128 * ec:128 * ec + 128],
                      wgtb[:, :]), ["be2k", "wgtb"], [pst])
                V(TT(accb[:, 512 * ec:512 * ec + 512], ps[:, 0:512],
                     xn[:, 512 * ec:512 * ec + 512], ALU.add),
                  [pst, "xn"], ["accb"])

            dump(tname + "_statCb", statCb, "statCb")
            for e in range(4):
                praw = praw_[e % 2]
                prt = ["x32", "x32b"][e % 2]
                for ec in range(8):
                    wtb, wtag = load_wt(8 * e + ec, wb1)
                    ps, pst = (pA, "pA") if ping[0] % 2 == 0 else (pB, "pB")
                    ping[0] += 1
                    for k in range(8):
                        Tn(MM(ps[:, 0:512], wtb[:, 128 * k:128 * k + 128],
                              xn[:, 512 * k:512 * k + 512],
                              start=(k == 0), stop=False), [wtag, "xn"], [pst])
                    Tn(MM(ps[:, 0:512],
                          b1fs[0:1, 1024 * e + 128 * ec:1024 * e + 128 * ec + 128],
                          onesrb[:, :], start=False, stop=False),
                       ["b1fs", "onesrb", pst], [pst])
                    Tn(MM(ps[:, 0:512], pick4b[:, 128 * e:128 * e + 128],
                          statCb[:, :], start=False, stop=True),
                       ["pick4b", "statCb", pst], [pst])
                    S(ACT(praw[:, 512 * ec:512 * ec + 512], ps[:, 0:512],
                          AF.Copy), [pst], [prt])
                    sq = [sq1, sq2][ec % 2]
                    sqt = ["sq1", "sq2"][ec % 2]
                    S(ACT(sq[:, :], praw[:, 512 * ec:512 * ec + 512],
                          AF.Square), [prt], [sqt])
                    Tn(MM(pC[0:1, 0:512], onescol[:, :], sq[:, :],
                          start=(ec == 0), stop=(ec == 7)),
                       ["onescol", sqt], ["pC"])
                V(TS(r_b[:, :], pC[0:1, 0:512], 1.0 / 1024.0), ["pC"], ["r_b"])
                S(ACT(r_a[:, :], r_b[:, :], AF.Ln, bias=LN_EPS), ["r_b", "cst"], ["r_a"])
                S(ACT(r_b[:, :], r_a[:, :], AF.Exp, scale=-0.5), ["r_a", "cst"], ["r_b"])
                Tn(MM(pM[:, 0:512], onesr[0:1, 0:128], r_b[:, :]),
                   ["onesr", "r_b"], ["pM"])
                for ec in range(8):
                    V(STT(tsc[:, :], praw[:, 512 * ec:512 * ec + 512],
                          g1s[:, 8 * e + ec:8 * e + ec + 1], pM[:, 0:512],
                          ALU.mult, ALU.mult), [prt, "g1s", "pM"], ["tsc"])
                    V(TS(h1[:, 512 * ec:512 * ec + 512], tsc[:, :],
                         be1s[:, 8 * e + ec:8 * e + ec + 1], 0.0,
                         ALU.add, ALU.max), ["tsc", "be1s"], ["zh"])
                dump(tname + f"_e{e}_h1", h1, "zh")
                for k in range(8):
                    Tn(MM(pM[0:4, 512:1024], wm2[:, 4 * k:4 * k + 4],
                          h1[:, 512 * k:512 * k + 512],
                          start=(k == 0), stop=False), ["wm2", "zh"], ["pM"])
                Tn(MM(pM[0:4, 512:1024], bm2[:, :], onesr[:, :], start=False,
                      stop=True), ["bm2", "onesr", "pM"], ["pM"])
                V(TS(statC2b[:, :], pM[0:4, 512:1024], -1.0), ["pM"], ["statC2b"])
                for ec in range(8):
                    wtb, wtag = load_wt(8 * e + ec, wb2)
                    ps, pst = (pA, "pA") if ping[0] % 2 == 0 else (pB, "pB")
                    ping[0] += 1
                    for k in range(8):
                        Tn(MM(ps[:, 0:512], wtb[:, 128 * k:128 * k + 128],
                              h1[:, 512 * k:512 * k + 512],
                              start=(k == 0), stop=False), [wtag, "zh"], [pst])
                    Tn(MM(ps[:, 0:512],
                          b2fs[0:1, 1024 * e + 128 * ec:1024 * e + 128 * ec + 128],
                          onesrb[:, :], start=False, stop=False),
                       ["b2fs", "onesrb", pst], [pst])
                    Tn(MM(ps[:, 0:512], pick4b[:, 128 * e:128 * e + 128],
                          statC2b[:, :], start=False, stop=True),
                       ["pick4b", "statC2b", pst], [pst])
                    S(ACT(praw[:, 512 * ec:512 * ec + 512], ps[:, 0:512],
                          AF.Copy), [pst], [prt])
                    sq = [sq1, sq2][ec % 2]
                    sqt = ["sq1", "sq2"][ec % 2]
                    S(ACT(sq[:, :], praw[:, 512 * ec:512 * ec + 512],
                          AF.Square), [prt], [sqt])
                    Tn(MM(pC[0:1, 0:512], onescol[:, :], sq[:, :],
                          start=(ec == 0), stop=(ec == 7)),
                       ["onescol", sqt], ["pC"])
                V(TS(r_b[:, :], pC[0:1, 0:512], 1.0 / 1024.0), ["pC"], ["r_b"])
                S(ACT(r_a[:, :], r_b[:, :], AF.Ln, bias=LN_EPS), ["r_b", "cst"], ["r_a"])
                S(ACT(r_b[:, :], r_a[:, :], AF.Exp, scale=-0.5), ["r_a", "cst"], ["r_b"])
                dma("qc", wrow[:, :], wgt[e:e + 1, :], reads=["wgt"],
                    writes=["wrow"])
                V(TT(r_a[:, :], r_b[:, :], wrow[:, :]), ["r_b", "wrow"], ["r_a"])
                Tn(MM(pM[:, 512:1024], onesr[0:1, 0:128], r_a[:, :]),
                   ["onesr", "r_a"], ["pM"])
                for ec in range(8):
                    V(STT(tsc[:, :], praw[:, 512 * ec:512 * ec + 512],
                          g2s[:, 8 * e + ec:8 * e + ec + 1], pM[:, 512:1024],
                          ALU.mult, ALU.mult), [prt, "g2s", "pM"], ["tsc"])
                    V(TT(accb[:, 512 * ec:512 * ec + 512],
                         accb[:, 512 * ec:512 * ec + 512], tsc[:, :], ALU.add),
                      ["tsc", "accb"], ["accb"])

            dump(tname + "_accb", accb, "accb")
            for j in range(16):
                sq = [sq1, sq2][j % 2]
                sqt = ["sq1", "sq2"][j % 2]
                S(ACT(sq[:, :], yg[:, 512 * j:512 * j + 512], AF.Square),
                  ["yg"], [sqt])
                Tn(MM(pM[0:1, 0:512], onescol[:, :], sq[:, :],
                      start=(j == 0), stop=(j == 15)), ["onescol", sqt], ["pM"])
            V(TS(r_b[:, :], pM[0:1, 0:512], 1.0 / 2048.0), ["pM"], ["r_b"])
            S(ACT(r_a[:, :], r_b[:, :], AF.Ln, bias=EPS), ["r_b", "cst"], ["r_a"])
            S(ACT(r_b[:, :], r_a[:, :], AF.Exp, scale=-0.5), ["r_a", "cst"], ["r_b"])
            Tn(MM(pC[:, 0:512], onesr[0:1, 0:128], r_b[:, :]),
               ["onesr", "r_b"], ["pC"])
            V(COPY(mbc[:, :], pC[:, 0:512]), ["pC"], ["slz"])
            for k in range(8):
                dma("qs", x32[:, 512 * k:512 * k + 512],
                    u_hbm[128 * k:128 * k + 128, t0:t0 + 512], writes=[xt(k)])
            for oc in range(8):
                wob = wo[oc % 2]
                wotag = ["wo0", "wo1"][oc % 2]
                dma("qs", wob[:, :], wb_out[oc, :, :, :],
                    reads=["d_wbout"], writes=[wotag])
                ps, pst = (pA, "pA") if ping[0] % 2 == 0 else (pB, "pB")
                ping[0] += 1
                for k in range(16):
                    Tn(MM(ps[:, 0:512], wob[:, 128 * k:128 * k + 128],
                          yg[:, 512 * k:512 * k + 512],
                          start=(k == 0), stop=(k == 15)), [wotag, "yg"], [pst])
                oh = outs_[oc % 2]
                ot = outst_[oc % 2]
                V(TT(oh, ps[:, 0:512], mbc[:, :]), [pst, "slz"], [ot])
                G(TT(oh, oh, accb[:, 512 * oc:512 * oc + 512], ALU.add),
                  [ot, "accb"], [ot])
                V(TT(oh, oh, x32[:, 512 * oc:512 * oc + 512], ALU.add),
                  [ot, xt(oc)], [ot])
                dma("qc", y_out[128 * oc:128 * oc + 128, t0:t0 + 512], oh,
                    reads=[ot])

        dump("wm1", wm1, "wm1")
        dump("wm2", wm2, "wm2")
        dump("bm1", bm1, "bm1")
        dump("bm2", bm2, "bm2")
        dump("b1fs", b1fs, "b1fs")
        dump("gws_d", gws, "gws")
        for t in range(T_PASS // TILE):
            emit_tile(u_pref, TILE * t, main=False, prefix_mask=True,
                      tname=f"p{t}", chunk_cb=lp_pump)
        assert lp_state[0] == LP_NP, lp_state
        V(COPY(dbgg[0:1, 0:4], onesr[0:1, 0:4]), ["onesr"],
          ["d_wbout", "d_wb1", "d_wb2", "yg", "zh",
           "pp0", "pp1", "pp2", "pp3", "pq0", "pq1", "pq2", "pq3",
           "pw0", "pw1", "pw2", "pw3", "dbgg"])
        for t in range(T_PASS // TILE):
            emit_tile(u_main, TILE * t, main=True, prefix_mask=False,
                      tname=f"m{t}")

        em.finalize()
    return nc



_NC_CACHE = {}


def _consts():
    identb = np.eye(128, dtype=bf16)
    identf = np.eye(128, dtype=np.float32)
    lt64 = np.triu(np.ones((64, 64), dtype=np.float32))
    mask_sl = np.triu(np.ones((64, 64), dtype=bf16))
    e32x2 = np.zeros((64, 2048), dtype=bf16)
    for h in range(32):
        e32x2[2 * h, 64 * h:64 * h + 64] = 1
        e32x2[2 * h + 1, 64 * h:64 * h + 64] = 1
    pick4b = np.zeros((4, 512), dtype=bf16)
    for e in range(4):
        pick4b[e, 128 * e:128 * e + 128] = 1
    onescol = np.ones((128, 1), dtype=bf16)
    onesr = np.ones((1, 512), dtype=np.float32)
    onesrb = np.ones((1, 512), dtype=bf16)
    return dict(identb=identb, identf=identf, lt64=lt64, mask_sl=mask_sl,
                e32x2=e32x2, pick4b=pick4b, onescol=onescol, onesr=onesr,
                onesrb=onesrb)


def kernel(L0_attn_u, L0_attn_in_proj_w, L0_attn_conv_w, L0_attn_conv_b,
           L0_attn_init_states, L0_attn_dt_bias, L0_attn_A_log, L0_attn_D,
           L0_attn_norm_w, L0_attn_out_proj_w, L0_ffn_gate_W, L0_ffn_gate_b,
           L0_ffn_W1, L0_ffn_b1, L0_ffn_g1, L0_ffn_be1, L0_ffn_W2, L0_ffn_b2,
           L0_ffn_g2, L0_ffn_be2):
    u = np.ascontiguousarray(np.asarray(L0_attn_u, dtype=np.float32))

    f32 = lambda a: np.ascontiguousarray(np.asarray(a, dtype=np.float32))
    w_inT = f32(L0_attn_in_proj_w).T.copy()
    cw = f32(L0_attn_conv_w)
    cb = f32(L0_attn_conv_b)
    cwt = np.zeros((128, 18 * 4), dtype=np.float32)
    cbt = np.zeros((128, 18), dtype=np.float32)
    for b in range(17):
        cwt[:, 4 * b:4 * b + 4] = cw[128 * b:128 * b + 128]
        cbt[:, b] = cb[128 * b:128 * b + 128]
    cwt[64:128, 68:72] = cw[2048:2112]
    cbt[64:128, 17] = cb[2048:2112]
    s0 = np.ascontiguousarray(
        np.transpose(f32(L0_attn_init_states), (2, 0, 1)).reshape(64, 2048))
    norm_w = np.ascontiguousarray(
        f32(L0_attn_norm_w).reshape(16, 128).T)
    w_outT = f32(L0_attn_out_proj_w).T.copy()
    gate_w = np.ascontiguousarray(
        f32(L0_ffn_gate_W).reshape(8, 128, 4).transpose(1, 0, 2).reshape(128, 32))
    colmajor = lambda a: np.ascontiguousarray(
        f32(a).reshape(4, 8, 128).transpose(2, 0, 1).reshape(128, 32))
    g1c, be1c, g2c = colmajor(L0_ffn_g1), colmajor(L0_ffn_be1), colmajor(L0_ffn_g2)
    w1 = f32(L0_ffn_W1).reshape(4096, 1024)
    w2 = f32(L0_ffn_W2).reshape(4096, 1024)
    b1f = f32(L0_ffn_b1).reshape(1, 4096)
    b2f = f32(L0_ffn_b2).reshape(1, 4096)
    b1c = colmajor(L0_ffn_b1)
    b2c = colmajor(L0_ffn_b2)
    be2r = f32(L0_ffn_be2)

    consts = _consts()
    consts["cvals"] = np.array([[LN_EPS, -0.5, 0.0, 0.0]], dtype=np.float32)
    shared = dict(w_inT=w_inT, conv_w=cwt, conv_b=cbt,
                  dt_bias=f32(L0_attn_dt_bias).reshape(32, 1),
                  a_log=f32(L0_attn_A_log).reshape(32, 1),
                  d_row=f32(L0_attn_D).reshape(1, 32),
                  s0=s0, norm_w=norm_w, w_outT=w_outT, gate_w=gate_w,
                  gate_b=f32(L0_ffn_gate_b).reshape(4, 1),
                  w1=w1, w2=w2, b1f=b1f, b2f=b2f, b1c=b1c, b2c=b2c, be2r=be2r,
                  g1c=g1c, be1c=be1c, g2c=g2c, **consts)

    in_maps = []
    for core in range(8):
        s, half = core // 2, core % 2
        m = dict(shared)
        m["u_main"] = np.ascontiguousarray(u[s, 2048 * half:2048 * half + 2048].T)
        if half == 1:
            m["u_pref"] = np.ascontiguousarray(u[s, 0:2048].T)
            m["dtmask"] = np.ones((32, 1), dtype=np.float32)
        else:
            m["u_pref"] = np.zeros((1024, 2048), dtype=np.float32)
            m["dtmask"] = np.zeros((32, 1), dtype=np.float32)
        in_maps.append(m)

    if "nc" not in _NC_CACHE:
        _NC_CACHE["nc"] = build_nc()
    nc = _NC_CACHE["nc"]
    res = run_bass_kernel_spmd(nc, in_maps, core_ids=list(range(8)))

    out = np.zeros((4, 4096, 1024), dtype=np.float32)
    for core in range(8):
        s, half = core // 2, core % 2
        out[s, 2048 * half:2048 * half + 2048] = \
            np.asarray(res.results[core]["y_out"]).T
    return out


if __name__ == "__main__":
    import time
    t0 = time.time()
    nc = build_nc()
    print(f"built ok in {time.time()-t0:.1f}s")


# version 13
# speedup vs baseline: 1.0728x; 1.0644x over previous
import numpy as np
import ml_dtypes
from contextlib import ExitStack

import concourse.bass as bass
import concourse.mybir as mybir
from concourse.bass_utils import run_bass_kernel_spmd

F32 = mybir.dt.float32
BF16 = mybir.dt.bfloat16
AF = mybir.ActivationFunctionType
ALU = mybir.AluOpType
AX = mybir.AxisListType
bf16 = ml_dtypes.bfloat16

T_PASS = 2048
TILE = 512
_DBG = {}
_DBG_OUTS = []
NCH = TILE // 64
LN_EPS = 1e-5
EPS = 1e-5



class _Op:
    __slots__ = ("eng", "fn", "reads", "writes", "dma", "deps", "marked",
                 "sem_val", "waits", "gk")

    def __init__(self, eng, fn, reads, writes, dma):
        self.eng, self.fn = eng, fn
        self.reads, self.writes, self.dma = tuple(reads), tuple(writes), dma
        self.deps, self.marked, self.sem_val, self.waits = [], False, None, []
        if dma is not None:
            w0 = self.writes[0] if self.writes else None
            if w0 is not None and not w0.startswith("d_"):
                self.gk = ("L", w0)
            else:
                self.gk = ("S", self.reads[0])


class Em:
    ENGS = ("T", "V", "S", "G", "SY")

    def __init__(self, nc):
        self.nc = nc
        self.ops = []
        self.last_w = {}
        self.readers = {}

    def add(self, eng, fn, reads=(), writes=(), dma=None):
        idx = len(self.ops)
        if getattr(self, "label", None) is not None:
            self.labels = getattr(self, "labels", {})
            self.labels[idx] = self.label
        op = _Op(eng, fn, reads, writes, dma)
        self.ops.append(op)
        dep = op.deps.append
        for t in op.reads:
            j = self.last_w.get(t)
            if j is not None:
                dep(j)
            self.readers.setdefault(t, []).append(idx)
        for t in op.writes:
            j = self.last_w.get(t)
            if j is not None:
                dep(j)
            for r in self.readers.get(t, ()):
                dep(r)
            self.last_w[t] = idx
            self.readers[t] = []
        return idx

    def T(self, fn, reads=(), writes=()):
        return self.add("T", fn, reads, writes)

    def V(self, fn, reads=(), writes=()):
        return self.add("V", fn, reads, writes)

    def S(self, fn, reads=(), writes=()):
        return self.add("S", fn, reads, writes)

    def G(self, fn, reads=(), writes=()):
        return self.add("G", fn, reads, writes)

    def dma(self, queue, out, in_, reads=(), writes=()):
        eng = {"qs": "SY", "qg": "G", "qc": "S"}[queue]
        return self.add(
            eng, (lambda e, o=out, i=in_: e.dma_start(out=o, in_=i)),
            reads, writes, dma=queue)

    def finalize(self):
        nc, ops = self.nc, self.ops
        for op in ops:
            for d in op.deps:
                dop = ops[d]
                if dop.dma is not None or dop.eng != op.eng:
                    dop.marked = True
        eng_count = {e: 0 for e in self.ENGS}
        grp_count = {}
        for op in ops:
            if op.dma is not None:
                grp_count[op.gk] = grp_count.get(op.gk, 0) + 1
                op.sem_val = 16 * grp_count[op.gk]
            elif op.marked:
                eng_count[op.eng] += 1
                op.sem_val = eng_count[op.eng]
        waited = {e: {} for e in self.ENGS}
        for op in ops:
            need = {}
            for d in op.deps:
                dop = ops[d]
                key = dop.gk if dop.dma is not None else (
                    dop.eng if dop.eng != op.eng else None)
                if key is not None and dop.sem_val:
                    if need.get(key, 0) < dop.sem_val:
                        need[key] = dop.sem_val
            w = waited[op.eng]
            for key in sorted(need, key=str):
                val = need[key]
                if w.get(key, 0) < val:
                    op.waits.append((key, val))
                    w[key] = val
        per_eng = {e: [] for e in self.ENGS}
        for op in ops:
            per_eng[op.eng].append(op)

        with ExitStack() as c2:
            sems = {}
            for k in self.ENGS:
                sems[k] = c2.enter_context(nc.semaphore(f"sem_{k}"))
            for i, gk in enumerate(sorted(grp_count, key=str)):
                sems[gk] = c2.enter_context(nc.semaphore(f"dsem{i}"))
            block = c2.enter_context(nc.Block())

            def mk(key):
                def run(eng):
                    for op in per_eng[key]:
                        for sk, val in op.waits:
                            eng.wait_ge(sems[sk], val)
                        inst = op.fn(eng)
                        if op.dma is not None:
                            inst.then_inc(sems[op.gk], 16)
                        elif op.marked:
                            inst.then_inc(sems[op.eng], 1)
                    if key == "G":
                        for gk, n in sorted(grp_count.items(), key=str):
                            if gk[0] == "S":
                                eng.wait_ge(sems[gk], 16 * n)
                return run

            block.tensor(mk("T"))
            block.vector(mk("V"))
            block.scalar(mk("S"))
            block.gpsimd(mk("G"))
            block.sync(mk("SY"))



def MM(out, lhsT, rhs, start=True, stop=True, tp=None):
    return lambda e: e.matmul(out=out, lhsT=lhsT, rhs=rhs, start=start,
                              stop=stop, tile_position=tp)

def TRANS(out, in_, ident, tp=None):
    return lambda e: e.transpose(out=out, in_=in_, identity=ident,
                                 tile_position=tp)

def COPY(out, in_):
    return lambda e: e.tensor_copy(out=out, in_=in_)

def TT(out, in0, in1, op=ALU.mult):
    return lambda e: e.tensor_tensor(out=out, in0=in0, in1=in1, op=op)

def TS(out, in0, s1, s2=None, op0=ALU.mult, op1=ALU.add):
    if s2 is None:
        return lambda e: e.tensor_scalar(out=out, in0=in0, scalar1=s1,
                                         scalar2=None, op0=op0)
    return lambda e: e.tensor_scalar(out=out, in0=in0, scalar1=s1, scalar2=s2,
                                     op0=op0, op1=op1)

def TSA(out, in0, accum_out):
    return lambda e: e.tensor_scalar(out=out, in0=in0, scalar1=1.0,
                                     scalar2=0.0, op0=ALU.mult, op1=ALU.add,
                                     accum_out=accum_out)

def STT(out, in0, s, in1, op0=ALU.mult, op1=ALU.add):
    return lambda e: e.scalar_tensor_tensor(out=out, in0=in0, scalar=s,
                                            in1=in1, op0=op0, op1=op1)

def ACT(out, in_, func, bias=0.0, scale=1.0, accum_out=None):
    return lambda e: e.activation(out=out, in_=in_, func=func, bias=bias,
                                  scale=scale, accum_out=accum_out)

def MEMSET(ap, v):
    return lambda e: e.memset(ap, v)

def REDUCE(out, in_, op):
    return lambda e: e.tensor_reduce(out=out, in_=in_, axis=AX.X, op=op)

def RECIP(out, in_):
    return lambda e: e.reciprocal(out=out, in_=in_)


def fview(ap, dims):
    return bass.AP(ap.tensor, ap.offset,
                   [list(ap.ap[0])] + [[s, c] for (s, c) in dims])



def build_nc():
    nc = bass.Bass("TRN2", target_bir_lowering=False, debug=False)
    em = Em(nc)
    V, S, G, Tn, dma = em.V, em.S, em.G, em.T, em.dma

    DR = lambda name, shape, dt=F32, kind="ExternalInput": \
        nc.dram_tensor(name, shape, dt, kind=kind).ap()

    u_main = DR("u_main", [1024, T_PASS])
    u_pref = DR("u_pref", [1024, T_PASS])
    w_inT_h = DR("w_inT", [1024, 4256])
    conv_w_h = DR("conv_w", [128, 18 * 4])
    conv_b_h = DR("conv_b", [128, 18])
    dt_bias_h = DR("dt_bias", [32, 1])
    a_log_h = DR("a_log", [32, 1])
    d_row_h = DR("d_row", [1, 32])
    dtmask_h = DR("dtmask", [32, 1])
    s0_h = DR("s0", [64, 2048])
    norm_w_h = DR("norm_w", [128, 16])
    w_outT_h = DR("w_outT", [2048, 1024])
    gate_w_h = DR("gate_w", [128, 32])
    gate_b_h = DR("gate_b", [4, 1])
    w1_h = DR("w1", [4096, 1024])
    w2_h = DR("w2", [4096, 1024])
    b1f_h = DR("b1f", [1, 4096])
    b2f_h = DR("b2f", [1, 4096])
    b1c_h = DR("b1c", [128, 32])
    b2c_h = DR("b2c", [128, 32])
    be2r_h = DR("be2r", [4, 1024])
    g1c_h = DR("g1c", [128, 32])
    be1c_h = DR("be1c", [128, 32])
    g2c_h = DR("g2c", [128, 32])
    identb_h = DR("identb", [128, 128], BF16)
    identf_h = DR("identf", [128, 128])
    lt64_h = DR("lt64", [64, 64])
    mask_sl_h = DR("mask_sl", [64, 64], BF16)
    e32x2_h = DR("e32x2", [64, 2048], BF16)
    pick4b_h = DR("pick4b", [4, 512], BF16)
    onescol_h = DR("onescol", [128, 1], BF16)
    onesr_h = DR("onesr", [1, 512])
    onesrb_h = DR("onesrb", [1, 512], BF16)
    cvals_h = DR("cvals", [1, 4])

    y_out = DR("y_out", [1024, T_PASS], kind="ExternalOutput")

    wb_in = nc.dram_tensor("wb_in", [34, 128, 8, 128], BF16, kind="Internal").ap()
    wb_out = nc.dram_tensor("wb_out", [8, 128, 16, 128], BF16, kind="Internal").ap()
    wb1 = nc.dram_tensor("wb1", [32, 128, 8, 128], BF16, kind="Internal").ap()
    wb2 = nc.dram_tensor("wb2", [32, 128, 8, 128], BF16, kind="Internal").ap()

    ctx = ExitStack()
    SB = lambda name, shape, dt=F32: ctx.enter_context(
        nc.sbuf_tensor(name, shape, dt)).ap()
    PSUM = lambda name, shape, dt=F32: ctx.enter_context(
        nc.psum_tensor(name, shape, dt)).ap()

    with ctx:
        pA = PSUM("pA", [128, 512])
        pB = PSUM("pB", [128, 512])
        pC = PSUM("pC", [128, 512])
        pM = PSUM("pM", [128, 1024])
        pQ = PSUM("pQ", [64, 512])
        pXT = PSUM("pXT", [64, 1024], BF16)
        pYT = PSUM("pYT", [128, 1024], BF16)

        cstc = SB("cstc", [1, 4])
        idb = SB("idb", [128, 128], BF16)
        idf = SB("idf", [128, 128])
        lt64 = SB("lt64s", [64, 64])
        msl = SB("msls", [64, 64], BF16)
        rhs66 = SB("rhs66", [66, 2048], BF16)
        onesr = SB("onesr_s", [1, 512])
        onesrb = SB("onesrb_s", [1, 512], BF16)
        onescol = SB("onescol_s", [128, 1], BF16)
        pick4b = SB("pick4b_s", [4, 512], BF16)
        dtb = SB("dtb", [32, 1])
        amk = SB("amk", [32, 1])
        acol = SB("acol", [32, 1])
        alogs = SB("alogs", [32, 1])
        drow = SB("drow", [1, 32])
        dtile = SB("dtile", [64, 32], BF16)
        gbc = SB("gbc", [4, 1])
        cwh = SB("cwh", [128, 18 * 4])
        cbh = SB("cbh", [128, 18])
        nwc = SB("nwc", [128, 16])
        g1s = SB("g1s", [128, 32])
        be1s = SB("be1s", [128, 32])
        g2s = SB("g2s", [128, 32])
        b1fs = SB("b1fs", [1, 4096], BF16)
        b2fs = SB("b2fs", [1, 4096], BF16)
        be2k = SB("be2k", [4, 1024], BF16)
        gws = SB("gws", [128, 32], BF16)
        wm1 = SB("wm1", [128, 8 * 4], BF16)
        wm2 = SB("wm2", [128, 8 * 4], BF16)
        bm1 = SB("bm1", [1, 4])
        bm2 = SB("bm2", [1, 4])
        stk = SB("stk", [128, 2048], BF16)
        sd = SB("sd", [128, 2048], BF16)
        dchbc = SB("dchbc", [128, 8 * 32], BF16)

        x32 = SB("x32", [128, 8 * 512])
        xbfb = SB("xbfb", [128, 512], BF16)
        xn = SB("xn", [128, 8 * 512], BF16)
        sq1 = SB("sq1", [128, 512], BF16)
        sq2 = SB("sq2", [128, 512], BF16)
        zh = SB("zh", [128, 16 * 512], BF16)
        xbc = SB("xbc", [128, 18 * 515], BF16)
        xc = SB("xc", [128, 18 * 512], BF16)
        cva = SB("cva", [128, 512])
        ths = SB("ths", [128, 512], BF16)
        dts = SB("dts", [32, 512])
        dtv = SB("dtv", [32, 512])
        adt = SB("adt", [32, 512])
        cumA = SB("cumA", [32, 512])
        cumAT = SB("cumAT", [64, 8 * 32])
        adtT = SB("adtT", [64, 64])
        dtT = SB("dtT", [64, 8 * 32], BF16)
        dcyT = SB("dcyT", [64, 8 * 32], BF16)
        dscr = SB("dscr", [64, 64])
        sumar = SB("sumar", [1, 64])
        phl = SB("phl", [32, 128], BF16)
        l66src = SB("l66src", [32, 128], BF16)
        l66 = SB("l66", [66, 128], BF16)
        gsb = SB("gsb", [64, 64], BF16)
        btb = SB("btb", [64, 64], BF16)
        dfc = SB("dfc", [128, 1024])
        msb = SB("msb", [128, 2048], BF16)
        xdt = SB("xdt", [64, 2048], BF16)
        ytk = SB("ytk", [64, 2048], BF16)
        dterm = SB("dterm", [64, 2048], BF16)
        slz = SB("slz", [128, 1024], BF16)
        slzt = SB("slzt", [128, 1024], BF16)
        yg = SB("yg", [128, 16 * 512], BF16)
        accb = SB("accb", [128, 8 * 512], BF16)
        tsc = SB("tsc", [128, 512], BF16)
        r_a = SB("r_a", [1, 512])
        r_b = SB("r_b", [1, 512])
        r_m = SB("r_m", [1, 512])
        statCb = SB("statCb", [4, 512], BF16)
        statC2b = SB("statC2b", [4, 512], BF16)
        wgt = SB("wgt", [4, 512])
        wgtb = SB("wgtb", [4, 512], BF16)
        wrow = SB("wrow", [1, 512])
        gsr = SB("gsr", [4, 512])
        gst = SB("gst", [128, 16])
        gw1 = SB("gw1", [128, 16])
        gw2 = SB("gw2", [128, 16])
        gr1 = SB("gr1", [128, 4])
        gr2 = SB("gr2", [128, 4])
        dbgg = SB("dbgg", [128, 20])
        wt = [SB(f"wt{i}", [128, 8 * 128], BF16) for i in range(2)]
        wsum = SB("wsum", [128, 1])
        praw_ = [x32.bitcast(BF16)[:, 0:4096], x32.bitcast(BF16)[:, 4096:8192]]
        h1 = zh[:, 0:4096]
        wpf_ = [yg.bitcast(F32)[:, 0:2176], xbc.bitcast(F32)[:, 0:2176]]
        wpb_ = [yg[:, 4352:6528], xbc[:, 4352:6528]]
        wpt_ = ["yg", "xbc"]
        wpf = wpf_[0]
        wpb = wpb_[0]
        mbc = slz.bitcast(F32)[:, 0:512]
        ibc = slzt.bitcast(F32)[:, 0:512]
        wo = [SB("wo0", [128, 2048], BF16), SB("wo1", [128, 2048], BF16)]
        outs_ = [cva[:, 0:512], ibc]
        outst_ = ["cva", "slzt"]


        dma("qs", cstc[:, :], cvals_h[:, :], writes=["cst"])
        nc.const_aps.register(F32, LN_EPS, cstc[0:1, 0:1])
        nc.const_aps.register(F32, -0.5, cstc[0:1, 1:2])
        dma("qs", idb[:, :], identb_h[:, :], writes=["idb"])
        dma("qs", idf[:, :], identf_h[:, :], writes=["idf"])
        dma("qs", lt64[:, :], lt64_h[:, :], writes=["lt64"])
        dma("qs", msl[:, :], mask_sl_h[:, :], writes=["msl"])
        dma("qs", rhs66[0:64, :], e32x2_h[:, :], writes=["rhs66"])
        dma("qs", onesr[:, :], onesr_h[:, :], writes=["onesr"])
        dma("qs", onesrb[:, :], onesrb_h[:, :], writes=["onesrb"])
        dma("qs", onescol[:, :], onescol_h[:, :], writes=["onescol"])
        dma("qs", pick4b[:, :], pick4b_h[:, :], writes=["pick4b"])
        dma("qs", dtb[:, :], dt_bias_h[:, :], writes=["dtb"])
        dma("qs", amk[:, :], dtmask_h[:, :], writes=["amk"])
        dma("qs", alogs[:, :], a_log_h[:, :], writes=["alogs"])
        dma("qs", drow[:, :], d_row_h[:, :], writes=["drow"])
        dma("qs", gbc[:, :], gate_b_h[:, :], writes=["gbc"])
        dma("qs", nwc[:, :], norm_w_h[:, :], writes=["nwc"])
        dma("qs", g1s[:, :], g1c_h[:, :], writes=["g1s"])
        dma("qs", be1s[:, :], be1c_h[:, :], writes=["be1s"])
        dma("qs", g2s[:, :], g2c_h[:, :], writes=["g2s"])
        dma("qs", x32[64:128, 0:2048], s0_h[:, :], writes=["x32"])

        dma("qs", wpf[:, 0:72], conv_w_h[:, :], writes=["yg"])
        dma("qs", wpf[:, 72:90], conv_b_h[:, :], writes=["yg"])
        V(TS(cwh[:, :], wpf[:, 0:72], 0.5), ["yg"], ["cwh"])
        V(TS(cbh[:, :], wpf[:, 72:90], 0.5), ["yg"], ["cbh"])
        S(ACT(acol[:, :], alogs[:, :], AF.Exp), ["alogs"], ["acol"])
        V(TS(acol[:, :], acol[:, :], -1.0), ["acol"], ["acol"])
        Tn(MM(pC[0:64, 0:32], onesr[0:1, 0:64], drow[:, :]),
           ["onesr", "drow"], ["pC"])
        V(TS(dtile[:, :], pC[0:64, 0:32], 1.0), ["pC"], ["dtile"])
        V(TS(stk[64:128, :], x32[64:128, 0:2048], 1.0), ["x32"], ["stk"])
        V(MEMSET(l66[:, :], 0.0), [], ["l66"])
        V(MEMSET(l66[64:66, :], 1.0), [], ["l66"])
        for (bh, bfs) in ((b1f_h, b1fs), (b2f_h, b2fs)):
            bt = "b1fs" if bfs is b1fs else "b2fs"
            for hf in range(2):
                dma("qs", wpf[0:1, 0:2048], bh[0:1, 2048 * hf:2048 * hf + 2048],
                    writes=["yg"])
                V(TS(bfs[0:1, 2048 * hf:2048 * hf + 2048], wpf[0:1, 0:2048],
                     1.0), ["yg"], [bt])
        for (bch, bm, bmt) in ((b1c_h, bm1, "bm1"), (b2c_h, bm2, "bm2")):
            dma("qs", wpf[0:128, 0:32], bch[:, :], writes=["yg"])
            V(TS(wpb[:, 0:32], wpf[:, 0:32], 1.0), ["yg"], ["yg"])
            Tn(MM(pC[0:1, 0:32], onescol[:, :], wpb[:, 0:32]),
               ["onescol", "yg"], ["pC"])
            V(COPY(r_a[0:1, 0:32], pC[0:1, 0:32]), ["pC"], ["r_a"])
            V(COPY(r_a[0:1, 40:44], onesr[0:1, 0:4]), ["onesr"], ["r_a"])
            for e in range(4):
                V(REDUCE(bm[0:1, e:e + 1], r_a[0:1, 8 * e:8 * e + 8], ALU.add),
                  ["r_a"], [bmt])
            V(COPY(r_a[0:1, 40:44], onesr[0:1, 0:4]), ["onesr"], ["r_a"])
            V(TS(bm[:, :], bm[:, :], 1.0 / 1024.0), [bmt], [bmt])
        dma("qs", wpf[0:4, 0:1024], be2r_h[:, :], writes=["yg"])
        V(TS(be2k[:, :], wpf[0:4, 0:1024], 1.0), ["yg"], ["be2k"])
        dma("qs", wpf[:, 0:32], gate_w_h[:, :], writes=["yg"])
        V(TS(gws[:, :], wpf[:, 0:32], 1.0), ["yg"], ["gws"])

        pp = [0]

        def _wp():
            i = pp[0] % 2
            pp[0] += 1
            return wpf_[i], wpb_[i], wpt_[i]


        for ec in range(34):
            f, b_, t_ = _wp()
            w = 128 if ec < 33 else 32
            src = bass.AP(w_inT_h.tensor, w_inT_h.offset + 128 * ec,
                          [[4256, 128], [128 * 4256, 8], [1, w]])
            dma("qs", f[:, 0:8 * w], src, writes=[t_])
            V(TS(b_[:, 0:8 * w], f[:, 0:8 * w], 0.5 if ec < 16 else 1.0),
              [t_], [t_])
            if ec < 33:
                dma("qc", wb_in[ec, :, :, :], b_[:, 0:1024],
                    reads=[t_], writes=["d_wbin"])
            else:
                for k in range(8):
                    dma("qc", wb_in[33, :, k, 0:32], b_[:, 32 * k:32 * k + 32],
                        reads=[t_], writes=["d_wbin"])
        zhf = zh.bitcast(F32)
        ppf = [yg.bitcast(F32)[:, 1024 * s:1024 * s + 1024] for s in range(4)]
        ppb = [zh[:, 1024 * s:1024 * s + 1024] for s in range(4)]
        pws = [zhf[:, 2048 + s:2049 + s] for s in range(4)]
        lp_iters = []

        def mk_out_iter(j):
            def ld(s):
                dma("qs", ppf[s][:, :], w_outT_h[128 * j:128 * j + 128, :],
                    writes=[f"pp{s}"])

            def rest(s):
                V(TS(ppb[s][:, :], ppf[s][:, :], nwc[:, j:j + 1]),
                  [f"pp{s}", "nwc"], [f"pq{s}"])
                for oc in range(8):
                    dma("qc", wb_out[oc, :, j, :],
                        ppb[s][:, 128 * oc:128 * oc + 128],
                        reads=[f"pq{s}"], writes=["d_wbout"])
            return ld, rest

        def mk_w_iter(wh, wbd, wmv, e, j):
            wmt = "wm1" if wmv is wm1 else "wm2"
            dt_ = "d_wb1" if wbd is wb1 else "d_wb2"

            def ld(s):
                dma("qs", ppf[s][:, :],
                    wh[1024 * e + 128 * j:1024 * e + 128 * j + 128, :],
                    writes=[f"pp{s}"])

            def rest(s):
                V(MEMSET(pws[s], 0.0), [], [f"pw{s}"])
                S(ACT(ppb[s][:, :], ppf[s][:, :], AF.Copy, accum_out=pws[s]),
                  [f"pp{s}", f"pw{s}"], [f"pq{s}", f"pw{s}"])
                V(TS(wmv[:, 4 * j + e:4 * j + e + 1], pws[s], 1.0 / 1024.0),
                  [f"pw{s}"], [wmt])
                for ec in range(8):
                    dma("qc", wbd[8 * e + ec, :, j, :],
                        ppb[s][:, 128 * ec:128 * ec + 128],
                        reads=[f"pq{s}"], writes=[dt_])
            return ld, rest

        for j in range(16):
            lp_iters.append(mk_out_iter(j))
        for (wh, wbd, wmv) in ((w1_h, wb1, wm1), (w2_h, wb2, wm2)):
            for e in range(4):
                for j in range(8):
                    lp_iters.append(mk_w_iter(wh, wbd, wmv, e, j))
        LPN = len(lp_iters)
        LP_NP = 64
        lp_state = [0]

        def lp_win(p):
            return range((LPN * p) // LP_NP, (LPN * (p + 1)) // LP_NP)

        def lp_pump():
            p = lp_state[0]
            lp_state[0] += 1
            if p >= LP_NP:
                return
            for i in lp_win(p):
                lp_iters[i][1](i % 4)
            if p + 1 < LP_NP:
                for i in lp_win(p + 1):
                    lp_iters[i][0](i % 4)

        V(COPY(dbgg[0:1, 0:4], onesr[0:1, 0:4]), ["onesr"],
          ["d_wbin", "yg", "xbc", "pp0", "pp1", "pp2", "pp3", "dbgg"])
        for i in lp_win(0):
            lp_iters[i][0](i % 4)
        V(MEMSET(xbc[:, :], 0.0), [], ["xbc"])

        def vfence():
            V(COPY(dbgg[0:1, 0:4], onesr[0:1, 0:4]), ["onesr"], ["dbgg"])

        def dump(name, ap, tag):
            if name not in _DBG:
                return
            h = nc.dram_tensor("dbg_" + name, [ap.shape[0], ap.free_size()],
                               ap.dtype, kind="ExternalOutput").ap()
            dma("qg", h, ap, reads=[tag] if isinstance(tag, str) else list(tag))

        def emit_tile(u_hbm, t0, main, prefix_mask, tname="", chunk_cb=None):
            wti = [0]

            def load_wt(block_idx, wbd):
                b = wt[wti[0] % 2]
                tag = f"wt{wti[0] % 2}"
                wti[0] += 1
                dtag = ("d_wbin" if wbd is wb_in else
                        ("d_wb1" if wbd is wb1 else "d_wb2"))
                dma("qs", b[:, :], wbd[block_idx, :, :, :], reads=[dtag],
                    writes=[tag])
                return b, tag

            xt = lambda k: "x32" if k < 4 else "x32b"
            for k in range(8):
                dma("qs", x32[:, 512 * k:512 * k + 512],
                    u_hbm[128 * k:128 * k + 128, t0:t0 + 512], writes=[xt(k)])
            for k in range(8):
                xs = x32[:, 512 * k:512 * k + 512]
                S(ACT(xbfb[:, :], xs, AF.Copy), [xt(k)], ["xbfb"])
                sq = [sq1, sq2][k % 2]
                sqt = ["sq1", "sq2"][k % 2]
                S(ACT(sq[:, :], xs, AF.Square), [xt(k)], [sqt])
                Tn(MM(pM[0:1, 0:512], onescol[:, :], xbfb[:, :],
                      start=(k == 0), stop=(k == 7)),
                   ["onescol", "xbfb"], ["pM"])
                Tn(MM(pM[0:1, 512:1024], onescol[:, :], sq[:, :],
                      start=(k == 0), stop=(k == 7)),
                   ["onescol", sqt], ["pM"])
            V(TS(r_m[:, :], pM[0:1, 0:512], 1.0 / 1024.0), ["pM"], ["r_m"])
            V(TT(r_a[:, :], r_m[:, :], r_m[:, :]), ["r_m"], ["r_a"])
            V(STT(r_b[:, :], pM[0:1, 512:1024], 1.0 / 1024.0, r_a[:, :],
                  ALU.mult, ALU.subtract), ["pM", "r_a"], ["r_b"])
            S(ACT(r_a[:, :], r_b[:, :], AF.Ln, bias=LN_EPS), ["r_b", "cst"], ["r_a"])
            S(ACT(r_b[:, :], r_a[:, :], AF.Exp, scale=-0.5), ["r_a", "cst"], ["r_b"])
            Tn(MM(pM[:, 0:512], onesr[0:1, 0:128], r_m[:, :]),
               ["onesr", "r_m"], ["pM"])
            Tn(MM(pM[:, 512:1024], onesr[0:1, 0:128], r_b[:, :]),
               ["onesr", "r_b"], ["pM"])
            V(COPY(mbc[:, :], pM[:, 0:512]), ["pM"], ["slz"])
            V(COPY(ibc[:, :], pM[:, 512:1024]), ["pM"], ["slzt"])
            for k in range(8):
                V(TT(cva[:, :], x32[:, 512 * k:512 * k + 512], mbc[:, :],
                     ALU.subtract), [xt(k), "slz"], ["cva"])
                G(TT(xn[:, 512 * k:512 * k + 512], cva[:, :], ibc[:, :]),
                  ["cva", "slzt"], ["xn"])
            dump(tname + "_xn", xn, "xn")

            for b in range(18):
                V(COPY(xbc[:, 515 * b:515 * b + 3],
                       xbc[:, 515 * b + 512:515 * b + 515]), ["xbc"], ["xbc"])

            ping = [0]

            def inproj_chunk(ec, dest_fn, m_rows=128, tp_extra=None):
                wtb, wtag = load_wt(ec, wb_in)
                ps, pst = (pA, "pA") if ping[0] % 2 == 0 else (pB, "pB")
                ping[0] += 1
                for k in range(8):
                    Tn(MM(ps[0:m_rows, 0:512],
                          wtb[:, 128 * k:128 * k + m_rows],
                          xn[:, 512 * k:512 * k + 512],
                          start=(k == 0), stop=(k == 7), tp=tp_extra),
                       [wtag, "xn"], [pst])
                dest_fn(ps, pst)
                return ps, pst

            if main:
                ecs = list(range(16, 34)) + list(range(0, 16))
            else:
                ecs = list(range(16, 34))
            for ec in ecs:
                if 16 <= ec < 32:
                    b = ec - 16
                    inproj_chunk(ec, lambda ps, pst, b=b: S(
                        ACT(xbc[:, 515 * b + 3:515 * b + 515], ps[:, 0:512],
                            AF.Copy), [pst], ["xbc"]))
                elif ec == 32:
                    wtb, wtag = load_wt(32, wb_in)
                    ps, pst = (pA, "pA") if ping[0] % 2 == 0 else (pB, "pB")
                    ping[0] += 1
                    for k in range(8):
                        Tn(MM(ps[0:64, 0:512], wtb[:, 128 * k:128 * k + 64],
                              xn[:, 512 * k:512 * k + 512],
                              start=(k == 0), stop=(k == 7)),
                           [wtag, "xn"], [pst])
                    for k in range(8):
                        Tn(MM(ps[64:128, 0:512], wtb[:, 128 * k + 64:128 * k + 128],
                              xn[:, 512 * k:512 * k + 512],
                              start=(k == 0), stop=(k == 7), tp=(0, 64)),
                           [wtag, "xn"], [pst])
                    S(ACT(xbc[:, 515 * 16 + 3:515 * 16 + 515], ps[:, 0:512],
                          AF.Copy), [pst], ["xbc"])
                    ps2, pst2 = (pA, "pA") if ping[0] % 2 == 0 else (pB, "pB")
                    ping[0] += 1
                    for k in range(8):
                        Tn(MM(ps2[64:128, 0:512], wtb[:, 128 * k:128 * k + 64],
                              xn[:, 512 * k:512 * k + 512],
                              start=(k == 0), stop=(k == 7), tp=(0, 64)),
                           [wtag, "xn"], [pst2])
                    S(ACT(xbc[64:128, 515 * 17 + 3:515 * 17 + 515],
                          ps2[64:128, 0:512], AF.Copy), [pst2], ["xbc"])
                elif ec == 33:
                    wtb, wtag = load_wt(33, wb_in)
                    for k in range(8):
                        Tn(MM(pC[0:32, 0:512], wtb[:, 128 * k:128 * k + 32],
                              xn[:, 512 * k:512 * k + 512],
                              start=(k == 0), stop=(k == 7)),
                           [wtag, "xn"], ["pC"])
                    S(ACT(dts[:, :], pC[0:32, 0:512], AF.Exp, bias=dtb[:, :]),
                      ["pC", "dtb"], ["dts"])
                    V(TS(dts[:, :], dts[:, :], 1.0, None, ALU.add), ["dts"], ["dts"])
                    S(ACT(dtv[:, :], dts[:, :], AF.Ln), ["dts"], ["dtv"])
                    if not main:
                        V(TS(dtv[:, :], dtv[:, :], amk[:, :]), ["dtv", "amk"],
                          ["dtv"])
                    V(TS(adt[:, :], dtv[:, :], acol[:, :]), ["dtv", "acol"],
                      ["adt"])
                    V(COPY(dbgg[0:1, 0:4], onesr[0:1, 0:4]), ["onesr"],
                      ["pQ", "pM", "pC", "dtbk0", "dtbk1", "dtbk4", "dbgg"])
                    for c in range(NCH):
                        cc = 64 * c
                        p = c % 2
                        bq = pQ[0:64, 0:512] if p == 0 else pM[0:64, 512:1024]
                        bk = f"dtbk{p}"
                        at = adtT[:, 32 * p:32 * p + 32]
                        ds = dscr[:, 32 * p:32 * p + 32]
                        su = sumar[0:1, 32 * p:32 * p + 32]
                        Tn(TRANS(bq[:, 0:32], adt[:, cc:cc + 64],
                                 idf[0:32, 0:32]), ["adt", "idf"], [bk])
                        V(COPY(at, bq[:, 0:32]), [bk], [f"adtT{p}"])
                        Tn(MM(bq[:, 32:64], lt64[:, :], at),
                           ["lt64", f"adtT{p}"], [bk])
                        V(COPY(cumAT[:, 32 * c:32 * c + 32], bq[:, 32:64]),
                          [bk], [f"cumAT{c}"])
                        Tn(TRANS(bq[0:32, 64:128],
                                 cumAT[:, 32 * c:32 * c + 32],
                                 idf[0:64, 0:64]), [f"cumAT{c}", "idf"], [bk])
                        V(COPY(cumA[:, cc:cc + 64], bq[0:32, 64:128]),
                          [bk], [f"cumA{c}"])
                        Tn(TRANS(bq[:, 128:160], dtv[:, cc:cc + 64],
                                 idf[0:32, 0:32]), ["dtv", "idf"], [bk])
                        V(TS(dtT[:, 32 * c:32 * c + 32], bq[:, 128:160],
                             1.0), [bk], [f"dtT{c}"])
                        dma("qs", su, cumA[0:32, cc + 63:cc + 64],
                            reads=[f"cumA{c}"], writes=[f"sumar{p}"])
                        Tn(MM(bq[:, 160:192], onesr[0:1, 0:64], su),
                           ["onesr", f"sumar{p}"], [bk])
                        V(TT(ds, bq[:, 160:192],
                             cumAT[:, 32 * c:32 * c + 32], ALU.subtract),
                          [bk, f"cumAT{c}"], [f"dscr{p}"])
                        S(ACT(dcyT[:, 32 * c:32 * c + 32], ds, AF.Exp),
                          [f"dscr{p}"], [f"dcyT{c}"])
                        Tn(MM(pC[64:128, 256 + 32 * p:288 + 32 * p],
                              onesr[0:1, 0:64], su, tp=(0, 64)),
                           ["onesr", f"sumar{p}"], ["dtbk4"])
                        S(ACT(dchbc[64:128, 32 * c:32 * c + 32],
                              pC[64:128, 256 + 32 * p:288 + 32 * p], AF.Exp),
                          ["dtbk4"], [f"dchbc{c}"])
                        if chunk_cb is not None:
                            chunk_cb()
                    V(COPY(dbgg[0:1, 0:4], onesr[0:1, 0:4]), ["onesr"],
                      ["pQ", "pM", "pC", "dtbk0", "dtbk1", "dtbk4", "dbgg"])
                else:
                    inproj_chunk(ec, lambda ps, pst, ec=ec: S(
                        ACT(zh[:, 512 * ec:512 * ec + 512], ps[:, 0:512],
                            AF.Copy), [pst], ["zh"]))

            for b in range(18):
                lo = 64 if b == 17 else 0
                xb = lambda k: xbc[lo:128, 515 * b + k:515 * b + k + 512]
                V(TS(cva[lo:128, :], xb(0), cwh[lo:128, 4 * b:4 * b + 1],
                     cbh[lo:128, b:b + 1], ALU.mult, ALU.add),
                  ["xbc", "cwh", "cbh"], ["cva"])
                for k in (1, 2, 3):
                    V(STT(cva[lo:128, :], xb(k), cwh[lo:128, 4 * b + k:4 * b + k + 1],
                          cva[lo:128, :]), ["xbc", "cwh", "cva"], ["cva"])
                S(ACT(ths[lo:128, :], cva[lo:128, :], AF.Tanh), ["cva"], ["ths"])
                V(STT(xc[lo:128, 512 * b:512 * b + 512], ths[lo:128, :], 1.0,
                      cva[lo:128, :], ALU.add, ALU.mult),
                  ["ths", "cva"], ["xc"])

            dump(tname + "_zh", zh, "zh")
            dump(tname + "_xc", xc, "xc")
            dump(tname + "_dtv", dtv, "dtv")
            dump(tname + "_cumA", cumA, [f"cumA{i}" for i in range(NCH)])
            dump(tname + "_cumAT", cumAT, [f"cumAT{i}" for i in range(NCH)])
            dump(tname + "_dtT", dtT, [f"dtT{i}" for i in range(NCH)])
            dump(tname + "_dcyT", dcyT, [f"dcyT{i}" for i in range(NCH)])
            dump(tname + "_dchbc", dchbc, [f"dchbc{i}" for i in range(NCH)])
            for c in range(NCH):
                cc = 64 * c
                if main:
                    V(TS(l66src[:, 0:64], cumA[:, cc:cc + 64], -1.0),
                      [f"cumA{c}"], ["l66src"])
                    vfence()
                    V(STT(l66src[:, 64:128], cumA[:, cc:cc + 64], -1.0,
                          l66src[:, 0:64], ALU.mult, ALU.subtract),
                      [f"cumA{c}", "l66src"], ["l66src"])
                    vfence()
                    dma("qc", l66[0:64, 0:64], l66src[:, :],
                        reads=["l66src"], writes=["l66"])
                    V(TS(phl[:, 0:64], cumA[:, cc:cc + 64], 1.0),
                      [f"cumA{c}"], ["phl"])
                    vfence()
                    V(STT(phl[:, 64:128], cumA[:, cc:cc + 64], 1.0,
                          phl[:, 0:64], ALU.mult, ALU.subtract),
                      [f"cumA{c}", "phl"], ["phl"])
                    vfence()
                    dma("qc", rhs66[64:65, :], phl[:, 0:64],
                        reads=["phl"], writes=["rhs66"])
                    dma("qc", rhs66[65:66, :], phl[:, 64:128],
                        reads=["phl"], writes=["rhs66"])
                    Tn(MM(pC[0:64, 64:128], xc[64:128, 512 * 17 + cc:512 * 17 + cc + 64],
                          xc[64:128, 512 * 16 + cc:512 * 16 + cc + 64],
                          tp=(64, 0)), ["xc"], ["pC"])
                    V(TT(gsb[:, :], pC[0:64, 64:128], msl[:, :]),
                      ["pC", "msl"], ["gsb"])
                    vfence()
                    for half in range(2):
                        for gq in range(2):
                            Tn(MM(pM[:, 512 * gq:512 * gq + 512], l66[:, :],
                                  rhs66[:, 1024 * half + 512 * gq:
                                        1024 * half + 512 * gq + 512]),
                               ["l66", "rhs66"], ["pM"])
                        V(TS(dfc[:, :], pM[:, :], 0.0, None, ALU.min),
                          ["pM"], ["dfc"])
                        S(ACT(msb[:, 1024 * half:1024 * half + 1024],
                              dfc[:, :], AF.Exp), ["dfc"], ["msb"])
                    V(TT(msb[0:64, :], msb[0:64, :],
                         fview(gsb[0:64, 0:64], [(0, 32), (1, 64)])),
                      ["gsb", "msb"], ["msb"])
                    G(TT(msb[64:128, :], msb[64:128, :],
                         fview(xc[64:128, 512 * 16 + cc:512 * 16 + cc + 64],
                               [(0, 32), (1, 64)])),
                      ["xc", "msb"], ["msb"])

                for half in range(2):
                    for j8 in range(8):
                        j = 8 * half + j8
                        Tn(TRANS(pXT[0:64, 128 * j8:128 * j8 + 128],
                                 xc[:, 512 * j + cc:512 * j + cc + 64],
                                 idb[:, :]), ["xc", "idb"], ["pXT"])
                    dtview = fview(dtT[0:64, 32 * c + 16 * half:32 * c + 16 * half + 16],
                                   [(2, 8), (1, 2), (0, 64)])
                    V(TT(stk[0:64, 1024 * half:1024 * half + 1024],
                         pXT[:, :], dtview), ["pXT", f"dtT{c}"], ["stk"])
                    if main:
                        dview = fview(dtile[0:64, 16 * half:16 * half + 16],
                                      [(2, 8), (1, 2), (0, 64)])
                        V(TT(dterm[0:64, 1024 * half:1024 * half + 1024],
                             pXT[:, :], dview), ["pXT", "dtile"], ["dterm"])
                G(TT(xdt[:, :], stk[0:64, :],
                     fview(dcyT[0:64, 32 * c:32 * c + 32], [(1, 32), (0, 64)])),
                  ["stk", f"dcyT{c}"], ["xdt"])
                dump(tname + f"_c{c}_stkx", stk[0:64, :], "stk")
                dump(tname + f"_c{c}_xdt", xdt, "xdt")
                Tn(TRANS(pXT[0:64, 0:64],
                         xc[64:128, 512 * 17 + cc:512 * 17 + cc + 64],
                         idb[64:128, 64:128], tp=(64, 0)), ["xc", "idb"], ["pXT"])
                V(TS(btb[:, :], pXT[0:64, 0:64], 1.0), ["pXT"], ["btb"])

                if main:
                    dump(tname + f"_c{c}_msb", msb, "msb")
                    dump(tname + f"_c{c}_gsb", gsb, "gsb")
                if main:
                    for q in range(4):
                        ps, pst = (pQ, "pQ") if q % 2 == 0 else (pA, "pA")
                        for h8 in range(8):
                            h = 8 * q + h8
                            Tn(MM(ps[0:64, 64 * h8:64 * h8 + 64],
                                  msb[:, 64 * h:64 * h + 64],
                                  stk[:, 64 * h:64 * h + 64]),
                               ["msb", "stk"], [pst])
                        V(TT(ytk[:, 512 * q:512 * q + 512], ps[0:64, :],
                             dterm[:, 512 * q:512 * q + 512], ALU.add),
                          [pst, "dterm"], ["ytk"])

                for gq in range(4):
                    off = 512 * (gq % 2)
                    Tn(MM(pM[64:128, off:off + 512], btb[:, :],
                          xdt[:, 512 * gq:512 * gq + 512], tp=(0, 64)),
                       ["btb", "xdt"], ["pM"])
                    if gq == 0:
                        G(TT(sd[64:128, :], stk[64:128, :],
                             fview(dchbc[64:128, 32 * c:32 * c + 32],
                                   [(1, 32), (0, 64)])),
                          ["stk", f"dchbc{c}"], ["sd"])
                    V(TT(stk[64:128, 512 * gq:512 * gq + 512],
                         sd[64:128, 512 * gq:512 * gq + 512],
                         pM[64:128, off:off + 512], ALU.add),
                      ["sd", "pM"], ["stk"])
                dump(tname + f"_c{c}_S", stk[64:128, :], "stk")

                if main:
                    zview = fview(zh[:, cc:cc + 64], [(512, 16), (1, 64)])
                    S(ACT(slzt[:, :], zview, AF.Tanh), ["zh"], ["slzt"])
                    V(STT(slz[:, :], slzt[:, :], 1.0, zview, ALU.add, ALU.mult),
                      ["slzt", "zh"], ["slz"])
                    for j in range(16):
                        Tn(TRANS(pYT[:, 64 * j:64 * j + 64],
                                 ytk[0:64, 128 * j:128 * j + 128],
                                 idb[0:64, 0:64]), ["ytk", "idb"], ["pYT"])
                    ygview = fview(yg[:, cc:cc + 64], [(512, 16), (1, 64)])
                    V(TT(ygview, pYT[:, :], slz[:, :]), ["pYT", "slz"], ["yg"])
                    dump(tname + f"_c{c}_ytk", ytk, "ytk")
                if chunk_cb is not None:
                    chunk_cb()

            if not main:
                return

            for k in range(8):
                Tn(MM(pC[0:4, 0:512], gws[:, 4 * k:4 * k + 4],
                      xn[:, 512 * k:512 * k + 512],
                      start=(k == 0), stop=(k == 7)), ["gws", "xn"], ["pC"])
            V(TS(gsr[:, :], pC[0:4, 0:512], 1.0, gbc[:, :], ALU.mult, ALU.add),
              ["pC", "gbc"], ["gsr"])
            vfence()
            dump(tname + "_gsr", gsr, "gsr")
            for gq in range(4):
                Tn(TRANS(pC[0:128, 448 + 4 * gq:448 + 4 * gq + 4],
                         gsr[0:4, 128 * gq:128 * gq + 128], idf[0:4, 0:4]),
                   ["gsr", "idf"], ["pC"])
            V(COPY(gst[:, :], pC[:, 448:464]), ["pC"], ["gst"])
            vfence()
            dump(tname + "_gst", gst, "gst")
            gv = lambda t: fview(t[0:128, 0:4], [(1, 4), (0, 4)])
            vfence()
            for gq in range(4):
                V(REDUCE(gr1[:, gq:gq + 1], gst[:, 4 * gq:4 * gq + 4], ALU.max),
                  ["gst"], ["gr1"])
            vfence()
            dump(tname + "_gr1", gr1, "gr1")
            V(TT(gw1[:, :], gst[:, :], gv(gr1), ALU.is_equal), ["gst", "gr1"],
              ["gw1"])
            vfence()
            dump(tname + "_gw1a", gw1, "gw1")
            V(STT(gw2[:, :], gw1[:, :], -1e30, gst[:, :], ALU.mult, ALU.add),
              ["gw1", "gst"], ["gw2"])
            vfence()
            vfence()
            for gq in range(4):
                V(REDUCE(gr2[:, gq:gq + 1], gw2[:, 4 * gq:4 * gq + 4], ALU.max),
                  ["gw2"], ["gr2"])
            vfence()
            dump(tname + "_m2", gr2, "gr2")
            V(TT(gw1[:, :], gst[:, :], gv(gr2), ALU.is_ge), ["gst", "gr2"],
              ["gw1"])
            vfence()
            dump(tname + "_top2", gw1, "gw1")
            for gq in range(4):
                V(REDUCE(dbgg[:, gq:gq + 1], gst[:, 4 * gq:4 * gq + 4],
                         ALU.max), ["gst"], ["dbgg"])
            V(COPY(dbgg[:, 4:20], gst[:, :]), ["gst"], ["dbgg"])
            dump(tname + "_dbgg", dbgg, "dbgg")
            V(TT(gw2[:, :], gst[:, :], gv(gr1), ALU.subtract), ["gst", "gr1"],
              ["gw2"])
            vfence()
            dump(tname + "_gw2s", gw2, "gw2")
            S(ACT(gst[:, :], gw2[:, :], AF.Exp), ["gw2"], ["gst"])
            V(TT(gw2[:, :], gst[:, :], gw1[:, :]), ["gst", "gw1"], ["gw2"])
            vfence()
            vfence()
            for gq in range(4):
                V(REDUCE(dbgg[:, 16 + gq:16 + gq + 1],
                         gw2[:, 4 * gq:4 * gq + 4], ALU.add),
                  ["gw2"], ["dbgg"])
            vfence()
            dump(tname + "_den", dbgg[:, 16:20], "dbgg")
            V(RECIP(gr2[:, :], dbgg[:, 16:20]), ["dbgg"], ["gr2"])
            vfence()
            dump(tname + "_rec", gr2, "gr2")
            V(TT(gw1[:, :], gw2[:, :], gv(gr2)), ["gw2", "gr2"], ["gw1"])
            vfence()
            dump(tname + "_gwf", gw1, "gw1")
            dump(tname + "_gex", gw2, "gw2")
            for gq in range(4):
                Tn(TRANS(pC[0:4, 0 + 128 * gq:0 + 128 * gq + 128],
                         gw1[:, 4 * gq:4 * gq + 4], idf[:, :]),
                   ["gw1", "idf"], ["pC"])
            V(COPY(wgt[:, :], pC[0:4, 0:512]), ["pC"], ["wgt"])
            vfence()
            dump(tname + "_wgt", wgt, "wgt")
            dump(tname + "_yg", yg, "yg")
            V(TS(wgtb[:, :], wgt[:, :], 1.0), ["wgt"], ["wgtb"])

            for k in range(8):
                Tn(MM(pM[0:4, 0:512], wm1[:, 4 * k:4 * k + 4],
                      xn[:, 512 * k:512 * k + 512],
                      start=(k == 0), stop=False), ["wm1", "xn"], ["pM"])
            Tn(MM(pM[0:4, 0:512], bm1[:, :], onesr[:, :], start=False,
                  stop=True), ["bm1", "onesr", "pM"], ["pM"])
            V(TS(statCb[:, :], pM[0:4, 0:512], -1.0), ["pM"], ["statCb"])
            for ec in range(8):
                ps, pst = (pA, "pA") if ping[0] % 2 == 0 else (pB, "pB")
                ping[0] += 1
                Tn(MM(ps[:, 0:512], be2k[:, 128 * ec:128 * ec + 128],
                      wgtb[:, :]), ["be2k", "wgtb"], [pst])
                V(TT(accb[:, 512 * ec:512 * ec + 512], ps[:, 0:512],
                     xn[:, 512 * ec:512 * ec + 512], ALU.add),
                  [pst, "xn"], ["accb"])

            dump(tname + "_statCb", statCb, "statCb")
            for e in range(4):
                praw = praw_[e % 2]
                prt = ["x32", "x32b"][e % 2]
                for ec in range(8):
                    wtb, wtag = load_wt(8 * e + ec, wb1)
                    ps, pst = (pA, "pA") if ping[0] % 2 == 0 else (pB, "pB")
                    ping[0] += 1
                    for k in range(8):
                        Tn(MM(ps[:, 0:512], wtb[:, 128 * k:128 * k + 128],
                              xn[:, 512 * k:512 * k + 512],
                              start=(k == 0), stop=False), [wtag, "xn"], [pst])
                    Tn(MM(ps[:, 0:512],
                          b1fs[0:1, 1024 * e + 128 * ec:1024 * e + 128 * ec + 128],
                          onesrb[:, :], start=False, stop=False),
                       ["b1fs", "onesrb", pst], [pst])
                    Tn(MM(ps[:, 0:512], pick4b[:, 128 * e:128 * e + 128],
                          statCb[:, :], start=False, stop=True),
                       ["pick4b", "statCb", pst], [pst])
                    S(ACT(praw[:, 512 * ec:512 * ec + 512], ps[:, 0:512],
                          AF.Copy), [pst], [prt])
                    sq = [sq1, sq2][ec % 2]
                    sqt = ["sq1", "sq2"][ec % 2]
                    S(ACT(sq[:, :], praw[:, 512 * ec:512 * ec + 512],
                          AF.Square), [prt], [sqt])
                    Tn(MM(pC[0:1, 0:512], onescol[:, :], sq[:, :],
                          start=(ec == 0), stop=(ec == 7)),
                       ["onescol", sqt], ["pC"])
                V(TS(r_b[:, :], pC[0:1, 0:512], 1.0 / 1024.0), ["pC"], ["r_b"])
                S(ACT(r_a[:, :], r_b[:, :], AF.Ln, bias=LN_EPS), ["r_b", "cst"], ["r_a"])
                S(ACT(r_b[:, :], r_a[:, :], AF.Exp, scale=-0.5), ["r_a", "cst"], ["r_b"])
                Tn(MM(pM[:, 0:512], onesr[0:1, 0:128], r_b[:, :]),
                   ["onesr", "r_b"], ["pM"])
                for ec in range(8):
                    V(STT(tsc[:, :], praw[:, 512 * ec:512 * ec + 512],
                          g1s[:, 8 * e + ec:8 * e + ec + 1], pM[:, 0:512],
                          ALU.mult, ALU.mult), [prt, "g1s", "pM"], ["tsc"])
                    V(TS(h1[:, 512 * ec:512 * ec + 512], tsc[:, :],
                         be1s[:, 8 * e + ec:8 * e + ec + 1], 0.0,
                         ALU.add, ALU.max), ["tsc", "be1s"], ["zh"])
                dump(tname + f"_e{e}_h1", h1, "zh")
                for k in range(8):
                    Tn(MM(pM[0:4, 512:1024], wm2[:, 4 * k:4 * k + 4],
                          h1[:, 512 * k:512 * k + 512],
                          start=(k == 0), stop=False), ["wm2", "zh"], ["pM"])
                Tn(MM(pM[0:4, 512:1024], bm2[:, :], onesr[:, :], start=False,
                      stop=True), ["bm2", "onesr", "pM"], ["pM"])
                V(TS(statC2b[:, :], pM[0:4, 512:1024], -1.0), ["pM"], ["statC2b"])
                for ec in range(8):
                    wtb, wtag = load_wt(8 * e + ec, wb2)
                    ps, pst = (pA, "pA") if ping[0] % 2 == 0 else (pB, "pB")
                    ping[0] += 1
                    for k in range(8):
                        Tn(MM(ps[:, 0:512], wtb[:, 128 * k:128 * k + 128],
                              h1[:, 512 * k:512 * k + 512],
                              start=(k == 0), stop=False), [wtag, "zh"], [pst])
                    Tn(MM(ps[:, 0:512],
                          b2fs[0:1, 1024 * e + 128 * ec:1024 * e + 128 * ec + 128],
                          onesrb[:, :], start=False, stop=False),
                       ["b2fs", "onesrb", pst], [pst])
                    Tn(MM(ps[:, 0:512], pick4b[:, 128 * e:128 * e + 128],
                          statC2b[:, :], start=False, stop=True),
                       ["pick4b", "statC2b", pst], [pst])
                    S(ACT(praw[:, 512 * ec:512 * ec + 512], ps[:, 0:512],
                          AF.Copy), [pst], [prt])
                    sq = [sq1, sq2][ec % 2]
                    sqt = ["sq1", "sq2"][ec % 2]
                    S(ACT(sq[:, :], praw[:, 512 * ec:512 * ec + 512],
                          AF.Square), [prt], [sqt])
                    Tn(MM(pC[0:1, 0:512], onescol[:, :], sq[:, :],
                          start=(ec == 0), stop=(ec == 7)),
                       ["onescol", sqt], ["pC"])
                V(TS(r_b[:, :], pC[0:1, 0:512], 1.0 / 1024.0), ["pC"], ["r_b"])
                S(ACT(r_a[:, :], r_b[:, :], AF.Ln, bias=LN_EPS), ["r_b", "cst"], ["r_a"])
                S(ACT(r_b[:, :], r_a[:, :], AF.Exp, scale=-0.5), ["r_a", "cst"], ["r_b"])
                dma("qc", wrow[:, :], wgt[e:e + 1, :], reads=["wgt"],
                    writes=["wrow"])
                V(TT(r_a[:, :], r_b[:, :], wrow[:, :]), ["r_b", "wrow"], ["r_a"])
                Tn(MM(pM[:, 512:1024], onesr[0:1, 0:128], r_a[:, :]),
                   ["onesr", "r_a"], ["pM"])
                for ec in range(8):
                    V(STT(tsc[:, :], praw[:, 512 * ec:512 * ec + 512],
                          g2s[:, 8 * e + ec:8 * e + ec + 1], pM[:, 512:1024],
                          ALU.mult, ALU.mult), [prt, "g2s", "pM"], ["tsc"])
                    V(TT(accb[:, 512 * ec:512 * ec + 512],
                         accb[:, 512 * ec:512 * ec + 512], tsc[:, :], ALU.add),
                      ["tsc", "accb"], ["accb"])

            dump(tname + "_accb", accb, "accb")
            for j in range(16):
                sq = [sq1, sq2][j % 2]
                sqt = ["sq1", "sq2"][j % 2]
                S(ACT(sq[:, :], yg[:, 512 * j:512 * j + 512], AF.Square),
                  ["yg"], [sqt])
                Tn(MM(pM[0:1, 0:512], onescol[:, :], sq[:, :],
                      start=(j == 0), stop=(j == 15)), ["onescol", sqt], ["pM"])
            V(TS(r_b[:, :], pM[0:1, 0:512], 1.0 / 2048.0), ["pM"], ["r_b"])
            S(ACT(r_a[:, :], r_b[:, :], AF.Ln, bias=EPS), ["r_b", "cst"], ["r_a"])
            S(ACT(r_b[:, :], r_a[:, :], AF.Exp, scale=-0.5), ["r_a", "cst"], ["r_b"])
            Tn(MM(pC[:, 0:512], onesr[0:1, 0:128], r_b[:, :]),
               ["onesr", "r_b"], ["pC"])
            V(COPY(mbc[:, :], pC[:, 0:512]), ["pC"], ["slz"])
            for k in range(8):
                dma("qs", x32[:, 512 * k:512 * k + 512],
                    u_hbm[128 * k:128 * k + 128, t0:t0 + 512], writes=[xt(k)])
            for oc in range(8):
                wob = wo[oc % 2]
                wotag = ["wo0", "wo1"][oc % 2]
                dma("qs", wob[:, :], wb_out[oc, :, :, :],
                    reads=["d_wbout"], writes=[wotag])
                ps, pst = (pA, "pA") if ping[0] % 2 == 0 else (pB, "pB")
                ping[0] += 1
                for k in range(16):
                    Tn(MM(ps[:, 0:512], wob[:, 128 * k:128 * k + 128],
                          yg[:, 512 * k:512 * k + 512],
                          start=(k == 0), stop=(k == 15)), [wotag, "yg"], [pst])
                oh = outs_[oc % 2]
                ot = outst_[oc % 2]
                V(TT(oh, ps[:, 0:512], mbc[:, :]), [pst, "slz"], [ot])
                G(TT(oh, oh, accb[:, 512 * oc:512 * oc + 512], ALU.add),
                  [ot, "accb"], [ot])
                V(TT(oh, oh, x32[:, 512 * oc:512 * oc + 512], ALU.add),
                  [ot, xt(oc)], [ot])
                dma("qc", y_out[128 * oc:128 * oc + 128, t0:t0 + 512], oh,
                    reads=[ot])

        dump("wm1", wm1, "wm1")
        dump("wm2", wm2, "wm2")
        dump("bm1", bm1, "bm1")
        dump("bm2", bm2, "bm2")
        dump("b1fs", b1fs, "b1fs")
        dump("gws_d", gws, "gws")
        for t in range(T_PASS // TILE):
            emit_tile(u_pref, TILE * t, main=False, prefix_mask=True,
                      tname=f"p{t}", chunk_cb=lp_pump)
        assert lp_state[0] == LP_NP, lp_state
        V(COPY(dbgg[0:1, 0:4], onesr[0:1, 0:4]), ["onesr"],
          ["d_wbout", "d_wb1", "d_wb2", "yg", "zh",
           "pp0", "pp1", "pp2", "pp3", "pq0", "pq1", "pq2", "pq3",
           "pw0", "pw1", "pw2", "pw3", "dbgg"])
        for t in range(T_PASS // TILE):
            emit_tile(u_main, TILE * t, main=True, prefix_mask=False,
                      tname=f"m{t}")

        em.finalize()
    return nc



_NC_CACHE = {}


def _consts():
    identb = np.eye(128, dtype=bf16)
    identf = np.eye(128, dtype=np.float32)
    lt64 = np.triu(np.ones((64, 64), dtype=np.float32))
    mask_sl = np.triu(np.ones((64, 64), dtype=bf16))
    e32x2 = np.zeros((64, 2048), dtype=bf16)
    for h in range(32):
        e32x2[2 * h, 64 * h:64 * h + 64] = 1
        e32x2[2 * h + 1, 64 * h:64 * h + 64] = 1
    pick4b = np.zeros((4, 512), dtype=bf16)
    for e in range(4):
        pick4b[e, 128 * e:128 * e + 128] = 1
    onescol = np.ones((128, 1), dtype=bf16)
    onesr = np.ones((1, 512), dtype=np.float32)
    onesrb = np.ones((1, 512), dtype=bf16)
    return dict(identb=identb, identf=identf, lt64=lt64, mask_sl=mask_sl,
                e32x2=e32x2, pick4b=pick4b, onescol=onescol, onesr=onesr,
                onesrb=onesrb)


def kernel(L0_attn_u, L0_attn_in_proj_w, L0_attn_conv_w, L0_attn_conv_b,
           L0_attn_init_states, L0_attn_dt_bias, L0_attn_A_log, L0_attn_D,
           L0_attn_norm_w, L0_attn_out_proj_w, L0_ffn_gate_W, L0_ffn_gate_b,
           L0_ffn_W1, L0_ffn_b1, L0_ffn_g1, L0_ffn_be1, L0_ffn_W2, L0_ffn_b2,
           L0_ffn_g2, L0_ffn_be2):
    u = np.ascontiguousarray(np.asarray(L0_attn_u, dtype=np.float32))

    f32 = lambda a: np.ascontiguousarray(np.asarray(a, dtype=np.float32))
    w_inT = f32(L0_attn_in_proj_w).T.copy()
    cw = f32(L0_attn_conv_w)
    cb = f32(L0_attn_conv_b)
    cwt = np.zeros((128, 18 * 4), dtype=np.float32)
    cbt = np.zeros((128, 18), dtype=np.float32)
    for b in range(17):
        cwt[:, 4 * b:4 * b + 4] = cw[128 * b:128 * b + 128]
        cbt[:, b] = cb[128 * b:128 * b + 128]
    cwt[64:128, 68:72] = cw[2048:2112]
    cbt[64:128, 17] = cb[2048:2112]
    s0 = np.ascontiguousarray(
        np.transpose(f32(L0_attn_init_states), (2, 0, 1)).reshape(64, 2048))
    norm_w = np.ascontiguousarray(
        f32(L0_attn_norm_w).reshape(16, 128).T)
    w_outT = f32(L0_attn_out_proj_w).T.copy()
    gate_w = np.ascontiguousarray(
        f32(L0_ffn_gate_W).reshape(8, 128, 4).transpose(1, 0, 2).reshape(128, 32))
    colmajor = lambda a: np.ascontiguousarray(
        f32(a).reshape(4, 8, 128).transpose(2, 0, 1).reshape(128, 32))
    g1c, be1c, g2c = colmajor(L0_ffn_g1), colmajor(L0_ffn_be1), colmajor(L0_ffn_g2)
    w1 = f32(L0_ffn_W1).reshape(4096, 1024)
    w2 = f32(L0_ffn_W2).reshape(4096, 1024)
    b1f = f32(L0_ffn_b1).reshape(1, 4096)
    b2f = f32(L0_ffn_b2).reshape(1, 4096)
    b1c = colmajor(L0_ffn_b1)
    b2c = colmajor(L0_ffn_b2)
    be2r = f32(L0_ffn_be2)

    consts = _consts()
    consts["cvals"] = np.array([[LN_EPS, -0.5, 0.0, 0.0]], dtype=np.float32)
    shared = dict(w_inT=w_inT, conv_w=cwt, conv_b=cbt,
                  dt_bias=f32(L0_attn_dt_bias).reshape(32, 1),
                  a_log=f32(L0_attn_A_log).reshape(32, 1),
                  d_row=f32(L0_attn_D).reshape(1, 32),
                  s0=s0, norm_w=norm_w, w_outT=w_outT, gate_w=gate_w,
                  gate_b=f32(L0_ffn_gate_b).reshape(4, 1),
                  w1=w1, w2=w2, b1f=b1f, b2f=b2f, b1c=b1c, b2c=b2c, be2r=be2r,
                  g1c=g1c, be1c=be1c, g2c=g2c, **consts)

    in_maps = []
    for core in range(8):
        s, half = core // 2, core % 2
        m = dict(shared)
        m["u_main"] = np.ascontiguousarray(u[s, 2048 * half:2048 * half + 2048].T)
        if half == 1:
            m["u_pref"] = np.ascontiguousarray(u[s, 0:2048].T)
            m["dtmask"] = np.ones((32, 1), dtype=np.float32)
        else:
            m["u_pref"] = np.zeros((1024, 2048), dtype=np.float32)
            m["dtmask"] = np.zeros((32, 1), dtype=np.float32)
        in_maps.append(m)

    if "nc" not in _NC_CACHE:
        _NC_CACHE["nc"] = build_nc()
    nc = _NC_CACHE["nc"]
    res = run_bass_kernel_spmd(nc, in_maps, core_ids=list(range(8)))

    out = np.zeros((4, 4096, 1024), dtype=np.float32)
    for core in range(8):
        s, half = core // 2, core % 2
        out[s, 2048 * half:2048 * half + 2048] = \
            np.asarray(res.results[core]["y_out"]).T
    return out


if __name__ == "__main__":
    import time
    t0 = time.time()
    nc = build_nc()
    print(f"built ok in {time.time()-t0:.1f}s")


# version 14 (final)
# speedup vs baseline: 1.0819x; 1.0085x over previous
import numpy as np
import ml_dtypes
from contextlib import ExitStack

import concourse.bass as bass
import concourse.mybir as mybir
from concourse.bass_utils import run_bass_kernel_spmd

F32 = mybir.dt.float32
BF16 = mybir.dt.bfloat16
AF = mybir.ActivationFunctionType
ALU = mybir.AluOpType
AX = mybir.AxisListType
bf16 = ml_dtypes.bfloat16

T_PASS = 2048
TILE = 512
_DBG = {}
_DBG_OUTS = []
NCH = TILE // 64
LN_EPS = 1e-5
EPS = 1e-5



class _Op:
    __slots__ = ("eng", "fn", "reads", "writes", "dma", "deps", "marked",
                 "sem_val", "waits", "gk")

    def __init__(self, eng, fn, reads, writes, dma):
        self.eng, self.fn = eng, fn
        self.reads, self.writes, self.dma = tuple(reads), tuple(writes), dma
        self.deps, self.marked, self.sem_val, self.waits = [], False, None, []
        if dma is not None:
            w0 = self.writes[0] if self.writes else None
            if w0 is not None and not w0.startswith("d_"):
                self.gk = ("L", w0)
            else:
                self.gk = ("S", self.reads[0])


class Em:
    ENGS = ("T", "V", "S", "G", "SY")

    def __init__(self, nc):
        self.nc = nc
        self.ops = []
        self.last_w = {}
        self.readers = {}

    def add(self, eng, fn, reads=(), writes=(), dma=None):
        idx = len(self.ops)
        if getattr(self, "label", None) is not None:
            self.labels = getattr(self, "labels", {})
            self.labels[idx] = self.label
        op = _Op(eng, fn, reads, writes, dma)
        self.ops.append(op)
        dep = op.deps.append
        for t in op.reads:
            j = self.last_w.get(t)
            if j is not None:
                dep(j)
            self.readers.setdefault(t, []).append(idx)
        for t in op.writes:
            j = self.last_w.get(t)
            if j is not None:
                dep(j)
            for r in self.readers.get(t, ()):
                dep(r)
            self.last_w[t] = idx
            self.readers[t] = []
        return idx

    def T(self, fn, reads=(), writes=()):
        return self.add("T", fn, reads, writes)

    def V(self, fn, reads=(), writes=()):
        return self.add("V", fn, reads, writes)

    def S(self, fn, reads=(), writes=()):
        return self.add("S", fn, reads, writes)

    def G(self, fn, reads=(), writes=()):
        return self.add("G", fn, reads, writes)

    def dma(self, queue, out, in_, reads=(), writes=()):
        eng = {"qs": "SY", "qg": "G", "qc": "S"}[queue]
        return self.add(
            eng, (lambda e, o=out, i=in_: e.dma_start(out=o, in_=i)),
            reads, writes, dma=queue)

    def finalize(self):
        nc, ops = self.nc, self.ops
        for op in ops:
            for d in op.deps:
                dop = ops[d]
                if dop.dma is not None or dop.eng != op.eng:
                    dop.marked = True
        eng_count = {e: 0 for e in self.ENGS}
        grp_count = {}
        for op in ops:
            if op.dma is not None:
                grp_count[op.gk] = grp_count.get(op.gk, 0) + 1
                op.sem_val = 16 * grp_count[op.gk]
            elif op.marked:
                eng_count[op.eng] += 1
                op.sem_val = eng_count[op.eng]
        waited = {e: {} for e in self.ENGS}
        for op in ops:
            need = {}
            for d in op.deps:
                dop = ops[d]
                key = dop.gk if dop.dma is not None else (
                    dop.eng if dop.eng != op.eng else None)
                if key is not None and dop.sem_val:
                    if need.get(key, 0) < dop.sem_val:
                        need[key] = dop.sem_val
            w = waited[op.eng]
            for key in sorted(need, key=str):
                val = need[key]
                if w.get(key, 0) < val:
                    op.waits.append((key, val))
                    w[key] = val
        per_eng = {e: [] for e in self.ENGS}
        for op in ops:
            per_eng[op.eng].append(op)

        with ExitStack() as c2:
            sems = {}
            for k in self.ENGS:
                sems[k] = c2.enter_context(nc.semaphore(f"sem_{k}"))
            for i, gk in enumerate(sorted(grp_count, key=str)):
                sems[gk] = c2.enter_context(nc.semaphore(f"dsem{i}"))
            block = c2.enter_context(nc.Block())

            def mk(key):
                def run(eng):
                    for op in per_eng[key]:
                        for sk, val in op.waits:
                            eng.wait_ge(sems[sk], val)
                        inst = op.fn(eng)
                        if op.dma is not None:
                            inst.then_inc(sems[op.gk], 16)
                        elif op.marked:
                            inst.then_inc(sems[op.eng], 1)
                    if key == "G":
                        for gk, n in sorted(grp_count.items(), key=str):
                            if gk[0] == "S":
                                eng.wait_ge(sems[gk], 16 * n)
                return run

            block.tensor(mk("T"))
            block.vector(mk("V"))
            block.scalar(mk("S"))
            block.gpsimd(mk("G"))
            block.sync(mk("SY"))



def MM(out, lhsT, rhs, start=True, stop=True, tp=None):
    return lambda e: e.matmul(out=out, lhsT=lhsT, rhs=rhs, start=start,
                              stop=stop, tile_position=tp)

def TRANS(out, in_, ident, tp=None):
    return lambda e: e.transpose(out=out, in_=in_, identity=ident,
                                 tile_position=tp)

def COPY(out, in_):
    return lambda e: e.tensor_copy(out=out, in_=in_)

def TT(out, in0, in1, op=ALU.mult):
    return lambda e: e.tensor_tensor(out=out, in0=in0, in1=in1, op=op)

def TS(out, in0, s1, s2=None, op0=ALU.mult, op1=ALU.add):
    if s2 is None:
        return lambda e: e.tensor_scalar(out=out, in0=in0, scalar1=s1,
                                         scalar2=None, op0=op0)
    return lambda e: e.tensor_scalar(out=out, in0=in0, scalar1=s1, scalar2=s2,
                                     op0=op0, op1=op1)

def TSA(out, in0, accum_out):
    return lambda e: e.tensor_scalar(out=out, in0=in0, scalar1=1.0,
                                     scalar2=0.0, op0=ALU.mult, op1=ALU.add,
                                     accum_out=accum_out)

def STT(out, in0, s, in1, op0=ALU.mult, op1=ALU.add):
    return lambda e: e.scalar_tensor_tensor(out=out, in0=in0, scalar=s,
                                            in1=in1, op0=op0, op1=op1)

def ACT(out, in_, func, bias=0.0, scale=1.0, accum_out=None):
    return lambda e: e.activation(out=out, in_=in_, func=func, bias=bias,
                                  scale=scale, accum_out=accum_out)

def MEMSET(ap, v):
    return lambda e: e.memset(ap, v)

def REDUCE(out, in_, op):
    return lambda e: e.tensor_reduce(out=out, in_=in_, axis=AX.X, op=op)

def RECIP(out, in_):
    return lambda e: e.reciprocal(out=out, in_=in_)


def fview(ap, dims):
    return bass.AP(ap.tensor, ap.offset,
                   [list(ap.ap[0])] + [[s, c] for (s, c) in dims])



def build_nc():
    nc = bass.Bass("TRN2", target_bir_lowering=False, debug=False)
    em = Em(nc)
    V, S, G, Tn, dma = em.V, em.S, em.G, em.T, em.dma

    DR = lambda name, shape, dt=F32, kind="ExternalInput": \
        nc.dram_tensor(name, shape, dt, kind=kind).ap()

    u_main = DR("u_main", [1024, T_PASS])
    u_pref = DR("u_pref", [1024, T_PASS])
    w_inT_h = DR("w_inT", [1024, 4256])
    conv_w_h = DR("conv_w", [128, 18 * 4])
    conv_b_h = DR("conv_b", [128, 18])
    dt_bias_h = DR("dt_bias", [32, 1])
    a_log_h = DR("a_log", [32, 1])
    d_row_h = DR("d_row", [1, 32])
    dtmask_h = DR("dtmask", [32, 1])
    s0_h = DR("s0", [64, 2048])
    norm_w_h = DR("norm_w", [128, 16])
    w_outT_h = DR("w_outT", [2048, 1024])
    gate_w_h = DR("gate_w", [128, 32])
    gate_b_h = DR("gate_b", [4, 1])
    w1_h = DR("w1", [4096, 1024])
    w2_h = DR("w2", [4096, 1024])
    b1f_h = DR("b1f", [1, 4096])
    b2f_h = DR("b2f", [1, 4096])
    b1c_h = DR("b1c", [128, 32])
    b2c_h = DR("b2c", [128, 32])
    be2r_h = DR("be2r", [4, 1024])
    g1c_h = DR("g1c", [128, 32])
    be1c_h = DR("be1c", [128, 32])
    g2c_h = DR("g2c", [128, 32])
    identb_h = DR("identb", [128, 128], BF16)
    identf_h = DR("identf", [128, 128])
    lt64_h = DR("lt64", [64, 64])
    mask_sl_h = DR("mask_sl", [64, 64], BF16)
    e32x2_h = DR("e32x2", [64, 2048], BF16)
    pick4b_h = DR("pick4b", [4, 512], BF16)
    onescol_h = DR("onescol", [128, 1], BF16)
    onesr_h = DR("onesr", [1, 512])
    onesrb_h = DR("onesrb", [1, 512], BF16)
    cvals_h = DR("cvals", [1, 4])

    y_out = DR("y_out", [1024, T_PASS], kind="ExternalOutput")

    wb_in = nc.dram_tensor("wb_in", [34, 128, 8, 128], BF16, kind="Internal").ap()
    wb_out = nc.dram_tensor("wb_out", [8, 128, 16, 128], BF16, kind="Internal").ap()
    wb1 = nc.dram_tensor("wb1", [32, 128, 8, 128], BF16, kind="Internal").ap()
    wb2 = nc.dram_tensor("wb2", [32, 128, 8, 128], BF16, kind="Internal").ap()

    ctx = ExitStack()
    SB = lambda name, shape, dt=F32: ctx.enter_context(
        nc.sbuf_tensor(name, shape, dt)).ap()
    PSUM = lambda name, shape, dt=F32: ctx.enter_context(
        nc.psum_tensor(name, shape, dt)).ap()

    with ctx:
        pA = PSUM("pA", [128, 512])
        pB = PSUM("pB", [128, 512])
        pC = PSUM("pC", [128, 512])
        pM = PSUM("pM", [128, 1024])
        pQ = PSUM("pQ", [64, 512])
        pXT = PSUM("pXT", [64, 1024], BF16)
        pYT = PSUM("pYT", [128, 1024], BF16)

        cstc = SB("cstc", [1, 4])
        idb = SB("idb", [128, 128], BF16)
        idf = SB("idf", [128, 128])
        lt64 = SB("lt64s", [64, 64])
        msl = SB("msls", [64, 64], BF16)
        rhs66 = SB("rhs66", [66, 2048], BF16)
        onesr = SB("onesr_s", [1, 512])
        onesrb = SB("onesrb_s", [1, 512], BF16)
        onescol = SB("onescol_s", [128, 1], BF16)
        pick4b = SB("pick4b_s", [4, 512], BF16)
        dtb = SB("dtb", [32, 1])
        amk = SB("amk", [32, 1])
        acol = SB("acol", [32, 1])
        alogs = SB("alogs", [32, 1])
        drow = SB("drow", [1, 32])
        dtile = SB("dtile", [64, 32], BF16)
        gbc = SB("gbc", [4, 1])
        cwh = SB("cwh", [128, 18 * 4])
        cbh = SB("cbh", [128, 18])
        nwc = SB("nwc", [128, 16])
        g1s = SB("g1s", [128, 32])
        be1s = SB("be1s", [128, 32])
        g2s = SB("g2s", [128, 32])
        b1fs = SB("b1fs", [1, 4096], BF16)
        b2fs = SB("b2fs", [1, 4096], BF16)
        be2k = SB("be2k", [4, 1024], BF16)
        gws = SB("gws", [128, 32], BF16)
        wm1 = SB("wm1", [128, 8 * 4], BF16)
        wm2 = SB("wm2", [128, 8 * 4], BF16)
        bm1 = SB("bm1", [1, 4])
        bm2 = SB("bm2", [1, 4])
        stk = SB("stk", [128, 2048], BF16)
        sd = SB("sd", [128, 2048], BF16)
        dchbc = SB("dchbc", [128, 8 * 32], BF16)

        x32 = SB("x32", [128, 8 * 512])
        xbfb = SB("xbfb", [128, 512], BF16)
        xn = SB("xn", [128, 8 * 512], BF16)
        sq1 = SB("sq1", [128, 512], BF16)
        sq2 = SB("sq2", [128, 512], BF16)
        zh = SB("zh", [128, 16 * 512], BF16)
        xbc = SB("xbc", [128, 18 * 515], BF16)
        xc = SB("xc", [128, 18 * 512], BF16)
        cva = SB("cva", [128, 512])
        ths = SB("ths", [128, 512], BF16)
        dts = SB("dts", [32, 512])
        dtv = SB("dtv", [32, 512])
        adt = SB("adt", [32, 512])
        cumA = SB("cumA", [32, 512])
        cumAT = SB("cumAT", [64, 8 * 32])
        adtT = SB("adtT", [64, 64])
        dtT = SB("dtT", [64, 8 * 32], BF16)
        dcyT = SB("dcyT", [64, 8 * 32], BF16)
        dscr = SB("dscr", [64, 64])
        sumar = SB("sumar", [1, 64])
        phl = SB("phl", [32, 128], BF16)
        l66src = SB("l66src", [32, 128], BF16)
        l66 = SB("l66", [66, 128], BF16)
        gsb = SB("gsb", [64, 64], BF16)
        btb = SB("btb", [64, 64], BF16)
        dfc = SB("dfc", [128, 1024])
        msb = SB("msb", [128, 2048], BF16)
        xdt = SB("xdt", [64, 2048], BF16)
        ytk = SB("ytk", [64, 2048], BF16)
        dterm = SB("dterm", [64, 2048], BF16)
        slz = SB("slz", [128, 1024], BF16)
        slzt = SB("slzt", [128, 1024], BF16)
        yg = SB("yg", [128, 16 * 512], BF16)
        accb = SB("accb", [128, 8 * 512], BF16)
        tsc = SB("tsc", [128, 512], BF16)
        r_a = SB("r_a", [1, 512])
        r_b = SB("r_b", [1, 512])
        r_m = SB("r_m", [1, 512])
        statCb = SB("statCb", [4, 512], BF16)
        statC2b = SB("statC2b", [4, 512], BF16)
        wgt = SB("wgt", [4, 512])
        wgtb = SB("wgtb", [4, 512], BF16)
        wrow = SB("wrow", [1, 512])
        gsr = SB("gsr", [4, 512])
        gst = SB("gst", [128, 16])
        gw1 = SB("gw1", [128, 16])
        gw2 = SB("gw2", [128, 16])
        gr1 = SB("gr1", [128, 4])
        gr2 = SB("gr2", [128, 4])
        dbgg = SB("dbgg", [128, 20])
        wt = [SB(f"wt{i}", [128, 8 * 128], BF16) for i in range(2)]
        wsum = SB("wsum", [128, 1])
        praw_ = [x32.bitcast(BF16)[:, 0:4096], x32.bitcast(BF16)[:, 4096:8192]]
        h1 = zh[:, 0:4096]
        wpf_ = [yg.bitcast(F32)[:, 0:2176], xbc.bitcast(F32)[:, 0:2176]]
        wpb_ = [yg[:, 4352:6528], xbc[:, 4352:6528]]
        wpt_ = ["yg", "xbc"]
        wpf = wpf_[0]
        wpb = wpb_[0]
        mbc = slz.bitcast(F32)[:, 0:512]
        ibc = slzt.bitcast(F32)[:, 0:512]
        wo = [SB("wo0", [128, 2048], BF16), SB("wo1", [128, 2048], BF16)]
        outs_ = [cva[:, 0:512], ibc]
        outst_ = ["cva", "slzt"]


        dma("qs", cstc[:, :], cvals_h[:, :], writes=["cst"])
        nc.const_aps.register(F32, LN_EPS, cstc[0:1, 0:1])
        nc.const_aps.register(F32, -0.5, cstc[0:1, 1:2])
        dma("qs", idb[:, :], identb_h[:, :], writes=["idb"])
        dma("qs", idf[:, :], identf_h[:, :], writes=["idf"])
        dma("qs", lt64[:, :], lt64_h[:, :], writes=["lt64"])
        dma("qs", msl[:, :], mask_sl_h[:, :], writes=["msl"])
        dma("qs", rhs66[0:64, :], e32x2_h[:, :], writes=["rhs66"])
        dma("qs", onesr[:, :], onesr_h[:, :], writes=["onesr"])
        dma("qs", onesrb[:, :], onesrb_h[:, :], writes=["onesrb"])
        dma("qs", onescol[:, :], onescol_h[:, :], writes=["onescol"])
        dma("qs", pick4b[:, :], pick4b_h[:, :], writes=["pick4b"])
        dma("qs", dtb[:, :], dt_bias_h[:, :], writes=["dtb"])
        dma("qs", amk[:, :], dtmask_h[:, :], writes=["amk"])
        dma("qs", alogs[:, :], a_log_h[:, :], writes=["alogs"])
        dma("qs", drow[:, :], d_row_h[:, :], writes=["drow"])
        dma("qs", gbc[:, :], gate_b_h[:, :], writes=["gbc"])
        dma("qs", nwc[:, :], norm_w_h[:, :], writes=["nwc"])
        dma("qs", g1s[:, :], g1c_h[:, :], writes=["g1s"])
        dma("qs", be1s[:, :], be1c_h[:, :], writes=["be1s"])
        dma("qs", g2s[:, :], g2c_h[:, :], writes=["g2s"])
        dma("qs", x32[64:128, 0:2048], s0_h[:, :], writes=["x32"])

        dma("qs", wpf[:, 0:72], conv_w_h[:, :], writes=["yg"])
        dma("qs", wpf[:, 72:90], conv_b_h[:, :], writes=["yg"])
        V(TS(cwh[:, :], wpf[:, 0:72], 0.5), ["yg"], ["cwh"])
        V(TS(cbh[:, :], wpf[:, 72:90], 0.5), ["yg"], ["cbh"])
        S(ACT(acol[:, :], alogs[:, :], AF.Exp), ["alogs"], ["acol"])
        V(TS(acol[:, :], acol[:, :], -1.0), ["acol"], ["acol"])
        Tn(MM(pC[0:64, 0:32], onesr[0:1, 0:64], drow[:, :]),
           ["onesr", "drow"], ["pC"])
        V(TS(dtile[:, :], pC[0:64, 0:32], 1.0), ["pC"], ["dtile"])
        V(TS(stk[64:128, :], x32[64:128, 0:2048], 1.0), ["x32"], ["stk"])
        V(MEMSET(l66[:, :], 0.0), [], ["l66"])
        V(MEMSET(l66[64:66, :], 1.0), [], ["l66"])
        for (bh, bfs) in ((b1f_h, b1fs), (b2f_h, b2fs)):
            bt = "b1fs" if bfs is b1fs else "b2fs"
            for hf in range(2):
                dma("qs", wpf[0:1, 0:2048], bh[0:1, 2048 * hf:2048 * hf + 2048],
                    writes=["yg"])
                V(TS(bfs[0:1, 2048 * hf:2048 * hf + 2048], wpf[0:1, 0:2048],
                     1.0), ["yg"], [bt])
        for (bch, bm, bmt) in ((b1c_h, bm1, "bm1"), (b2c_h, bm2, "bm2")):
            dma("qs", wpf[0:128, 0:32], bch[:, :], writes=["yg"])
            V(TS(wpb[:, 0:32], wpf[:, 0:32], 1.0), ["yg"], ["yg"])
            Tn(MM(pC[0:1, 0:32], onescol[:, :], wpb[:, 0:32]),
               ["onescol", "yg"], ["pC"])
            V(COPY(r_a[0:1, 0:32], pC[0:1, 0:32]), ["pC"], ["r_a"])
            V(COPY(r_a[0:1, 40:44], onesr[0:1, 0:4]), ["onesr"], ["r_a"])
            for e in range(4):
                V(REDUCE(bm[0:1, e:e + 1], r_a[0:1, 8 * e:8 * e + 8], ALU.add),
                  ["r_a"], [bmt])
            V(COPY(r_a[0:1, 40:44], onesr[0:1, 0:4]), ["onesr"], ["r_a"])
            V(TS(bm[:, :], bm[:, :], 1.0 / 1024.0), [bmt], [bmt])
        dma("qs", wpf[0:4, 0:1024], be2r_h[:, :], writes=["yg"])
        V(TS(be2k[:, :], wpf[0:4, 0:1024], 1.0), ["yg"], ["be2k"])
        dma("qs", wpf[:, 0:32], gate_w_h[:, :], writes=["yg"])
        V(TS(gws[:, :], wpf[:, 0:32], 1.0), ["yg"], ["gws"])

        pp = [0]

        def _wp():
            i = pp[0] % 2
            pp[0] += 1
            return wpf_[i], wpb_[i], wpt_[i]


        for ec in range(34):
            f, b_, t_ = _wp()
            w = 128 if ec < 33 else 32
            src = bass.AP(w_inT_h.tensor, w_inT_h.offset + 128 * ec,
                          [[4256, 128], [128 * 4256, 8], [1, w]])
            dma("qs", f[:, 0:8 * w], src, writes=[t_])
            V(TS(b_[:, 0:8 * w], f[:, 0:8 * w], 0.5 if ec < 16 else 1.0),
              [t_], [t_])
            if ec < 33:
                dma("qc", wb_in[ec, :, :, :], b_[:, 0:1024],
                    reads=[t_], writes=["d_wbin"])
            else:
                for k in range(8):
                    dma("qc", wb_in[33, :, k, 0:32], b_[:, 32 * k:32 * k + 32],
                        reads=[t_], writes=["d_wbin"])
        zhf = zh.bitcast(F32)
        ppf = [yg.bitcast(F32)[:, 1024 * s:1024 * s + 1024] for s in range(4)]
        ppb = [zh[:, 1024 * s:1024 * s + 1024] for s in range(4)]
        pws = [zhf[:, 2048 + s:2049 + s] for s in range(4)]
        lp_iters = []

        def mk_out_iter(j):
            def ld(s):
                dma("qs", ppf[s][:, :], w_outT_h[128 * j:128 * j + 128, :],
                    writes=[f"pp{s}"])

            def rest(s):
                V(TS(ppb[s][:, :], ppf[s][:, :], nwc[:, j:j + 1]),
                  [f"pp{s}", "nwc"], [f"pq{s}"])
                for oc in range(8):
                    dma("qc", wb_out[oc, :, j, :],
                        ppb[s][:, 128 * oc:128 * oc + 128],
                        reads=[f"pq{s}"], writes=["d_wbout"])
            return ld, rest

        def mk_w_iter(wh, wbd, wmv, e, j):
            wmt = "wm1" if wmv is wm1 else "wm2"
            dt_ = "d_wb1" if wbd is wb1 else "d_wb2"

            def ld(s):
                dma("qs", ppf[s][:, :],
                    wh[1024 * e + 128 * j:1024 * e + 128 * j + 128, :],
                    writes=[f"pp{s}"])

            def rest(s):
                V(MEMSET(pws[s], 0.0), [], [f"pw{s}"])
                S(ACT(ppb[s][:, :], ppf[s][:, :], AF.Copy, accum_out=pws[s]),
                  [f"pp{s}", f"pw{s}"], [f"pq{s}", f"pw{s}"])
                V(TS(wmv[:, 4 * j + e:4 * j + e + 1], pws[s], 1.0 / 1024.0),
                  [f"pw{s}"], [wmt])
                for ec in range(8):
                    dma("qc", wbd[8 * e + ec, :, j, :],
                        ppb[s][:, 128 * ec:128 * ec + 128],
                        reads=[f"pq{s}"], writes=[dt_])
            return ld, rest

        for j in range(16):
            lp_iters.append(mk_out_iter(j))
        for (wh, wbd, wmv) in ((w1_h, wb1, wm1), (w2_h, wb2, wm2)):
            for e in range(4):
                for j in range(8):
                    lp_iters.append(mk_w_iter(wh, wbd, wmv, e, j))
        LPN = len(lp_iters)
        LP_NP = 64
        lp_state = [0]

        def lp_win(p):
            return range((LPN * p) // LP_NP, (LPN * (p + 1)) // LP_NP)

        def lp_pump():
            p = lp_state[0]
            lp_state[0] += 1
            if p >= LP_NP:
                return
            for i in lp_win(p):
                lp_iters[i][1](i % 4)
            if p + 1 < LP_NP:
                for i in lp_win(p + 1):
                    lp_iters[i][0](i % 4)

        V(COPY(dbgg[0:1, 0:4], onesr[0:1, 0:4]), ["onesr"],
          ["d_wbin", "yg", "xbc", "pp0", "pp1", "pp2", "pp3", "dbgg"])
        for i in lp_win(0):
            lp_iters[i][0](i % 4)
        V(MEMSET(xbc[:, :], 0.0), [], ["xbc"])

        def vfence():
            V(COPY(dbgg[0:1, 0:4], onesr[0:1, 0:4]), ["onesr"], ["dbgg"])

        def dump(name, ap, tag):
            if name not in _DBG:
                return
            h = nc.dram_tensor("dbg_" + name, [ap.shape[0], ap.free_size()],
                               ap.dtype, kind="ExternalOutput").ap()
            dma("qg", h, ap, reads=[tag] if isinstance(tag, str) else list(tag))

        def emit_tile(u_hbm, t0, main, prefix_mask, tname="", chunk_cb=None):
            wti = [0]

            def load_wt(block_idx, wbd):
                b = wt[wti[0] % 2]
                tag = f"wt{wti[0] % 2}"
                wti[0] += 1
                dtag = ("d_wbin" if wbd is wb_in else
                        ("d_wb1" if wbd is wb1 else "d_wb2"))
                dma("qs", b[:, :], wbd[block_idx, :, :, :], reads=[dtag],
                    writes=[tag])
                return b, tag

            xt = lambda k: "x32" if k < 4 else "x32b"
            for hf in range(2):
                src = bass.AP(u_hbm.tensor,
                              u_hbm.offset + 512 * hf * T_PASS + t0,
                              [[T_PASS, 128], [128 * T_PASS, 4], [1, 512]])
                dma("qs", x32[:, 2048 * hf:2048 * hf + 2048], src,
                    writes=[xt(4 * hf)])
            for k in range(8):
                xs = x32[:, 512 * k:512 * k + 512]
                S(ACT(xbfb[:, :], xs, AF.Copy), [xt(k)], ["xbfb"])
                sq = [sq1, sq2][k % 2]
                sqt = ["sq1", "sq2"][k % 2]
                S(ACT(sq[:, :], xs, AF.Square), [xt(k)], [sqt])
                Tn(MM(pM[0:1, 0:512], onescol[:, :], xbfb[:, :],
                      start=(k == 0), stop=(k == 7)),
                   ["onescol", "xbfb"], ["pM"])
                Tn(MM(pM[0:1, 512:1024], onescol[:, :], sq[:, :],
                      start=(k == 0), stop=(k == 7)),
                   ["onescol", sqt], ["pM"])
            V(TS(r_m[:, :], pM[0:1, 0:512], 1.0 / 1024.0), ["pM"], ["r_m"])
            V(TT(r_a[:, :], r_m[:, :], r_m[:, :]), ["r_m"], ["r_a"])
            V(STT(r_b[:, :], pM[0:1, 512:1024], 1.0 / 1024.0, r_a[:, :],
                  ALU.mult, ALU.subtract), ["pM", "r_a"], ["r_b"])
            S(ACT(r_a[:, :], r_b[:, :], AF.Ln, bias=LN_EPS), ["r_b", "cst"], ["r_a"])
            S(ACT(r_b[:, :], r_a[:, :], AF.Exp, scale=-0.5), ["r_a", "cst"], ["r_b"])
            Tn(MM(pM[:, 0:512], onesr[0:1, 0:128], r_m[:, :]),
               ["onesr", "r_m"], ["pM"])
            Tn(MM(pM[:, 512:1024], onesr[0:1, 0:128], r_b[:, :]),
               ["onesr", "r_b"], ["pM"])
            V(COPY(mbc[:, :], pM[:, 0:512]), ["pM"], ["slz"])
            V(COPY(ibc[:, :], pM[:, 512:1024]), ["pM"], ["slzt"])
            for k in range(8):
                V(TT(cva[:, :], x32[:, 512 * k:512 * k + 512], mbc[:, :],
                     ALU.subtract), [xt(k), "slz"], ["cva"])
                G(TT(xn[:, 512 * k:512 * k + 512], cva[:, :], ibc[:, :]),
                  ["cva", "slzt"], ["xn"])
            dump(tname + "_xn", xn, "xn")

            for b in range(18):
                V(COPY(xbc[:, 515 * b:515 * b + 3],
                       xbc[:, 515 * b + 512:515 * b + 515]), ["xbc"], ["xbc"])

            ping = [0]

            def inproj_chunk(ec, dest_fn, m_rows=128, tp_extra=None):
                wtb, wtag = load_wt(ec, wb_in)
                ps, pst = (pA, "pA") if ping[0] % 2 == 0 else (pB, "pB")
                ping[0] += 1
                for k in range(8):
                    Tn(MM(ps[0:m_rows, 0:512],
                          wtb[:, 128 * k:128 * k + m_rows],
                          xn[:, 512 * k:512 * k + 512],
                          start=(k == 0), stop=(k == 7), tp=tp_extra),
                       [wtag, "xn"], [pst])
                dest_fn(ps, pst)
                return ps, pst

            if main:
                ecs = list(range(16, 34)) + list(range(0, 16))
            else:
                ecs = list(range(16, 34))
            for ec in ecs:
                if 16 <= ec < 32:
                    b = ec - 16
                    inproj_chunk(ec, lambda ps, pst, b=b: S(
                        ACT(xbc[:, 515 * b + 3:515 * b + 515], ps[:, 0:512],
                            AF.Copy), [pst], ["xbc"]))
                elif ec == 32:
                    wtb, wtag = load_wt(32, wb_in)
                    ps, pst = (pA, "pA") if ping[0] % 2 == 0 else (pB, "pB")
                    ping[0] += 1
                    for k in range(8):
                        Tn(MM(ps[0:64, 0:512], wtb[:, 128 * k:128 * k + 64],
                              xn[:, 512 * k:512 * k + 512],
                              start=(k == 0), stop=(k == 7)),
                           [wtag, "xn"], [pst])
                    for k in range(8):
                        Tn(MM(ps[64:128, 0:512], wtb[:, 128 * k + 64:128 * k + 128],
                              xn[:, 512 * k:512 * k + 512],
                              start=(k == 0), stop=(k == 7), tp=(0, 64)),
                           [wtag, "xn"], [pst])
                    S(ACT(xbc[:, 515 * 16 + 3:515 * 16 + 515], ps[:, 0:512],
                          AF.Copy), [pst], ["xbc"])
                    ps2, pst2 = (pA, "pA") if ping[0] % 2 == 0 else (pB, "pB")
                    ping[0] += 1
                    for k in range(8):
                        Tn(MM(ps2[64:128, 0:512], wtb[:, 128 * k:128 * k + 64],
                              xn[:, 512 * k:512 * k + 512],
                              start=(k == 0), stop=(k == 7), tp=(0, 64)),
                           [wtag, "xn"], [pst2])
                    S(ACT(xbc[64:128, 515 * 17 + 3:515 * 17 + 515],
                          ps2[64:128, 0:512], AF.Copy), [pst2], ["xbc"])
                elif ec == 33:
                    wtb, wtag = load_wt(33, wb_in)
                    for k in range(8):
                        Tn(MM(pC[0:32, 0:512], wtb[:, 128 * k:128 * k + 32],
                              xn[:, 512 * k:512 * k + 512],
                              start=(k == 0), stop=(k == 7)),
                           [wtag, "xn"], ["pC"])
                    S(ACT(dts[:, :], pC[0:32, 0:512], AF.Exp, bias=dtb[:, :]),
                      ["pC", "dtb"], ["dts"])
                    V(TS(dts[:, :], dts[:, :], 1.0, None, ALU.add), ["dts"], ["dts"])
                    S(ACT(dtv[:, :], dts[:, :], AF.Ln), ["dts"], ["dtv"])
                    if not main:
                        V(TS(dtv[:, :], dtv[:, :], amk[:, :]), ["dtv", "amk"],
                          ["dtv"])
                    V(TS(adt[:, :], dtv[:, :], acol[:, :]), ["dtv", "acol"],
                      ["adt"])
                    V(COPY(dbgg[0:1, 0:4], onesr[0:1, 0:4]), ["onesr"],
                      ["pQ", "pM", "pC", "dtbk0", "dtbk1", "dtbk4", "dbgg"])
                    for c in range(NCH):
                        cc = 64 * c
                        p = c % 2
                        bq = pQ[0:64, 0:512] if p == 0 else pM[0:64, 512:1024]
                        bk = f"dtbk{p}"
                        at = adtT[:, 32 * p:32 * p + 32]
                        ds = dscr[:, 32 * p:32 * p + 32]
                        su = sumar[0:1, 32 * p:32 * p + 32]
                        Tn(TRANS(bq[:, 0:32], adt[:, cc:cc + 64],
                                 idf[0:32, 0:32]), ["adt", "idf"], [bk])
                        V(COPY(at, bq[:, 0:32]), [bk], [f"adtT{p}"])
                        Tn(MM(bq[:, 32:64], lt64[:, :], at),
                           ["lt64", f"adtT{p}"], [bk])
                        V(COPY(cumAT[:, 32 * c:32 * c + 32], bq[:, 32:64]),
                          [bk], [f"cumAT{c}"])
                        Tn(TRANS(bq[0:32, 64:128],
                                 cumAT[:, 32 * c:32 * c + 32],
                                 idf[0:64, 0:64]), [f"cumAT{c}", "idf"], [bk])
                        V(COPY(cumA[:, cc:cc + 64], bq[0:32, 64:128]),
                          [bk], [f"cumA{c}"])
                        Tn(TRANS(bq[:, 128:160], dtv[:, cc:cc + 64],
                                 idf[0:32, 0:32]), ["dtv", "idf"], [bk])
                        V(TS(dtT[:, 32 * c:32 * c + 32], bq[:, 128:160],
                             1.0), [bk], [f"dtT{c}"])
                        dma("qs", su, cumA[0:32, cc + 63:cc + 64],
                            reads=[f"cumA{c}"], writes=[f"sumar{p}"])
                        Tn(MM(bq[:, 160:192], onesr[0:1, 0:64], su),
                           ["onesr", f"sumar{p}"], [bk])
                        V(TT(ds, bq[:, 160:192],
                             cumAT[:, 32 * c:32 * c + 32], ALU.subtract),
                          [bk, f"cumAT{c}"], [f"dscr{p}"])
                        S(ACT(dcyT[:, 32 * c:32 * c + 32], ds, AF.Exp),
                          [f"dscr{p}"], [f"dcyT{c}"])
                        Tn(MM(pC[64:128, 256 + 32 * p:288 + 32 * p],
                              onesr[0:1, 0:64], su, tp=(0, 64)),
                           ["onesr", f"sumar{p}"], ["dtbk4"])
                        S(ACT(dchbc[64:128, 32 * c:32 * c + 32],
                              pC[64:128, 256 + 32 * p:288 + 32 * p], AF.Exp),
                          ["dtbk4"], [f"dchbc{c}"])
                        if chunk_cb is not None:
                            chunk_cb()
                    V(COPY(dbgg[0:1, 0:4], onesr[0:1, 0:4]), ["onesr"],
                      ["pQ", "pM", "pC", "dtbk0", "dtbk1", "dtbk4", "dbgg"])
                else:
                    inproj_chunk(ec, lambda ps, pst, ec=ec: S(
                        ACT(zh[:, 512 * ec:512 * ec + 512], ps[:, 0:512],
                            AF.Copy), [pst], ["zh"]))

            for b in range(18):
                lo = 64 if b == 17 else 0
                xb = lambda k: xbc[lo:128, 515 * b + k:515 * b + k + 512]
                V(TS(cva[lo:128, :], xb(0), cwh[lo:128, 4 * b:4 * b + 1],
                     cbh[lo:128, b:b + 1], ALU.mult, ALU.add),
                  ["xbc", "cwh", "cbh"], ["cva"])
                for k in (1, 2, 3):
                    V(STT(cva[lo:128, :], xb(k), cwh[lo:128, 4 * b + k:4 * b + k + 1],
                          cva[lo:128, :]), ["xbc", "cwh", "cva"], ["cva"])
                S(ACT(ths[lo:128, :], cva[lo:128, :], AF.Tanh), ["cva"], ["ths"])
                V(STT(xc[lo:128, 512 * b:512 * b + 512], ths[lo:128, :], 1.0,
                      cva[lo:128, :], ALU.add, ALU.mult),
                  ["ths", "cva"], ["xc"])

            dump(tname + "_zh", zh, "zh")
            dump(tname + "_xc", xc, "xc")
            dump(tname + "_dtv", dtv, "dtv")
            dump(tname + "_cumA", cumA, [f"cumA{i}" for i in range(NCH)])
            dump(tname + "_cumAT", cumAT, [f"cumAT{i}" for i in range(NCH)])
            dump(tname + "_dtT", dtT, [f"dtT{i}" for i in range(NCH)])
            dump(tname + "_dcyT", dcyT, [f"dcyT{i}" for i in range(NCH)])
            dump(tname + "_dchbc", dchbc, [f"dchbc{i}" for i in range(NCH)])
            for c in range(NCH):
                cc = 64 * c
                if main:
                    V(TS(l66src[:, 0:64], cumA[:, cc:cc + 64], -1.0),
                      [f"cumA{c}"], ["l66src"])
                    vfence()
                    V(STT(l66src[:, 64:128], cumA[:, cc:cc + 64], -1.0,
                          l66src[:, 0:64], ALU.mult, ALU.subtract),
                      [f"cumA{c}", "l66src"], ["l66src"])
                    vfence()
                    dma("qc", l66[0:64, 0:64], l66src[:, :],
                        reads=["l66src"], writes=["l66"])
                    V(TS(phl[:, 0:64], cumA[:, cc:cc + 64], 1.0),
                      [f"cumA{c}"], ["phl"])
                    vfence()
                    V(STT(phl[:, 64:128], cumA[:, cc:cc + 64], 1.0,
                          phl[:, 0:64], ALU.mult, ALU.subtract),
                      [f"cumA{c}", "phl"], ["phl"])
                    vfence()
                    dma("qc", rhs66[64:65, :], phl[:, 0:64],
                        reads=["phl"], writes=["rhs66"])
                    dma("qc", rhs66[65:66, :], phl[:, 64:128],
                        reads=["phl"], writes=["rhs66"])
                    Tn(MM(pC[0:64, 64:128], xc[64:128, 512 * 17 + cc:512 * 17 + cc + 64],
                          xc[64:128, 512 * 16 + cc:512 * 16 + cc + 64],
                          tp=(64, 0)), ["xc"], ["pC"])
                    V(TT(gsb[:, :], pC[0:64, 64:128], msl[:, :]),
                      ["pC", "msl"], ["gsb"])
                    vfence()
                    for half in range(2):
                        for gq in range(2):
                            Tn(MM(pM[:, 512 * gq:512 * gq + 512], l66[:, :],
                                  rhs66[:, 1024 * half + 512 * gq:
                                        1024 * half + 512 * gq + 512]),
                               ["l66", "rhs66"], ["pM"])
                        V(TS(dfc[:, :], pM[:, :], 0.0, None, ALU.min),
                          ["pM"], ["dfc"])
                        S(ACT(msb[:, 1024 * half:1024 * half + 1024],
                              dfc[:, :], AF.Exp), ["dfc"], ["msb"])
                    V(TT(msb[0:64, :], msb[0:64, :],
                         fview(gsb[0:64, 0:64], [(0, 32), (1, 64)])),
                      ["gsb", "msb"], ["msb"])
                    G(TT(msb[64:128, :], msb[64:128, :],
                         fview(xc[64:128, 512 * 16 + cc:512 * 16 + cc + 64],
                               [(0, 32), (1, 64)])),
                      ["xc", "msb"], ["msb"])

                for half in range(2):
                    for j8 in range(8):
                        j = 8 * half + j8
                        Tn(TRANS(pXT[0:64, 128 * j8:128 * j8 + 128],
                                 xc[:, 512 * j + cc:512 * j + cc + 64],
                                 idb[:, :]), ["xc", "idb"], ["pXT"])
                    dtview = fview(dtT[0:64, 32 * c + 16 * half:32 * c + 16 * half + 16],
                                   [(2, 8), (1, 2), (0, 64)])
                    V(TT(stk[0:64, 1024 * half:1024 * half + 1024],
                         pXT[:, :], dtview), ["pXT", f"dtT{c}"], ["stk"])
                    if main:
                        dview = fview(dtile[0:64, 16 * half:16 * half + 16],
                                      [(2, 8), (1, 2), (0, 64)])
                        V(TT(dterm[0:64, 1024 * half:1024 * half + 1024],
                             pXT[:, :], dview), ["pXT", "dtile"], ["dterm"])
                G(TT(xdt[:, :], stk[0:64, :],
                     fview(dcyT[0:64, 32 * c:32 * c + 32], [(1, 32), (0, 64)])),
                  ["stk", f"dcyT{c}"], ["xdt"])
                dump(tname + f"_c{c}_stkx", stk[0:64, :], "stk")
                dump(tname + f"_c{c}_xdt", xdt, "xdt")
                Tn(TRANS(pXT[0:64, 0:64],
                         xc[64:128, 512 * 17 + cc:512 * 17 + cc + 64],
                         idb[64:128, 64:128], tp=(64, 0)), ["xc", "idb"], ["pXT"])
                V(TS(btb[:, :], pXT[0:64, 0:64], 1.0), ["pXT"], ["btb"])

                if main:
                    dump(tname + f"_c{c}_msb", msb, "msb")
                    dump(tname + f"_c{c}_gsb", gsb, "gsb")
                if main:
                    for q in range(4):
                        ps, pst = (pQ, "pQ") if q % 2 == 0 else (pA, "pA")
                        for h8 in range(8):
                            h = 8 * q + h8
                            Tn(MM(ps[0:64, 64 * h8:64 * h8 + 64],
                                  msb[:, 64 * h:64 * h + 64],
                                  stk[:, 64 * h:64 * h + 64]),
                               ["msb", "stk"], [pst])
                        V(TT(ytk[:, 512 * q:512 * q + 512], ps[0:64, :],
                             dterm[:, 512 * q:512 * q + 512], ALU.add),
                          [pst, "dterm"], ["ytk"])

                for gq in range(4):
                    off = 512 * (gq % 2)
                    Tn(MM(pM[64:128, off:off + 512], btb[:, :],
                          xdt[:, 512 * gq:512 * gq + 512], tp=(0, 64)),
                       ["btb", "xdt"], ["pM"])
                    if gq == 0:
                        G(TT(sd[64:128, :], stk[64:128, :],
                             fview(dchbc[64:128, 32 * c:32 * c + 32],
                                   [(1, 32), (0, 64)])),
                          ["stk", f"dchbc{c}"], ["sd"])
                    V(TT(stk[64:128, 512 * gq:512 * gq + 512],
                         sd[64:128, 512 * gq:512 * gq + 512],
                         pM[64:128, off:off + 512], ALU.add),
                      ["sd", "pM"], ["stk"])
                dump(tname + f"_c{c}_S", stk[64:128, :], "stk")

                if main:
                    zview = fview(zh[:, cc:cc + 64], [(512, 16), (1, 64)])
                    S(ACT(slzt[:, :], zview, AF.Tanh), ["zh"], ["slzt"])
                    V(STT(slz[:, :], slzt[:, :], 1.0, zview, ALU.add, ALU.mult),
                      ["slzt", "zh"], ["slz"])
                    for j in range(16):
                        Tn(TRANS(pYT[:, 64 * j:64 * j + 64],
                                 ytk[0:64, 128 * j:128 * j + 128],
                                 idb[0:64, 0:64]), ["ytk", "idb"], ["pYT"])
                    ygview = fview(yg[:, cc:cc + 64], [(512, 16), (1, 64)])
                    V(TT(ygview, pYT[:, :], slz[:, :]), ["pYT", "slz"], ["yg"])
                    dump(tname + f"_c{c}_ytk", ytk, "ytk")
                if chunk_cb is not None:
                    chunk_cb()

            if not main:
                return

            for k in range(8):
                Tn(MM(pC[0:4, 0:512], gws[:, 4 * k:4 * k + 4],
                      xn[:, 512 * k:512 * k + 512],
                      start=(k == 0), stop=(k == 7)), ["gws", "xn"], ["pC"])
            V(TS(gsr[:, :], pC[0:4, 0:512], 1.0, gbc[:, :], ALU.mult, ALU.add),
              ["pC", "gbc"], ["gsr"])
            vfence()
            dump(tname + "_gsr", gsr, "gsr")
            for gq in range(4):
                Tn(TRANS(pC[0:128, 448 + 4 * gq:448 + 4 * gq + 4],
                         gsr[0:4, 128 * gq:128 * gq + 128], idf[0:4, 0:4]),
                   ["gsr", "idf"], ["pC"])
            V(COPY(gst[:, :], pC[:, 448:464]), ["pC"], ["gst"])
            vfence()
            dump(tname + "_gst", gst, "gst")
            gv = lambda t: fview(t[0:128, 0:4], [(1, 4), (0, 4)])
            vfence()
            for gq in range(4):
                V(REDUCE(gr1[:, gq:gq + 1], gst[:, 4 * gq:4 * gq + 4], ALU.max),
                  ["gst"], ["gr1"])
            vfence()
            dump(tname + "_gr1", gr1, "gr1")
            V(TT(gw1[:, :], gst[:, :], gv(gr1), ALU.is_equal), ["gst", "gr1"],
              ["gw1"])
            vfence()
            dump(tname + "_gw1a", gw1, "gw1")
            V(STT(gw2[:, :], gw1[:, :], -1e30, gst[:, :], ALU.mult, ALU.add),
              ["gw1", "gst"], ["gw2"])
            vfence()
            vfence()
            for gq in range(4):
                V(REDUCE(gr2[:, gq:gq + 1], gw2[:, 4 * gq:4 * gq + 4], ALU.max),
                  ["gw2"], ["gr2"])
            vfence()
            dump(tname + "_m2", gr2, "gr2")
            V(TT(gw1[:, :], gst[:, :], gv(gr2), ALU.is_ge), ["gst", "gr2"],
              ["gw1"])
            vfence()
            dump(tname + "_top2", gw1, "gw1")
            for gq in range(4):
                V(REDUCE(dbgg[:, gq:gq + 1], gst[:, 4 * gq:4 * gq + 4],
                         ALU.max), ["gst"], ["dbgg"])
            V(COPY(dbgg[:, 4:20], gst[:, :]), ["gst"], ["dbgg"])
            dump(tname + "_dbgg", dbgg, "dbgg")
            V(TT(gw2[:, :], gst[:, :], gv(gr1), ALU.subtract), ["gst", "gr1"],
              ["gw2"])
            vfence()
            dump(tname + "_gw2s", gw2, "gw2")
            S(ACT(gst[:, :], gw2[:, :], AF.Exp), ["gw2"], ["gst"])
            V(TT(gw2[:, :], gst[:, :], gw1[:, :]), ["gst", "gw1"], ["gw2"])
            vfence()
            vfence()
            for gq in range(4):
                V(REDUCE(dbgg[:, 16 + gq:16 + gq + 1],
                         gw2[:, 4 * gq:4 * gq + 4], ALU.add),
                  ["gw2"], ["dbgg"])
            vfence()
            dump(tname + "_den", dbgg[:, 16:20], "dbgg")
            V(RECIP(gr2[:, :], dbgg[:, 16:20]), ["dbgg"], ["gr2"])
            vfence()
            dump(tname + "_rec", gr2, "gr2")
            V(TT(gw1[:, :], gw2[:, :], gv(gr2)), ["gw2", "gr2"], ["gw1"])
            vfence()
            dump(tname + "_gwf", gw1, "gw1")
            dump(tname + "_gex", gw2, "gw2")
            for gq in range(4):
                Tn(TRANS(pC[0:4, 0 + 128 * gq:0 + 128 * gq + 128],
                         gw1[:, 4 * gq:4 * gq + 4], idf[:, :]),
                   ["gw1", "idf"], ["pC"])
            V(COPY(wgt[:, :], pC[0:4, 0:512]), ["pC"], ["wgt"])
            vfence()
            dump(tname + "_wgt", wgt, "wgt")
            dump(tname + "_yg", yg, "yg")
            V(TS(wgtb[:, :], wgt[:, :], 1.0), ["wgt"], ["wgtb"])

            for k in range(8):
                Tn(MM(pM[0:4, 0:512], wm1[:, 4 * k:4 * k + 4],
                      xn[:, 512 * k:512 * k + 512],
                      start=(k == 0), stop=False), ["wm1", "xn"], ["pM"])
            Tn(MM(pM[0:4, 0:512], bm1[:, :], onesr[:, :], start=False,
                  stop=True), ["bm1", "onesr", "pM"], ["pM"])
            V(TS(statCb[:, :], pM[0:4, 0:512], -1.0), ["pM"], ["statCb"])
            for ec in range(8):
                ps, pst = (pA, "pA") if ping[0] % 2 == 0 else (pB, "pB")
                ping[0] += 1
                Tn(MM(ps[:, 0:512], be2k[:, 128 * ec:128 * ec + 128],
                      wgtb[:, :]), ["be2k", "wgtb"], [pst])
                V(TT(accb[:, 512 * ec:512 * ec + 512], ps[:, 0:512],
                     xn[:, 512 * ec:512 * ec + 512], ALU.add),
                  [pst, "xn"], ["accb"])

            dump(tname + "_statCb", statCb, "statCb")
            for e in range(4):
                praw = praw_[e % 2]
                prt = ["x32", "x32b"][e % 2]
                for ec in range(8):
                    wtb, wtag = load_wt(8 * e + ec, wb1)
                    ps, pst = (pA, "pA") if ping[0] % 2 == 0 else (pB, "pB")
                    ping[0] += 1
                    for k in range(8):
                        Tn(MM(ps[:, 0:512], wtb[:, 128 * k:128 * k + 128],
                              xn[:, 512 * k:512 * k + 512],
                              start=(k == 0), stop=False), [wtag, "xn"], [pst])
                    Tn(MM(ps[:, 0:512],
                          b1fs[0:1, 1024 * e + 128 * ec:1024 * e + 128 * ec + 128],
                          onesrb[:, :], start=False, stop=False),
                       ["b1fs", "onesrb", pst], [pst])
                    Tn(MM(ps[:, 0:512], pick4b[:, 128 * e:128 * e + 128],
                          statCb[:, :], start=False, stop=True),
                       ["pick4b", "statCb", pst], [pst])
                    S(ACT(praw[:, 512 * ec:512 * ec + 512], ps[:, 0:512],
                          AF.Copy), [pst], [prt])
                    sq = [sq1, sq2][ec % 2]
                    sqt = ["sq1", "sq2"][ec % 2]
                    S(ACT(sq[:, :], praw[:, 512 * ec:512 * ec + 512],
                          AF.Square), [prt], [sqt])
                    Tn(MM(pC[0:1, 0:512], onescol[:, :], sq[:, :],
                          start=(ec == 0), stop=(ec == 7)),
                       ["onescol", sqt], ["pC"])
                V(TS(r_b[:, :], pC[0:1, 0:512], 1.0 / 1024.0), ["pC"], ["r_b"])
                S(ACT(r_a[:, :], r_b[:, :], AF.Ln, bias=LN_EPS), ["r_b", "cst"], ["r_a"])
                S(ACT(r_b[:, :], r_a[:, :], AF.Exp, scale=-0.5), ["r_a", "cst"], ["r_b"])
                Tn(MM(pM[:, 0:512], onesr[0:1, 0:128], r_b[:, :]),
                   ["onesr", "r_b"], ["pM"])
                for ec in range(8):
                    V(STT(tsc[:, :], praw[:, 512 * ec:512 * ec + 512],
                          g1s[:, 8 * e + ec:8 * e + ec + 1], pM[:, 0:512],
                          ALU.mult, ALU.mult), [prt, "g1s", "pM"], ["tsc"])
                    V(TS(h1[:, 512 * ec:512 * ec + 512], tsc[:, :],
                         be1s[:, 8 * e + ec:8 * e + ec + 1], 0.0,
                         ALU.add, ALU.max), ["tsc", "be1s"], ["zh"])
                dump(tname + f"_e{e}_h1", h1, "zh")
                for k in range(8):
                    Tn(MM(pM[0:4, 512:1024], wm2[:, 4 * k:4 * k + 4],
                          h1[:, 512 * k:512 * k + 512],
                          start=(k == 0), stop=False), ["wm2", "zh"], ["pM"])
                Tn(MM(pM[0:4, 512:1024], bm2[:, :], onesr[:, :], start=False,
                      stop=True), ["bm2", "onesr", "pM"], ["pM"])
                V(TS(statC2b[:, :], pM[0:4, 512:1024], -1.0), ["pM"], ["statC2b"])
                for ec in range(8):
                    wtb, wtag = load_wt(8 * e + ec, wb2)
                    ps, pst = (pA, "pA") if ping[0] % 2 == 0 else (pB, "pB")
                    ping[0] += 1
                    for k in range(8):
                        Tn(MM(ps[:, 0:512], wtb[:, 128 * k:128 * k + 128],
                              h1[:, 512 * k:512 * k + 512],
                              start=(k == 0), stop=False), [wtag, "zh"], [pst])
                    Tn(MM(ps[:, 0:512],
                          b2fs[0:1, 1024 * e + 128 * ec:1024 * e + 128 * ec + 128],
                          onesrb[:, :], start=False, stop=False),
                       ["b2fs", "onesrb", pst], [pst])
                    Tn(MM(ps[:, 0:512], pick4b[:, 128 * e:128 * e + 128],
                          statC2b[:, :], start=False, stop=True),
                       ["pick4b", "statC2b", pst], [pst])
                    S(ACT(praw[:, 512 * ec:512 * ec + 512], ps[:, 0:512],
                          AF.Copy), [pst], [prt])
                    sq = [sq1, sq2][ec % 2]
                    sqt = ["sq1", "sq2"][ec % 2]
                    S(ACT(sq[:, :], praw[:, 512 * ec:512 * ec + 512],
                          AF.Square), [prt], [sqt])
                    Tn(MM(pC[0:1, 0:512], onescol[:, :], sq[:, :],
                          start=(ec == 0), stop=(ec == 7)),
                       ["onescol", sqt], ["pC"])
                V(TS(r_b[:, :], pC[0:1, 0:512], 1.0 / 1024.0), ["pC"], ["r_b"])
                S(ACT(r_a[:, :], r_b[:, :], AF.Ln, bias=LN_EPS), ["r_b", "cst"], ["r_a"])
                S(ACT(r_b[:, :], r_a[:, :], AF.Exp, scale=-0.5), ["r_a", "cst"], ["r_b"])
                dma("qc", wrow[:, :], wgt[e:e + 1, :], reads=["wgt"],
                    writes=["wrow"])
                V(TT(r_a[:, :], r_b[:, :], wrow[:, :]), ["r_b", "wrow"], ["r_a"])
                Tn(MM(pM[:, 512:1024], onesr[0:1, 0:128], r_a[:, :]),
                   ["onesr", "r_a"], ["pM"])
                for ec in range(8):
                    V(STT(tsc[:, :], praw[:, 512 * ec:512 * ec + 512],
                          g2s[:, 8 * e + ec:8 * e + ec + 1], pM[:, 512:1024],
                          ALU.mult, ALU.mult), [prt, "g2s", "pM"], ["tsc"])
                    V(TT(accb[:, 512 * ec:512 * ec + 512],
                         accb[:, 512 * ec:512 * ec + 512], tsc[:, :], ALU.add),
                      ["tsc", "accb"], ["accb"])

            dump(tname + "_accb", accb, "accb")
            for j in range(16):
                sq = [sq1, sq2][j % 2]
                sqt = ["sq1", "sq2"][j % 2]
                S(ACT(sq[:, :], yg[:, 512 * j:512 * j + 512], AF.Square),
                  ["yg"], [sqt])
                Tn(MM(pM[0:1, 0:512], onescol[:, :], sq[:, :],
                      start=(j == 0), stop=(j == 15)), ["onescol", sqt], ["pM"])
            V(TS(r_b[:, :], pM[0:1, 0:512], 1.0 / 2048.0), ["pM"], ["r_b"])
            S(ACT(r_a[:, :], r_b[:, :], AF.Ln, bias=EPS), ["r_b", "cst"], ["r_a"])
            S(ACT(r_b[:, :], r_a[:, :], AF.Exp, scale=-0.5), ["r_a", "cst"], ["r_b"])
            Tn(MM(pC[:, 0:512], onesr[0:1, 0:128], r_b[:, :]),
               ["onesr", "r_b"], ["pC"])
            V(COPY(mbc[:, :], pC[:, 0:512]), ["pC"], ["slz"])
            for hf in range(2):
                src = bass.AP(u_hbm.tensor,
                              u_hbm.offset + 512 * hf * T_PASS + t0,
                              [[T_PASS, 128], [128 * T_PASS, 4], [1, 512]])
                dma("qs", x32[:, 2048 * hf:2048 * hf + 2048], src,
                    writes=[xt(4 * hf)])
            for oc in range(8):
                wob = wo[oc % 2]
                wotag = ["wo0", "wo1"][oc % 2]
                dma("qs", wob[:, :], wb_out[oc, :, :, :],
                    reads=["d_wbout"], writes=[wotag])
                ps, pst = (pA, "pA") if ping[0] % 2 == 0 else (pB, "pB")
                ping[0] += 1
                for k in range(16):
                    Tn(MM(ps[:, 0:512], wob[:, 128 * k:128 * k + 128],
                          yg[:, 512 * k:512 * k + 512],
                          start=(k == 0), stop=(k == 15)), [wotag, "yg"], [pst])
                oh = outs_[oc % 2]
                ot = outst_[oc % 2]
                V(TT(oh, ps[:, 0:512], mbc[:, :]), [pst, "slz"], [ot])
                G(TT(oh, oh, accb[:, 512 * oc:512 * oc + 512], ALU.add),
                  [ot, "accb"], [ot])
                V(TT(oh, oh, x32[:, 512 * oc:512 * oc + 512], ALU.add),
                  [ot, xt(oc)], [ot])
                dma("qc", y_out[128 * oc:128 * oc + 128, t0:t0 + 512], oh,
                    reads=[ot])

        dump("wm1", wm1, "wm1")
        dump("wm2", wm2, "wm2")
        dump("bm1", bm1, "bm1")
        dump("bm2", bm2, "bm2")
        dump("b1fs", b1fs, "b1fs")
        dump("gws_d", gws, "gws")
        for t in range(T_PASS // TILE):
            emit_tile(u_pref, TILE * t, main=False, prefix_mask=True,
                      tname=f"p{t}", chunk_cb=lp_pump)
        assert lp_state[0] == LP_NP, lp_state
        V(COPY(dbgg[0:1, 0:4], onesr[0:1, 0:4]), ["onesr"],
          ["d_wbout", "d_wb1", "d_wb2", "yg", "zh",
           "pp0", "pp1", "pp2", "pp3", "pq0", "pq1", "pq2", "pq3",
           "pw0", "pw1", "pw2", "pw3", "dbgg"])
        for t in range(T_PASS // TILE):
            emit_tile(u_main, TILE * t, main=True, prefix_mask=False,
                      tname=f"m{t}")

        em.finalize()
    return nc



_NC_CACHE = {}


def _consts():
    identb = np.eye(128, dtype=bf16)
    identf = np.eye(128, dtype=np.float32)
    lt64 = np.triu(np.ones((64, 64), dtype=np.float32))
    mask_sl = np.triu(np.ones((64, 64), dtype=bf16))
    e32x2 = np.zeros((64, 2048), dtype=bf16)
    for h in range(32):
        e32x2[2 * h, 64 * h:64 * h + 64] = 1
        e32x2[2 * h + 1, 64 * h:64 * h + 64] = 1
    pick4b = np.zeros((4, 512), dtype=bf16)
    for e in range(4):
        pick4b[e, 128 * e:128 * e + 128] = 1
    onescol = np.ones((128, 1), dtype=bf16)
    onesr = np.ones((1, 512), dtype=np.float32)
    onesrb = np.ones((1, 512), dtype=bf16)
    return dict(identb=identb, identf=identf, lt64=lt64, mask_sl=mask_sl,
                e32x2=e32x2, pick4b=pick4b, onescol=onescol, onesr=onesr,
                onesrb=onesrb)


def kernel(L0_attn_u, L0_attn_in_proj_w, L0_attn_conv_w, L0_attn_conv_b,
           L0_attn_init_states, L0_attn_dt_bias, L0_attn_A_log, L0_attn_D,
           L0_attn_norm_w, L0_attn_out_proj_w, L0_ffn_gate_W, L0_ffn_gate_b,
           L0_ffn_W1, L0_ffn_b1, L0_ffn_g1, L0_ffn_be1, L0_ffn_W2, L0_ffn_b2,
           L0_ffn_g2, L0_ffn_be2):
    u = np.ascontiguousarray(np.asarray(L0_attn_u, dtype=np.float32))

    f32 = lambda a: np.ascontiguousarray(np.asarray(a, dtype=np.float32))
    w_inT = f32(L0_attn_in_proj_w).T.copy()
    cw = f32(L0_attn_conv_w)
    cb = f32(L0_attn_conv_b)
    cwt = np.zeros((128, 18 * 4), dtype=np.float32)
    cbt = np.zeros((128, 18), dtype=np.float32)
    for b in range(17):
        cwt[:, 4 * b:4 * b + 4] = cw[128 * b:128 * b + 128]
        cbt[:, b] = cb[128 * b:128 * b + 128]
    cwt[64:128, 68:72] = cw[2048:2112]
    cbt[64:128, 17] = cb[2048:2112]
    s0 = np.ascontiguousarray(
        np.transpose(f32(L0_attn_init_states), (2, 0, 1)).reshape(64, 2048))
    norm_w = np.ascontiguousarray(
        f32(L0_attn_norm_w).reshape(16, 128).T)
    w_outT = f32(L0_attn_out_proj_w).T.copy()
    gate_w = np.ascontiguousarray(
        f32(L0_ffn_gate_W).reshape(8, 128, 4).transpose(1, 0, 2).reshape(128, 32))
    colmajor = lambda a: np.ascontiguousarray(
        f32(a).reshape(4, 8, 128).transpose(2, 0, 1).reshape(128, 32))
    g1c, be1c, g2c = colmajor(L0_ffn_g1), colmajor(L0_ffn_be1), colmajor(L0_ffn_g2)
    w1 = f32(L0_ffn_W1).reshape(4096, 1024)
    w2 = f32(L0_ffn_W2).reshape(4096, 1024)
    b1f = f32(L0_ffn_b1).reshape(1, 4096)
    b2f = f32(L0_ffn_b2).reshape(1, 4096)
    b1c = colmajor(L0_ffn_b1)
    b2c = colmajor(L0_ffn_b2)
    be2r = f32(L0_ffn_be2)

    consts = _consts()
    consts["cvals"] = np.array([[LN_EPS, -0.5, 0.0, 0.0]], dtype=np.float32)
    shared = dict(w_inT=w_inT, conv_w=cwt, conv_b=cbt,
                  dt_bias=f32(L0_attn_dt_bias).reshape(32, 1),
                  a_log=f32(L0_attn_A_log).reshape(32, 1),
                  d_row=f32(L0_attn_D).reshape(1, 32),
                  s0=s0, norm_w=norm_w, w_outT=w_outT, gate_w=gate_w,
                  gate_b=f32(L0_ffn_gate_b).reshape(4, 1),
                  w1=w1, w2=w2, b1f=b1f, b2f=b2f, b1c=b1c, b2c=b2c, be2r=be2r,
                  g1c=g1c, be1c=be1c, g2c=g2c, **consts)

    in_maps = []
    for core in range(8):
        s, half = core // 2, core % 2
        m = dict(shared)
        m["u_main"] = np.ascontiguousarray(u[s, 2048 * half:2048 * half + 2048].T)
        if half == 1:
            m["u_pref"] = np.ascontiguousarray(u[s, 0:2048].T)
            m["dtmask"] = np.ones((32, 1), dtype=np.float32)
        else:
            m["u_pref"] = np.zeros((1024, 2048), dtype=np.float32)
            m["dtmask"] = np.zeros((32, 1), dtype=np.float32)
        in_maps.append(m)

    if "nc" not in _NC_CACHE:
        _NC_CACHE["nc"] = build_nc()
    nc = _NC_CACHE["nc"]
    res = run_bass_kernel_spmd(nc, in_maps, core_ids=list(range(8)))

    out = np.zeros((4, 4096, 1024), dtype=np.float32)
    for core in range(8):
        s, half = core // 2, core % 2
        out[s, 2048 * half:2048 * half + 2048] = \
            np.asarray(res.results[core]["y_out"]).T
    return out


if __name__ == "__main__":
    import time
    t0 = time.time()
    nc = build_nc()
    print(f"built ok in {time.time()-t0:.1f}s")
